# version 08
# baseline (speedup 1.0000x reference)
import numpy as np
import ml_dtypes
from contextlib import ExitStack
import concourse.bass as bass
import concourse.mybir as mybir
from concourse.bass_utils import run_bass_kernel_spmd

F32 = mybir.dt.float32
BF16 = mybir.dt.bfloat16
I32 = mybir.dt.int32
AF = mybir.ActivationFunctionType
ALU = mybir.AluOpType
AX = mybir.AxisListType
BF = ml_dtypes.bfloat16

B, S, D = 16, 2048, 1024
NCORES = 8
TOK = (B // NCORES) * S
NT = TOK // 128
NTS = NT // 2
TPG = 4
NG = NT // TPG
GPS = NG // 2
HC = D // 128
II = 2816
IC = II // 128
MAGIC = float(np.float32(1.5 * 2 ** 23))
R127 = float(np.float32(1.0) / np.float32(127.0))
_DBG = False
EPS_LN = 1e-5
EPS_ATTN = 1e-6
EPS_ILN = 1e-12


class Ctr:
    __slots__ = ("h", "n")

    def __init__(self, h):
        self.h = h
        self.n = 0

    def bump(self, k=1):
        self.n += k
        return self.n


def build_nc(stage="full"):
    nc = bass.Bass("TRN2", target_bir_lowering=False,
                   detect_race_conditions=False)
    cleanup_cm = nc.cleanup_on_exit()
    cleanup_cm.__enter__()

    def din(name, shape, dt=F32):
        return nc.dram_tensor(name, shape, dt, kind="ExternalInput").ap()

    x4 = din("x4", [TOK, D])
    wqT = din("wqT", [D, D]); wkT = din("wkT", [D, D])
    wvT = din("wvT", [D, D]); wdT = din("wdT", [D, D])
    bq_in = din("bq_in", [D]); bk_in = din("bk_in", [1, D])
    bv_in = din("bv_in", [1, D]); bd_in = din("bd_in", [1, D])
    gamma_in = din("gamma_in", [1, D]); beta_in = din("beta_in", [1, D])
    clipx_in = din("clipx_in", [1, 1]); clipd_in = din("clipd_in", [1, 1])
    wg_n = din("wg_n", [II, D]); wgT = din("wgT", [D, II])
    wu_n = din("wu_n", [II, D]); wuT = din("wuT", [D, II])
    wdn_n = din("wdn_n", [D, II]); wdnT = din("wdnT", [II, D])
    ident_in = din("ident_in", [128, 128], BF16)
    ones1b_in = din("ones1b_in", [1, 128], BF16)
    ones1f_in = din("ones1f_in", [1, 128])
    onescol_in = din("onescol_in", [128, 1], BF16)
    bc_in = din("bc_in", [16, HC * 128], BF16)

    outb = nc.dram_tensor("outb", [TOK, D], F32, kind="ExternalOutput").ap()

    h_spill = nc.dram_tensor("h_spill", [II, TOK], BF16)
    wg_sp = nc.dram_tensor("wg_sp", [D, II], BF16)
    wu_sp = nc.dram_tensor("wu_sp", [D, II], BF16)
    wdn_sp = nc.dram_tensor("wdn_sp", [II, D], BF16)
    v_rg = nc.dram_tensor("v_rg", [II], F32)
    v_ru = nc.dram_tensor("v_ru", [II], F32)
    v_rdn = nc.dram_tensor("v_rdn", [II], F32)
    v_rds = nc.dram_tensor("v_rds", [D], F32)
    cc1_in = nc.dram_tensor("cc1_in", [128, 1], F32)
    cc1_out = nc.dram_tensor("cc1_out", [128, 1], F32)
    cc2_in = nc.dram_tensor("cc2_in", [128, 1], F32)
    cc2_out = nc.dram_tensor("cc2_out", [128, 1], F32)
    dbg_xq = nc.dram_tensor("dbg_xq", [D, TOK], BF16)
    dbg_xr = nc.dram_tensor("dbg_xr", [TOK, D], BF16)
    dbg_xt = nc.dram_tensor("dbg_xt", [TOK, D], BF16)
    dbg_s = nc.dram_tensor("dbg_s", [128, 2 * NT], F32)

    prog = {k: [] for k in ("sync", "scalar", "vector", "tensor", "gpsimd")}
    watermark = {k: {} for k in prog}
    ENGS = ("sync", "scalar", "vector", "tensor", "gpsimd")

    def em(eng, fn):
        prog[eng].append(fn)

    def W(eng, ctr, val):
        if val is None or val <= 0:
            return
        wm = watermark[eng]
        if wm.get(id(ctr), -1) >= val:
            return
        wm[id(ctr)] = val
        em(eng, lambda e, h=ctr.h, v=val: e.wait_ge(h, v))

    def R(eng, maker, inc=None, inck=1, ss=True):
        if inc is None:
            em(eng, maker)
            return None
        v = inc.bump(inck)
        em(eng, lambda e, m=maker, h=inc.h, k=inck: m(e).then_inc(h, k))
        if ss and eng == "vector" and inck == 1:
            W(eng, inc, v)
        return v

    ctx_sems = []

    def mk_sem(name):
        cm = nc.semaphore(name)
        h = cm.__enter__()
        ctx_sems.append(cm)
        return Ctr(h)

    Spe = mk_sem("Spe")
    Sve = mk_sem("Sve")
    Sac = mk_sem("Sac")
    Scc = mk_sem("Scc")
    Dpers = mk_sem("Dpers")
    Drt = mk_sem("Drt")
    Dx = mk_sem("Dx")
    DwA = mk_sem("DwA")
    DwB = mk_sem("DwB")
    DwQ = mk_sem("DwQ")
    Dspill = mk_sem("Dspill")
    DawA = mk_sem("DawA")
    DawB = mk_sem("DawB")
    DA_bk = mk_sem("DA_bk"); DA_bv = mk_sem("DA_bv")
    DA_wk = mk_sem("DA_wk"); DA_wv = mk_sem("DA_wv")
    DB_wd = mk_sem("DB_wd"); DB_wq = mk_sem("DB_wq"); DB_bd = mk_sem("DB_bd")
    Dh0 = mk_sem("Dh0"); Dh1 = mk_sem("Dh1"); Dh2 = mk_sem("Dh2")
    DwC = mk_sem("DwC")
    DwD = mk_sem("DwD")
    Dhre = mk_sem("Dhre")
    Dout0 = mk_sem("Dout0"); Dout1 = mk_sem("Dout1")
    Ddbg = mk_sem("Ddbg")
    ALL_SEMS = [Spe, Sve, Sac, Dpers, Drt, Dx, DwA, DwB, DwQ, Dspill,
                DawA, DawB, DA_bk, DA_bv, DA_wk, DA_wv, DB_wd, DB_wq, DB_bd,
                Dh0, Dh1, Dh2, DwC, DwD, Dhre, Dout0, Dout1]

    def barrier():
        for eng in ENGS:
            for sm in ALL_SEMS:
                W(eng, sm, sm.n)

    stk = ExitStack()

    def sb(name, shape, dt):
        return stk.enter_context(nc.sbuf_tensor(name, shape, dt))

    x_res = sb("x_res", [128, NT, D], BF16)
    ident = sb("ident", [128, 128], BF16)
    ones1b = sb("ones1b", [1, 128], BF16)
    ones1f = sb("ones1f", [1, 128], F32)
    onescol = sb("onescol", [128, 1], BF16)
    bc_sb = sb("bc_sb", [16, HC * 128], BF16)
    bq_cols = sb("bq_cols", [128, HC], F32)
    clipx_sb = sb("clipx_sb", [1, 1], F32)
    clipd_sb = sb("clipd_sb", [1, 1], F32)
    mu1s = sb("mu1s", [128, NT], F32)
    rstd1s = sb("rstd1s", [128, NT], F32)
    nmu1s = sb("nmu1s", [128, NT], F32)
    mu2s = sb("mu2s", [128, NT], F32)
    rstd2s = sb("rstd2s", [128, NT], F32)
    s1s = sb("s1s", [128, NT], F32)
    s2s = sb("s2s", [128, NT], F32)
    kv_sb = sb("kv_sb", [128, 2, 1024], BF16)
    ksum_acc = sb("ksum_acc", [128, 16], F32)
    kcols = sb("kcols", [128, 2, HC, 16], BF16)
    scg_cols = sb("scg_cols", [128, IC], F32)
    scu_cols = sb("scu_cols", [128, IC], F32)
    scd_cols = sb("scd_cols", [128, HC], F32)
    sxg_cols = sb("sxg_cols", [128, IC], F32)
    sxu_cols = sb("sxu_cols", [128, IC], F32)
    ymax_acc = sb("ymax_acc", [128, 1], F32)
    hmax_acc = sb("hmax_acc", [128, 1], F32)
    yrow = sb("yrow", [1, 128], F32)
    hrow = sb("hrow", [1, 128], F32)
    sc1 = sb("sc1", [1, 4], F32)
    sx_col = sb("sx_col", [128, 1], F32)
    invsx_col = sb("invsx_col", [128, 1], F32)
    sd_col = sb("sd_col", [128, 1], F32)
    invsd_col = sb("invsd_col", [128, 1], F32)

    R("sync", lambda e: e.dma_start(out=ident[:], in_=ident_in), Dpers, 16)
    R("sync", lambda e: e.dma_start(out=ones1b[:], in_=ones1b_in), Dpers, 16)
    R("sync", lambda e: e.dma_start(out=ones1f[:], in_=ones1f_in), Dpers, 16)
    R("sync", lambda e: e.dma_start(out=onescol[:], in_=onescol_in), Dpers, 16)
    R("sync", lambda e: e.dma_start(out=bc_sb[:], in_=bc_in), Dpers, 16)
    R("sync", lambda e: e.dma_start(
        out=bq_cols[:], in_=bq_in.rearrange("(c p) -> p c", p=128),
        allow_slow_non_contiguous=True), Dpers, 16)
    R("sync", lambda e: e.dma_start(out=clipx_sb[:], in_=clipx_in), Dpers, 16)
    pers_x = R("sync", lambda e: e.dma_start(out=clipd_sb[:], in_=clipd_in), Dpers, 16)
    for eng in ("tensor", "vector", "scalar"):
        W(eng, Dpers, pers_x)

    R("vector", lambda e: e.memset(ksum_acc[:], 0.0), Sve)
    R("vector", lambda e: e.memset(kcols[:], 0.0), Sve)
    R("vector", lambda e: e.memset(kv_sb[:], 0.0), Sve)
    R("vector", lambda e: e.memset(ymax_acc[:], 0.0), Sve)
    R("vector", lambda e: e.memset(hmax_acc[:], 0.0), Sve)

    with ExitStack() as a_stk:
        def asb(name, shape, dt):
            return a_stk.enter_context(nc.sbuf_tensor(name, shape, dt))

        def aps(name, shape, dt=F32):
            return a_stk.enter_context(nc.psum_tensor(name, shape, dt))

        bk_row = asb("bk_row", [1, D], BF16)
        bv_row = asb("bv_row", [1, D], BF16)
        wk_sb = asb("wk_sb", [128, HC, D], BF16)
        wv_sb = asb("wv_sb", [128, HC, D], BF16)
        xf32A = asb("xf32A", [128, D], F32)
        xf32B = asb("xf32B", [128, D], F32)
        xlntokA = asb("xlntokA", [128, D], BF16)
        xlntokB = asb("xlntokB", [128, D], BF16)
        xfmA = asb("xfmA", [128, HC, 128], BF16)
        xfmB = asb("xfmB", [128, HC, 128], BF16)
        k_act = asb("k_act", [128, D], BF16)
        v_act = asb("v_act", [128, D], BF16)
        asq = asb("asq", [128, D], F32)
        vcol = asb("vcol", [128, 1], F32)
        scol = asb("scol", [128, 1], F32)
        qcol = asb("qcol", [128, 1], F32)
        scrE = asb("scrE", [128, 512], F32)
        scrR = asb("scrR", [128, 512], F32)
        ksum_t = asb("ksum_t", [128, 8], F32)
        kv_acc = asb("kv_acc", [128, 512], F32)

        ps_k = aps("ps_k", [128, 2, 512])
        ps_v = aps("ps_v", [128, 2, 512])
        ps_kv = aps("ps_kv", [128, 512])
        ps_ks = aps("ps_ks", [128, 512])
        ps_tr = aps("ps_tr", [128, 4, 128])

        wstA = asb("wstA", [128, II], F32)
        wstB = asb("wstB", [128, II], F32)
        r_b = asb("r_b", [128, II], F32)
        wi32 = asb("wi32", [128, II], I32)
        wint = asb("wint", [128, II], BF16)
        rrow = asb("rrow", [1, II], F32)
        rcols = asb("rcols", [128, IC], F32)
        ps_bc = aps("ps_bc", [128, 512])

        setup_units = []
        pstate = {"buf_rd": [0, 0], "wint_sp": 0, "bc_evac": 0}
        plvs = {}

        def prep_weight_units(nat_ap, sc_cols, n_chunks, nat_w,
                              T_ap, nT_chunks, T_width, vec_hbm, spill_ap, wid):
            def ld(c):
                buf = wstA if c % 2 == 0 else wstB
                dsem = DwA if c % 2 == 0 else DwB
                def f(c=c, buf=buf, dsem=dsem):
                    W("scalar", Sve, pstate["buf_rd"][c % 2])
                    plvs[(wid, "r", c)] = R(
                        "scalar", lambda e, c=c, buf=buf: e.dma_start(
                            out=buf[:, 0:nat_w],
                            in_=nat_ap[c * 128:(c + 1) * 128, :]), dsem, 16)
                return f

            def red(c):
                buf = wstA if c % 2 == 0 else wstB
                dsem = DwA if c % 2 == 0 else DwB
                def f(c=c, buf=buf, dsem=dsem):
                    W("vector", dsem, plvs[(wid, "r", c)])
                    pstate["buf_rd"][c % 2] = R(
                        "vector", lambda e, c=c, buf=buf: e.tensor_reduce(
                            out=sc_cols[:, c:c + 1], in_=buf[:, 0:nat_w],
                            axis=AX.X, op=ALU.max,
                            apply_absolute_value=True), Sve)
                return f

            def glue1():
                def f():
                    R("vector", lambda e: e.tensor_scalar(
                        out=sc_cols[:, 0:n_chunks], in0=sc_cols[:, 0:n_chunks],
                        scalar1=R127, scalar2=None, op0=ALU.mult), Sve)
                    rv = R("vector", lambda e: e.reciprocal(
                        out=rcols[:, 0:n_chunks], in_=sc_cols[:, 0:n_chunks]), Sve)
                    W("sync", Sve, rv)
                    wv = R("sync", lambda e: e.dma_start(
                        out=vec_hbm.ap().rearrange("(c p) -> p c", p=128)[:, 0:n_chunks],
                        in_=rcols[:, 0:n_chunks],
                        allow_slow_non_contiguous=True), Drt, 16)
                    W("sync", Drt, wv)
                    plvs[(wid, "rt")] = R("sync", lambda e: e.dma_start(
                        out=rrow[:, 0:T_width],
                        in_=vec_hbm.ap().rearrange("(a i) -> a i", a=1)[:, 0:T_width]),
                        Drt, 16)
                return f

            def glue2():
                def f():
                    W("tensor", Drt, plvs[(wid, "rt")])
                    nb = (T_width + 511) // 512
                    for b_ in range(nb):
                        lo, hi = b_ * 512, min(T_width, b_ * 512 + 512)
                        W("tensor", Sve, pstate["bc_evac"])
                        mv = R("tensor", lambda e, lo=lo, hi=hi: e.matmul(
                            ps_bc[:, 0:hi - lo], ones1f[:1, :], rrow[:1, lo:hi],
                            start=True, stop=True), Spe)
                        W("vector", Spe, mv)
                        pstate["bc_evac"] = R(
                            "vector", lambda e, lo=lo, hi=hi: e.tensor_copy(
                                out=r_b[:, lo:hi], in_=ps_bc[:, 0:hi - lo]), Sve)
                return f

            def qld(c):
                buf = wstA if c % 2 == 0 else wstB
                dsem = DwA if c % 2 == 0 else DwB
                def f(c=c, buf=buf, dsem=dsem):
                    W("scalar", Sve, pstate["buf_rd"][c % 2])
                    plvs[(wid, "q", c)] = R(
                        "scalar", lambda e, c=c, buf=buf: e.dma_start(
                            out=buf[:, 0:T_width],
                            in_=T_ap[c * 128:(c + 1) * 128, :]), dsem, 16)
                return f

            def quant(c):
                buf = wstA if c % 2 == 0 else wstB
                dsem = DwA if c % 2 == 0 else DwB
                def f(c=c, buf=buf, dsem=dsem):
                    W("vector", dsem, plvs[(wid, "q", c)])
                    W("vector", Spe, pstate.get("bc_pe", 0))
                    R("vector", lambda e, buf=buf: e.tensor_mul(
                        buf[:, 0:T_width], buf[:, 0:T_width],
                        r_b[:, 0:T_width]), Sve, 1, False)
                    pstate["buf_rd"][c % 2] = R(
                        "vector", lambda e, buf=buf: e.tensor_scalar(
                            out=wi32[:, 0:T_width], in0=buf[:, 0:T_width],
                            scalar1=MAGIC, scalar2=None, op0=ALU.add), Sve, 1, False)
                    W("vector", Dspill, pstate["wint_sp"])
                    qv = R("vector", lambda e: e.tensor_scalar(
                        out=wint[:, 0:T_width], in0=wi32[:, 0:T_width],
                        scalar1=-MAGIC, scalar2=None, op0=ALU.add), Sve, 1, False)
                    W("scalar", Sve, qv)
                    pstate["wint_sp"] = R(
                        "scalar", lambda e, c=c: e.dma_start(
                            out=spill_ap.ap()[c * 128:(c + 1) * 128, :],
                            in_=wint[:, 0:T_width]), Dspill, 16)
                return f

            def combine(*fs):
                def f():
                    for g_ in fs:
                        g_()
                return f

            u = [combine(ld(0), ld(1))]
            for c in range(n_chunks):
                fs = [red(c)]
                if c + 2 < n_chunks:
                    fs.append(ld(c + 2))
                u.append(combine(*fs))
            u.append(glue1())
            u.append(glue2())
            u.append(combine(qld(0), qld(1)))
            for c in range(nT_chunks):
                fs = [quant(c)]
                if c + 2 < nT_chunks:
                    fs.append(qld(c + 2))
                u.append(combine(*fs))
            return u

        if not (stage.endswith('_np') or stage.endswith('_min')):
            setup_units += prep_weight_units(
                wg_n, scg_cols, IC, D, wgT, HC, II, v_rg, wg_sp, "g")
            setup_units += prep_weight_units(
                wu_n, scu_cols, IC, D, wuT, HC, II, v_ru, wu_sp, "u")
            setup_units += prep_weight_units(
                wdn_n, scd_cols, HC, II, wdnT, IC, D, v_rdn, wdn_sp, "d")

        R("vector", lambda e: e.memset(kv_acc[:], 0.0), Sve)
        R("gpsimd", lambda e: e.dma_start(out=bk_row[:], in_=bk_in), DA_bk, 16)
        R("gpsimd", lambda e: e.dma_start(out=bv_row[:], in_=bv_in), DA_bv, 16)
        R("gpsimd", lambda e: e.dma_start(
            out=wk_sb[:], in_=wkT.rearrange("(c p) j -> p c j", p=128)), DA_wk, 16)
        R("gpsimd", lambda e: e.dma_start(
            out=wv_sb[:], in_=wvT.rearrange("(c p) j -> p c j", p=128)), DA_wv, 16)

        state = {
            "xf32_ve": [0, 0], "xf32_ac": [0, 0], "asq_rd": 0,
            "xlntok_pe": [0, 0],
            "tr_evac": 0,
            "xfm_pe": [0, 0],
            "kv_mm": 0,
            "psk_rd": 0, "scr_rd": 0,
            "ksum_evac": 0, "kv_evac": 0,
        }

        def emit_ln1_tile(T):
            xf32 = xf32A if T % 2 == 0 else xf32B
            W("sync", Sve, state["xf32_ve"][T % 2])
            W("sync", Sac, state["xf32_ac"][T % 2])
            lv = R("sync", lambda e, T=T, xf32=xf32: e.dma_start(
                out=xf32[:], in_=x4[128 * T:128 * (T + 1), :]), Dx, 16)
            W("scalar", Dx, lv)
            W("scalar", Sve, state["asq_rd"])
            sqv = R("scalar", lambda e, xf32=xf32: e.activation(
                out=asq[:], in_=xf32[:], func=AF.Square), Sac)
            W("vector", Dx, lv)
            R("vector", lambda e, xf32=xf32: e.tensor_reduce(
                out=scol[:], in_=xf32[:], axis=AX.X, op=ALU.add), Sve)
            W("vector", Sac, sqv)
            qv_ = R("vector", lambda e: e.tensor_reduce(
                out=qcol[:], in_=asq[:], axis=AX.X, op=ALU.add), Sve)
            state["asq_rd"] = qv_
            R("vector", lambda e, T=T: e.tensor_scalar(
                out=mu1s[:, T:T + 1], in0=scol[:], scalar1=1.0 / D, scalar2=None,
                op0=ALU.mult), Sve)
            R("vector", lambda e: e.tensor_scalar(
                out=qcol[:], in0=qcol[:], scalar1=1.0 / D, scalar2=None,
                op0=ALU.mult), Sve)
            R("vector", lambda e, T=T: e.tensor_scalar(
                out=vcol[:], in0=mu1s[:, T:T + 1], scalar1=mu1s[:, T:T + 1],
                scalar2=None, op0=ALU.mult), Sve)
            R("vector", lambda e: e.tensor_scalar(
                out=vcol[:], in0=vcol[:], scalar1=-1.0, scalar2=None,
                op0=ALU.mult), Sve)
            R("vector", lambda e: e.tensor_add(scol[:], vcol[:], qcol[:]), Sve)
            av = R("vector", lambda e: e.tensor_scalar(
                out=vcol[:], in0=scol[:], scalar1=EPS_LN, scalar2=None,
                op0=ALU.add), Sve)
            W("scalar", Sve, av)
            sv = R("scalar", lambda e: e.activation(out=scol[:], in_=vcol[:], func=AF.Sqrt), Sac)
            W("vector", Sac, sv)
            R("vector", lambda e, T=T: e.reciprocal(out=rstd1s[:, T:T + 1], in_=scol[:]), Sve)
            R("vector", lambda e, T=T: e.tensor_scalar(
                out=vcol[:], in0=mu1s[:, T:T + 1], scalar1=-1.0, scalar2=None,
                op0=ALU.mult), Sve)
            nv = R("vector", lambda e, T=T: e.tensor_mul(
                nmu1s[:, T:T + 1], vcol[:], rstd1s[:, T:T + 1]), Sve)
            cv = R("vector", lambda e, T=T, xf32=xf32: e.tensor_copy(
                out=x_res[:, T, :], in_=xf32[:]), Sve, 1, False)
            xt = xlntokA if T % 2 == 0 else xlntokB
            W("scalar", Spe, state["xlntok_pe"][T % 2])
            W("scalar", Sve, nv)
            xv = R("scalar", lambda e, xt=xt, T=T, xf32=xf32: e.activation(
                out=xt[:], in_=xf32[:], func=AF.Identity,
                bias=nmu1s[:, T:T + 1], scale=rstd1s[:, T:T + 1]), Sac)
            state["xf32_ve"][T % 2] = cv
            state["xf32_ac"][T % 2] = xv
            return xt, xv

        def emit_transposes(T, xt, xv):
            xfm = xfmA if T % 2 == 0 else xfmB
            W("tensor", Sac, xv)
            W("tensor", Spe, state["xfm_pe"][T % 2])
            last_mv = None
            for half in range(2):
                W("tensor", Sve, state["tr_evac"])
                for cc in range(4):
                    c = 4 * half + cc
                    last_mv = R("tensor", lambda e, c=c, cc=cc, xt=xt: e.matmul(
                        ps_tr[:, cc, :], xt[:, 128 * c:128 * (c + 1)], ident[:],
                        start=True, stop=True), Spe)
                W("vector", Spe, last_mv)
                state["tr_evac"] = R("vector", lambda e, half=half, xfm=xfm: e.tensor_copy(
                    out=xfm[:, 4 * half:4 * half + 4, :], in_=ps_tr[:]), Sve, 1, False)
            state["xlntok_pe"][T % 2] = last_mv
            return xfm

        def emit_kv_tile(T, seq, tis, xfm):
            W("tensor", Sve, state["psk_rd"])
            W("tensor", Sve, state["tr_evac"])
            W("tensor", DA_bk, 16)
            W("tensor", DA_bv, 16)
            W("tensor", DA_wk, 16)
            W("tensor", DA_wv, 16)
            kv_last = None
            for tgt, wsb, brow in ((ps_k, wk_sb, bk_row), (ps_v, wv_sb, bv_row)):
                for fh in range(2):
                    for c in range(HC):
                        R("tensor", lambda e, tgt=tgt, wsb=wsb, c=c, fh=fh, xfm=xfm:
                          e.matmul(tgt[:, fh, :], xfm[:, c, :],
                                   wsb[:, c, 512 * fh:512 * (fh + 1)],
                                   start=(c == 0), stop=False), Spe)
                    kv_last = R("tensor", lambda e, tgt=tgt, brow=brow, fh=fh: e.matmul(
                        tgt[:, fh, :], ones1b[:1, :],
                        brow[:1, 512 * fh:512 * (fh + 1)],
                        start=False, stop=True), Spe)
            state["xfm_pe"][T % 2] = kv_last
            W("scalar", Spe, kv_last)
            W("vector", Spe, state["kv_mm"])
            for fh in range(2):
                W("scalar", Sve, state["psk_rd"])
                e1 = R("scalar", lambda e, fh=fh: e.activation(
                    out=scrE[:], in_=ps_k[:, fh, :], func=AF.Exp), Sac)
                W("vector", Sac, e1)
                R("vector", lambda e, fh=fh: e.tensor_scalar(
                    out=scrR[:], in0=ps_k[:, fh, :], scalar1=0.0,
                    scalar2=None, op0=ALU.max), Sve, 1, False)
                ev2 = R("vector", lambda e, fh=fh: e.scalar_tensor_tensor(
                    out=k_act[:, 512 * fh:512 * (fh + 1)], in0=scrE[:],
                    scalar=1.0, in1=scrR[:], op0=ALU.min, op1=ALU.add), Sve, 1, False)
                state["psk_rd"] = ev2
            R("vector", lambda e: e.tensor_copy(out=v_act[:, 0:512], in_=ps_v[:, 0, :]), Sve, 1, False)
            vv_ = R("vector", lambda e: e.tensor_copy(out=v_act[:, 512:1024], in_=ps_v[:, 1, :]), Sve, 1, False)
            state["psk_rd"] = vv_
            W("tensor", Sve, vv_)
            W("tensor", Sve, state["kv_evac"])
            kvm = None
            for hd in range(16):
                hh = hd % 2
                p = hd // 2
                kvm = R("tensor", lambda e, hd=hd, hh=hh, p=p: e.matmul(
                    ps_kv[64 * hh:64 * (hh + 1), 64 * p:64 * (p + 1)],
                    k_act[:, 64 * hd:64 * (hd + 1)],
                    v_act[:, 64 * hd:64 * (hd + 1)],
                    start=True, stop=True), Spe)
            W("vector", Spe, kvm)
            state["kv_evac"] = R("vector", lambda e: e.tensor_tensor(
                out=kv_acc[:], in0=kv_acc[:], in1=ps_kv[:], op=ALU.add), Sve, 1, False)
            W("tensor", Sve, state["ksum_evac"])
            km = None
            for hd in range(16):
                hh = hd % 2
                p = hd // 2
                km = R("tensor", lambda e, hd=hd, hh=hh, p=p: e.matmul(
                    ps_ks[64 * hh:64 * (hh + 1), p:p + 1],
                    k_act[:, 64 * hd:64 * (hd + 1)], onescol[:],
                    start=True, stop=True), Spe)
            state["kv_mm"] = km
            W("vector", Spe, km)
            R("vector", lambda e: e.tensor_copy(out=ksum_t[:], in_=ps_ks[:, 0:8]), Sve)
            state["ksum_evac"] = R("vector", lambda e, seq=seq: e.tensor_tensor(
                out=ksum_acc[:, 8 * seq:8 * seq + 8],
                in0=ksum_acc[:, 8 * seq:8 * seq + 8],
                in1=ksum_t[:], op=ALU.add), Sve)

        minimal = stage.endswith("_min")
        n_su = len(setup_units)
        su_popped = 0
        ln1_out = [emit_ln1_tile(0), emit_ln1_tile(1)]
        for T in range(NT):
            seq = T // NTS
            xt, xv = ln1_out[T % 2]
            if not minimal:
                xfm = emit_transposes(T, xt, xv)
            if T + 2 < NT:
                ln1_out[T % 2] = emit_ln1_tile(T + 2)
            if minimal:
                continue
            emit_kv_tile(T, seq, T % NTS, xfm)
            want = (T + 1) * n_su // NT
            while su_popped < want:
                setup_units[su_popped]()
                su_popped += 1

            if T % NTS == NTS - 1 and not minimal:
                for p in range(8):
                    R("vector", lambda e, seq=seq, p=p: e.tensor_copy(
                        out=kv_sb[0:64, seq, 128 * p:128 * p + 64],
                        in_=kv_acc[0:64, 64 * p:64 * (p + 1)]), Sve)
                    R("vector", lambda e, seq=seq, p=p: e.tensor_copy(
                        out=kv_sb[64:128, seq, 128 * p + 64:128 * p + 128],
                        in_=kv_acc[64:128, 64 * p:64 * (p + 1)]), Sve)
                if seq == 0:
                    R("vector", lambda e: e.memset(kv_acc[:], 0.0), Sve)
                R("vector", lambda e, seq=seq: e.tensor_scalar(
                    out=ksum_acc[:, 8 * seq:8 * seq + 8],
                    in0=ksum_acc[:, 8 * seq:8 * seq + 8],
                    scalar1=EPS_ATTN, scalar2=None, op0=ALU.add), Sve)
                kcv = None
                for c in range(HC):
                    R("vector", lambda e, seq=seq, c=c: e.tensor_scalar(
                        out=kcols[0:64, seq, c, 2 * c:2 * c + 1],
                        in0=onescol[0:64, :],
                        scalar1=ksum_acc[0:64, 8 * seq + c:8 * seq + c + 1],
                        scalar2=None, op0=ALU.mult), Sve)
                    kcv = R("vector", lambda e, seq=seq, c=c: e.tensor_scalar(
                        out=kcols[64:128, seq, c, 2 * c + 1:2 * c + 2],
                        in0=onescol[64:128, :],
                        scalar1=ksum_acc[64:128, 8 * seq + c:8 * seq + c + 1],
                        scalar2=None, op0=ALU.mult), Sve)
                W("tensor", Sve, kcv)

    barrier()

    if stage == "A":
        _dbg_out(W, R, outb, x_res, Sve, Ddbg)
        barrier()
        return _finish(nc, prog, stk, ctx_sems, cleanup_cm)

    if stage.startswith("Astats"):
        W("gpsimd", Sve, Sve.n)
        R("gpsimd", lambda e: e.dma_start(
            out=outb.rearrange("a b -> (a b)")[0:128 * 32]
            .rearrange("(p f) -> p f", p=128), in_=mu1s[:]), Ddbg, 16)
        R("gpsimd", lambda e: e.dma_start(
            out=outb.rearrange("a b -> (a b)")[128 * 32:128 * 64]
            .rearrange("(p f) -> p f", p=128), in_=rstd1s[:]), Ddbg, 16)
        W("gpsimd", Ddbg, Ddbg.n)
        W("sync", Ddbg, Ddbg.n)
        barrier()
        return _finish(nc, prog, stk, ctx_sems, cleanup_cm)

    with ExitStack() as b_stk:
        def bsb(name, shape, dt):
            return b_stk.enter_context(nc.sbuf_tensor(name, shape, dt))

        def bps(name, shape, dt=F32):
            return b_stk.enter_context(nc.psum_tensor(name, shape, dt))

        wq_sb = bsb("wq_sb", [128, HC, D], BF16)
        wd_sb = bsb("wd_sb", [128, HC, D], BF16)
        gam_row = bsb("gam_row", [1, D], F32)
        bet_row = bsb("bet_row", [1, D], F32)
        bd_row = bsb("bd_row", [1, D], BF16)
        gamma_b = bsb("gamma_b", [128, D], F32)
        beta_b = bsb("beta_b", [128, D], F32)
        xlntok_gA = bsb("xlntok_gA", [128, TPG, D], BF16)
        xlntok_gB = bsb("xlntok_gB", [128, TPG, D], BF16)
        xfm_gA = bsb("xfm_gA", [128, HC, 512], BF16)
        xfm_gB = bsb("xfm_gB", [128, HC, 512], BF16)
        q_gA = bsb("q_gA", [128, HC, 512], BF16)
        q_gB = bsb("q_gB", [128, HC, 512], BF16)
        ctx_sb = bsb("ctx_sb", [128, HC, 512], BF16)
        z_all = bsb("z_all", [16, 512], BF16)
        z_f32 = bsb("z_f32", [16, 512], F32)
        zb_sb = bsb("zb_sb", [128, 512], BF16)
        hs_sb = bsb("hs_sb", [128, D], F32)
        ym_sb = bsb("ym_sb", [128, D], F32)
        vcolb = bsb("vcolb", [128, 1], F32)
        scolb = bsb("scolb", [128, 1], F32)
        qcolb = bsb("qcolb", [128, 1], F32)
        mucol = bsb("mucol", [128, 1], F32)
        rstdi = bsb("rstdi", [128, 1], F32)
        nmui = bsb("nmui", [128, 1], F32)
        ycol = bsb("ycol", [128, 1], F32)
        scrEb = bsb("scrEb", [128, 512], F32)
        scrRb = bsb("scrRb", [128, 512], F32)

        ps_tr_b = bps("ps_tr_b", [128, 4, 128])
        ps_tr_b2 = bps("ps_tr_b2", [128, 4, 128])
        ps_q1 = bps("ps_q1", [128, 512])
        ps_q2 = bps("ps_q2", [128, 512])
        ps_dn = bps("ps_dn", [16, 512])
        ps_zb = bps("ps_zb", [128, 512])
        ps_hs = bps("ps_hs", [128, 1024])

        R("sync", lambda e: e.dma_start(out=gam_row[:], in_=gamma_in), DawB, 16)
        R("sync", lambda e: e.dma_start(out=bet_row[:], in_=beta_in), DawB, 16)
        R("gpsimd", lambda e: e.dma_start(
            out=wd_sb[:], in_=wdT.rearrange("(c p) j -> p c j", p=128)), DB_wd, 16)
        R("gpsimd", lambda e: e.dma_start(
            out=wq_sb[:], in_=wqT.rearrange("(c p) j -> p c j", p=128)), DB_wq, 16)
        R("gpsimd", lambda e: e.dma_start(out=bd_row[:], in_=bd_in), DB_bd, 16)
        W("tensor", DB_wd, 16)
        W("tensor", DB_wq, 16)
        W("tensor", DB_bd, 16)
        for row, dstb in ((gam_row, gamma_b), (bet_row, beta_b)):
            for half in range(2):
                mv = R("tensor", lambda e, row=row, half=half: e.matmul(
                    ps_zb[:], ones1f[:1, :], row[:1, 512 * half:512 * (half + 1)],
                    start=True, stop=True), Spe)
                W("vector", Spe, mv)
                cv = R("vector", lambda e, dstb=dstb, half=half: e.tensor_copy(
                    out=dstb[:, 512 * half:512 * (half + 1)], in_=ps_zb[:]), Sve)
                W("tensor", Sve, cv)

        bstate = {
            "tr_evac": 0, "tr_evac2": [0, 0],
            "xlntok_rd": [0, 0], "tr_pe": [0, 0], "xfm_pe": [0, 0],
            "q_stt": [0, 0], "bank_free": [0, 0],
            "ctx_pe": 0, "zb_evac": 0, "hs_rd": 0, "scr_rd": 0,
        }

        def prep_xln(g):
            par = g % 2
            xlg = xlntok_gA if par == 0 else xlntok_gB
            xvs = []
            for tl in range(TPG):
                T = TPG * g + tl
                W("scalar", Spe, bstate["tr_pe"][par])
                W("scalar", Sve, bstate["xlntok_rd"][par])
                xvs.append(R("scalar", lambda e, T=T, tl=tl, xlg=xlg: e.activation(
                    out=xlg[:, tl, :], in_=x_res[:, T, :], func=AF.Identity,
                    bias=nmu1s[:, T:T + 1], scale=rstd1s[:, T:T + 1]), Sac))
            return xvs

        def prep_tr_tile(g, tl, xv):
            par = g % 2
            xlg = xlntok_gA if par == 0 else xlntok_gB
            xfg = xfm_gA if par == 0 else xfm_gB
            W("tensor", Sac, xv)
            last_mv = None
            for half in range(2):
                ps = ps_tr_b if half == 0 else ps_tr_b2
                W("tensor", Sve, bstate["tr_evac2"][half])
                for cc in range(4):
                    c = 4 * half + cc
                    last_mv = R("tensor", lambda e, c=c, cc=cc, tl=tl, xlg=xlg, ps=ps:
                                e.matmul(ps[:, cc, :],
                                         xlg[:, tl, 128 * c:128 * (c + 1)],
                                         ident[:], start=True, stop=True), Spe)
                W("vector", Spe, last_mv)
                ev = R(
                    "vector", lambda e, half=half, tl=tl, xfg=xfg, ps=ps: e.tensor_copy(
                        out=xfg[:, 4 * half:4 * half + 4, 128 * tl:128 * (tl + 1)],
                        in_=ps[:]), Sve, 1, False)
                bstate["tr_evac2"][half] = ev
                bstate["tr_evac"] = ev
            if tl == TPG - 1:
                bstate["tr_pe"][par] = last_mv

        def prep_q(g):
            par = g % 2
            xfg = xfm_gA if par == 0 else xfm_gB
            qg = q_gA if par == 0 else q_gB
            W("tensor", Sve, bstate["tr_evac"])
            for j in range(HC):
                bank = ps_q1 if j % 2 == 0 else ps_q2
                W("tensor", Sve, bstate["bank_free"][j % 2])
                mv = None
                for c in range(HC):
                    mv = R("tensor", lambda e, c=c, j=j, bank=bank, xfg=xfg: e.matmul(
                        bank[:], wq_sb[:, c, 128 * j:128 * (j + 1)],
                        xfg[:, c, :], start=(c == 0), stop=(c == HC - 1)), Spe)
                W("scalar", Spe, mv)
                W("scalar", Sve, bstate["scr_rd"])
                ev1 = R("scalar", lambda e, j=j, bank=bank: e.activation(
                    out=scrEb[:], in_=bank[:], func=AF.Exp,
                    bias=bq_cols[:, j:j + 1], scale=1.0), Sac)
                W("vector", Sac, ev1)
                R("vector", lambda e, j=j, bank=bank: e.tensor_scalar(
                    out=scrRb[:], in0=bank[:], scalar1=bq_cols[:, j:j + 1],
                    scalar2=0.0, op0=ALU.add, op1=ALU.max), Sve, 1, False)
                W("vector", Spe, bstate["q_pe"][par] if "q_pe" in bstate else 0)
                ev2 = R("vector", lambda e, j=j, qg=qg: e.scalar_tensor_tensor(
                    out=qg[:, j, :], in0=scrEb[:], scalar=1.0,
                    in1=scrRb[:], op0=ALU.min, op1=ALU.add), Sve, 1, False)
                bstate["bank_free"][j % 2] = ev2
                bstate["scr_rd"] = ev2
            bstate["q_stt"][par] = bstate["bank_free"][(HC - 1) % 2]

        def prep_group(g):
            xvs = prep_xln(g)
            for tl in range(TPG):
                prep_tr_tile(g, tl, xvs[tl])
            prep_q(g)

        bstate["q_pe"] = [0, 0]
        prep_group(0)

        for g in range(NG):
            seq = g // GPS
            par = g % 2
            xlg = xlntok_gA if par == 0 else xlntok_gB
            qg = q_gA if par == 0 else q_gB
            W("tensor", Sve, bstate["q_stt"][par])
            dv = None
            for c in range(HC):
                dv = R("tensor", lambda e, c=c, seq=seq, qg=qg: e.matmul(
                    ps_dn[:], kcols[:, seq, c, :], qg[:, c, :],
                    start=(c == 0), stop=(c == HC - 1)), Spe)
            W("vector", Spe, dv)
            R("vector", lambda e: e.reciprocal(out=z_f32[:], in_=ps_dn[:]), Sve)
            zv = R("vector", lambda e: e.tensor_copy(out=z_all[:], in_=z_f32[:]), Sve)
            W("tensor", Sve, zv)
            first_ctx = True
            for c in range(HC):
                W("tensor", Sve, bstate["zb_evac"])
                bv_ = R("tensor", lambda e, c=c: e.matmul(
                    ps_zb[:], bc_sb[:, 128 * c:128 * (c + 1)], z_all[:],
                    start=True, stop=True), Spe)
                W("vector", Spe, bv_)
                bstate["zb_evac"] = R("vector", lambda e: e.tensor_copy(
                    out=zb_sb[:], in_=ps_zb[:]), Sve, 1, False)
                bank = ps_q1 if c % 2 == 0 else ps_q2
                W("tensor", Sve, bstate["bank_free"][c % 2])
                cv = None
                for hh in range(2):
                    cv = R("tensor", lambda e, c=c, hh=hh, seq=seq, bank=bank, qg=qg:
                           e.matmul(bank[64 * hh:64 * (hh + 1), :],
                                    kv_sb[:, seq, 128 * c + 64 * hh:128 * c + 64 * hh + 64],
                                    qg[:, c, :], start=True, stop=True), Spe)
                W("vector", Spe, cv)
                W("vector", Sve, bstate["zb_evac"])
                if first_ctx:
                    W("vector", Spe, bstate["ctx_pe"])
                    first_ctx = False
                ev = R("vector", lambda e, c=c, bank=bank: e.tensor_tensor(
                    out=ctx_sb[:, c, :], in0=bank[:], in1=zb_sb[:],
                    op=ALU.mult), Sve, 1, False)
                bstate["bank_free"][c % 2] = ev
            bstate["q_pe"][par] = Spe.n
            ctx_done = bstate["bank_free"][:]
            xvs_next = prep_xln(g + 1) if g + 1 < NG else None
            W("tensor", Sve, ctx_done[0])
            W("tensor", Sve, ctx_done[1])
            W("tensor", DB_wd, 16)
            for tl in range(TPG):
                T = TPG * g + tl
                tloc = 128 * tl
                W("tensor", Sve, bstate["hs_rd"])
                mv = None
                for jh in range(2):
                    for c in range(HC):
                        R("tensor", lambda e, c=c, jh=jh, tloc=tloc: e.matmul(
                            ps_hs[:, 512 * jh:512 * (jh + 1)],
                            ctx_sb[:, c, tloc:tloc + 128],
                            wd_sb[:, c, 512 * jh:512 * (jh + 1)],
                            start=(c == 0), stop=False), Spe)
                    mv = R("tensor", lambda e, jh=jh: e.matmul(
                        ps_hs[:, 512 * jh:512 * (jh + 1)], ones1b[:1, :],
                        bd_row[:1, 512 * jh:512 * (jh + 1)],
                        start=False, stop=True), Spe)
                W("vector", Spe, mv)
                hv = R("vector", lambda e, tl=tl, xlg=xlg: e.scalar_tensor_tensor(
                    out=hs_sb[:], in0=ps_hs[:], scalar=0.0, in1=xlg[:, tl, :],
                    op0=ALU.add, op1=ALU.add), Sve, 1, False)
                bstate["hs_rd"] = hv
                if tl == TPG - 1:
                    bstate["xlntok_rd"][par] = hv
                W("scalar", Sve, hv)
                sqv = R("scalar", lambda e: e.activation(
                    out=ym_sb[:], in_=hs_sb[:], func=AF.Square), Sac)
                R("vector", lambda e: e.tensor_reduce(
                    out=scolb[:], in_=hs_sb[:], axis=AX.X, op=ALU.add), Sve)
                W("vector", Sac, sqv)
                R("vector", lambda e: e.tensor_reduce(
                    out=qcolb[:], in_=ym_sb[:], axis=AX.X, op=ALU.add), Sve)
                R("vector", lambda e: e.tensor_scalar(
                    out=mucol[:], in0=scolb[:], scalar1=1.0 / D, scalar2=None,
                    op0=ALU.mult), Sve)
                R("vector", lambda e: e.tensor_scalar(
                    out=qcolb[:], in0=qcolb[:], scalar1=1.0 / D, scalar2=None,
                    op0=ALU.mult), Sve)
                R("vector", lambda e: e.tensor_scalar(
                    out=vcolb[:], in0=mucol[:], scalar1=mucol[:],
                    scalar2=None, op0=ALU.mult), Sve)
                R("vector", lambda e: e.tensor_scalar(
                    out=vcolb[:], in0=vcolb[:], scalar1=-1.0, scalar2=None,
                    op0=ALU.mult), Sve)
                R("vector", lambda e: e.tensor_add(scolb[:], vcolb[:], qcolb[:]), Sve)
                av = R("vector", lambda e: e.tensor_scalar(
                    out=vcolb[:], in0=scolb[:], scalar1=EPS_ILN, scalar2=None,
                    op0=ALU.add), Sve)
                W("scalar", Sve, av)
                sv = R("scalar", lambda e: e.activation(
                    out=scolb[:], in_=vcolb[:], func=AF.Sqrt), Sac)
                W("vector", Sac, sv)
                R("vector", lambda e: e.reciprocal(out=rstdi[:], in_=scolb[:]), Sve)
                R("vector", lambda e: e.tensor_scalar(
                    out=vcolb[:], in0=mucol[:], scalar1=-1.0, scalar2=None,
                    op0=ALU.mult), Sve)
                nv = R("vector", lambda e: e.tensor_mul(nmui[:], vcolb[:], rstdi[:]), Sve)
                W("scalar", Sve, nv)
                hn = R("scalar", lambda e: e.activation(
                    out=hs_sb[:], in_=hs_sb[:], func=AF.Identity,
                    bias=nmui[:], scale=rstdi[:]), Sac)
                W("vector", Sac, hn)
                R("vector", lambda e: e.scalar_tensor_tensor(
                    out=hs_sb[:], in0=hs_sb[:], scalar=0.0, in1=gamma_b[:],
                    op0=ALU.add, op1=ALU.mult), Sve, 1, False)
                R("vector", lambda e: e.tensor_add(hs_sb[:], hs_sb[:], beta_b[:]), Sve, 1, False)
                xmv = R("vector", lambda e, T=T: e.tensor_tensor(
                    out=x_res[:, T, :], in0=hs_sb[:], in1=x_res[:, T, :],
                    op=ALU.add), Sve, 1, False)
                W("scalar", Sve, xmv)
                sqv = R("scalar", lambda e, T=T: e.activation(
                    out=ym_sb[:], in_=x_res[:, T, :], func=AF.Square), Sac)
                R("vector", lambda e, T=T: e.tensor_reduce(
                    out=scolb[:], in_=x_res[:, T, :], axis=AX.X, op=ALU.add), Sve)
                W("vector", Sac, sqv)
                R("vector", lambda e: e.tensor_reduce(
                    out=qcolb[:], in_=ym_sb[:], axis=AX.X, op=ALU.add), Sve)
                R("vector", lambda e, T=T: e.tensor_scalar(
                    out=mu2s[:, T:T + 1], in0=scolb[:], scalar1=1.0 / D, scalar2=None,
                    op0=ALU.mult), Sve)
                R("vector", lambda e: e.tensor_scalar(
                    out=qcolb[:], in0=qcolb[:], scalar1=1.0 / D, scalar2=None,
                    op0=ALU.mult), Sve)
                R("vector", lambda e, T=T: e.tensor_scalar(
                    out=vcolb[:], in0=mu2s[:, T:T + 1], scalar1=mu2s[:, T:T + 1],
                    scalar2=None, op0=ALU.mult), Sve)
                R("vector", lambda e: e.tensor_scalar(
                    out=vcolb[:], in0=vcolb[:], scalar1=-1.0, scalar2=None,
                    op0=ALU.mult), Sve)
                R("vector", lambda e: e.tensor_add(scolb[:], vcolb[:], qcolb[:]), Sve)
                av = R("vector", lambda e: e.tensor_scalar(
                    out=vcolb[:], in0=scolb[:], scalar1=EPS_LN, scalar2=None,
                    op0=ALU.add), Sve)
                W("scalar", Sve, av)
                sv = R("scalar", lambda e: e.activation(
                    out=scolb[:], in_=vcolb[:], func=AF.Sqrt), Sac)
                W("vector", Sac, sv)
                R("vector", lambda e, T=T: e.reciprocal(out=rstd2s[:, T:T + 1], in_=scolb[:]), Sve)
                R("vector", lambda e, T=T: e.tensor_scalar(
                    out=ym_sb[:], in0=x_res[:, T, :], scalar1=mu2s[:, T:T + 1],
                    scalar2=None, op0=ALU.subtract), Sve, 1, False)
                R("vector", lambda e: e.tensor_reduce(
                    out=ycol[:], in_=ym_sb[:], axis=AX.X, op=ALU.max,
                    apply_absolute_value=True), Sve)
                R("vector", lambda e, T=T: e.tensor_single_scalar(
                    out=ycol[:], in_=ycol[:], scalar=rstd2s[:, T:T + 1],
                    op=ALU.mult), Sve)
                R("vector", lambda e: e.tensor_max(ymax_acc[:], ymax_acc[:], ycol[:]), Sve)
                if xvs_next is not None:
                    prep_tr_tile(g + 1, tl, xvs_next[tl])
            if xvs_next is not None:
                prep_q(g + 1)
            bstate["ctx_pe"] = Spe.n

        W("sync", Sve, Sve.n)
        c1 = R("sync", lambda e: e.dma_start(out=cc1_in.ap(), in_=ymax_acc[:]), Drt, 16)
        W("gpsimd", Drt, c1)
        cv = R("gpsimd", lambda e: e.collective_compute(
            "AllReduce", ALU.max, replica_groups=[list(range(NCORES))],
            ins=[cc1_in.ap()], outs=[cc1_out.ap()]), Scc, 1)
        W("sync", Scc, cv)
        rv = R("sync", lambda e: e.dma_start(
            out=yrow[:], in_=cc1_out.ap().rearrange("p a -> a p")), Drt, 16)
        W("vector", Drt, rv)
        R("vector", lambda e: e.tensor_reduce(
            out=sc1[:, 0:1], in_=yrow[:], axis=AX.X, op=ALU.max), Sve)
        sgv = R("scalar", lambda e: e.activation(
            out=sc1[:, 1:2], in_=clipx_sb[:], func=AF.Sigmoid), Sac)
        W("vector", Sac, sgv)
        R("vector", lambda e: e.tensor_tensor(
            out=sc1[:, 2:3], in0=sc1[:, 0:1], in1=sc1[:, 1:2], op=ALU.mult), Sve)
        R("vector", lambda e: e.tensor_scalar(
            out=sc1[:, 2:3], in0=sc1[:, 2:3], scalar1=R127, scalar2=None,
            op0=ALU.mult), Sve)
        sxv = R("vector", lambda e: e.reciprocal(out=sc1[:, 3:4], in_=sc1[:, 2:3]), Sve)
        W("tensor", Sve, sxv)
        mv = R("tensor", lambda e: e.matmul(
            ps_zb[:, 0:2], ones1f[:1, :], sc1[:1, 2:4], start=True, stop=True), Spe)
        W("vector", Spe, mv)
        R("vector", lambda e: e.tensor_copy(out=sx_col[:], in_=ps_zb[:, 0:1]), Sve)
        R("vector", lambda e: e.tensor_copy(out=invsx_col[:], in_=ps_zb[:, 1:2]), Sve)
        R("vector", lambda e: e.tensor_single_scalar(
            out=s1s[:], in_=rstd2s[:], scalar=invsx_col[:], op=ALU.mult), Sve)
        R("vector", lambda e: e.tensor_scalar(
            out=s2s[:], in0=mu2s[:], scalar1=-1.0, scalar2=None,
            op0=ALU.mult), Sve)
        R("vector", lambda e: e.tensor_mul(s2s[:], s2s[:], s1s[:]), Sve)
        R("vector", lambda e: e.tensor_single_scalar(
            out=sxg_cols[:], in_=scg_cols[:], scalar=sx_col[:], op=ALU.mult), Sve)
        R("vector", lambda e: e.tensor_single_scalar(
            out=sxu_cols[:], in_=scu_cols[:], scalar=sx_col[:], op=ALU.mult), Sve)

    barrier()

    if stage in ("Bq", "Bctx", "Bxln"):
        W("gpsimd", Ddbg, Ddbg.n)
        W("sync", Ddbg, Ddbg.n)
        barrier()
        return _finish(nc, prog, stk, ctx_sems, cleanup_cm)
    if stage == "B":
        _dbg_out(W, R, outb, x_res, Sve, Ddbg)
        barrier()
        return _finish(nc, prog, stk, ctx_sems, cleanup_cm)

    with ExitStack() as c_stk:
        def csb(name, shape, dt):
            return c_stk.enter_context(nc.sbuf_tensor(name, shape, dt))

        def cps(name, shape, dt=F32):
            return c_stk.enter_context(nc.psum_tensor(name, shape, dt))

        wgint = csb("wgint", [128, HC, II], BF16)
        wuint = csb("wuint", [128, HC, II], BF16)
        xqscr = csb("xqscr", [128, D], F32)
        xqi32 = csb("xqi32", [128, D], I32)
        xqtokA = csb("xqtokA", [128, D], BF16)
        xqtokB = csb("xqtokB", [128, D], BF16)
        xq_fm = csb("xq_fm", [128, HC, 512], BF16)
        silu_sb = csb("silu_sb", [128, 512], F32)
        h_sb0 = csb("h_sb0", [128, 512], BF16)
        h_sb1 = csb("h_sb1", [128, 512], BF16)
        h_sb2 = csb("h_sb2", [128, 512], BF16)
        hcol = csb("hcol", [128, 1], F32)

        ps_gA = cps("ps_gA", [128, 512])
        ps_gB = cps("ps_gB", [128, 512])
        ps_uA = cps("ps_uA", [128, 512])
        ps_uB = cps("ps_uB", [128, 512])
        ps_tr2 = cps("ps_tr2", [128, 4, 128])

        R("scalar", lambda e: e.dma_start(
            out=wgint[:], in_=wg_sp.ap().rearrange("(c p) i -> p c i", p=128)), DwC, 16)
        wu_l = R("scalar", lambda e: e.dma_start(
            out=wuint[:], in_=wu_sp.ap().rearrange("(c p) i -> p c i", p=128)), DwC, 16)

        if _DBG:
            for T in range(NT):
                R("sync", lambda e, T=T: e.dma_start(
                    out=dbg_xr[128 * T:128 * (T + 1), :], in_=x_res[:, T, :]), Drt, 16)
            R("sync", lambda e: e.dma_start(out=dbg_s.ap()[:, 0:NT], in_=s1s[:]), Drt, 16)
            R("sync", lambda e: e.dma_start(out=dbg_s.ap()[:, NT:2 * NT], in_=s2s[:]), Drt, 16)
        h_bufs = [h_sb0, h_sb1, h_sb2]
        h_sems = [Dh0, Dh1, Dh2]
        cstate = {"hdma": [0, 0, 0], "tr2_evac": 0, "xqtok_pe": [0, 0],
                  "xqfm_pe": 0, "g_evac": [0, 0], "u_evac": [0, 0],
                  "xqscr_rd": 0, "h_stt": 0}

        for g in range(NG):
            gsl = slice(512 * g, 512 * (g + 1))
            for tl in range(TPG):
                T = TPG * g + tl
                tloc = 128 * tl
                W("scalar", Sve, cstate["xqscr_rd"])
                uv = R("scalar", lambda e, T=T: e.activation(
                    out=xqscr[:], in_=x_res[:, T, :], func=AF.Identity,
                    bias=s2s[:, T:T + 1], scale=s1s[:, T:T + 1]), Sac)
                W("vector", Sac, uv)
                R("vector", lambda e: e.tensor_scalar(
                    out=xqscr[:], in0=xqscr[:], scalar1=127.0, scalar2=-128.0,
                    op0=ALU.min, op1=ALU.max), Sve, 1, False)
                cstate["xqscr_rd"] = R("vector", lambda e: e.tensor_scalar(
                    out=xqi32[:], in0=xqscr[:], scalar1=MAGIC, scalar2=None,
                    op0=ALU.add), Sve, 1, False)
                xt = xqtokA if T % 2 == 0 else xqtokB
                W("vector", Spe, cstate["xqtok_pe"][T % 2])
                qv = R("vector", lambda e, xt=xt: e.tensor_scalar(
                    out=xt[:], in0=xqi32[:], scalar1=-MAGIC, scalar2=None,
                    op0=ALU.add), Sve, 1, False)
                if _DBG:
                    W("sync", Sve, qv)
                    dmp = R("sync", lambda e, T=T, xt=xt: e.dma_start(
                        out=dbg_xt[128 * T:128 * (T + 1), :], in_=xt[:]), Drt, 16)
                    W("vector", Drt, dmp)
                W("tensor", Sve, qv)
                W("tensor", Spe, cstate["xqfm_pe"])
                last_mv = None
                for half in range(2):
                    W("tensor", Sve, cstate["tr2_evac"])
                    for cc in range(4):
                        c = 4 * half + cc
                        last_mv = R("tensor", lambda e, c=c, cc=cc, xt=xt: e.matmul(
                            ps_tr2[:, cc, :], xt[:, 128 * c:128 * (c + 1)],
                            ident[:], start=True, stop=True), Spe)
                    W("vector", Spe, last_mv)
                    cstate["tr2_evac"] = R("vector", lambda e, half=half, tloc=tloc: e.tensor_copy(
                        out=xq_fm[:, 4 * half:4 * half + 4, tloc:tloc + 128],
                        in_=ps_tr2[:]),Sve, 1, False)
                cstate["xqtok_pe"][T % 2] = last_mv
            if _DBG:
                W("sync", Sve, cstate["tr2_evac"])
                dmp = R("sync", lambda e, g=g: e.dma_start(
                    out=dbg_xq.ap().rearrange("(c p) t -> p c t", p=128)[:, :, 512 * g:512 * (g + 1)],
                    in_=xq_fm[:]), Drt, 16)
                W("vector", Drt, dmp)
            W("tensor", Sve, cstate["tr2_evac"])
            W("tensor", DwC, wu_l)
            for ic in range(IC):
                gbank = ps_gA if ic % 2 == 0 else ps_gB
                ubank = ps_uA if ic % 2 == 0 else ps_uB
                W("tensor", Sve, cstate["g_evac"][ic % 2])
                W("tensor", Sve, cstate["u_evac"][ic % 2])
                gv = uv2 = None
                for c in range(HC):
                    gv = R("tensor", lambda e, c=c, ic=ic, gbank=gbank: e.matmul(
                        gbank[:], wgint[:, c, 128 * ic:128 * (ic + 1)],
                        xq_fm[:, c, :], start=(c == 0), stop=(c == HC - 1)), Spe)
                for c in range(HC):
                    uv2 = R("tensor", lambda e, c=c, ic=ic, ubank=ubank: e.matmul(
                        ubank[:], wuint[:, c, 128 * ic:128 * (ic + 1)],
                        xq_fm[:, c, :], start=(c == 0), stop=(c == HC - 1)), Spe)
                W("scalar", Spe, gv)
                W("scalar", Sve, cstate["h_stt"])
                sv = R("scalar", lambda e, ic=ic, gbank=gbank: e.activation(
                    out=silu_sb[:], in_=gbank[:], func=AF.Silu,
                    scale=sxg_cols[:, ic:ic + 1]), Sac)
                bidx = (g * IC + ic) % 3
                hb = h_bufs[bidx]
                W("vector", h_sems[bidx], cstate["hdma"][bidx])
                W("vector", Sac, sv)
                W("vector", Spe, uv2)
                hv = R("vector", lambda e, ic=ic, ubank=ubank, hb=hb: e.scalar_tensor_tensor(
                    out=hb[:], in0=ubank[:], scalar=sxu_cols[:, ic:ic + 1],
                    in1=silu_sb[:], op0=ALU.mult, op1=ALU.mult), Sve, 1, False)
                cstate["g_evac"][ic % 2] = hv
                cstate["u_evac"][ic % 2] = hv
                cstate["h_stt"] = hv
                R("vector", lambda e, hb=hb: e.tensor_reduce(
                    out=hcol[:], in_=hb[:], axis=AX.X, op=ALU.max,
                    apply_absolute_value=True), Sve)
                R("vector", lambda e: e.tensor_max(hmax_acc[:], hmax_acc[:], hcol[:]), Sve)
                W("sync", Sve, hv)
                cstate["hdma"][bidx] = R("sync", lambda e, ic=ic, gsl=gsl, hb=hb: e.dma_start(
                    out=h_spill.ap()[128 * ic:128 * (ic + 1), gsl], in_=hb[:]),
                    h_sems[bidx], 16)
            cstate["xqfm_pe"] = Spe.n

        W("sync", Sve, Sve.n)
        c2 = R("sync", lambda e: e.dma_start(out=cc2_in.ap(), in_=hmax_acc[:]), Drt, 16)
        W("gpsimd", Drt, c2)
        cv2 = R("gpsimd", lambda e: e.collective_compute(
            "AllReduce", ALU.max, replica_groups=[list(range(NCORES))],
            ins=[cc2_in.ap()], outs=[cc2_out.ap()]), Scc, 1)
        W("sync", Scc, cv2)
        rv = R("sync", lambda e: e.dma_start(
            out=hrow[:], in_=cc2_out.ap().rearrange("p a -> a p")), Drt, 16)
        W("vector", Drt, rv)
        R("vector", lambda e: e.tensor_reduce(
            out=sc1[:, 0:1], in_=hrow[:], axis=AX.X, op=ALU.max), Sve)
        sgv = R("scalar", lambda e: e.activation(
            out=sc1[:, 1:2], in_=clipd_sb[:], func=AF.Sigmoid), Sac)
        W("vector", Sac, sgv)
        R("vector", lambda e: e.tensor_tensor(
            out=sc1[:, 2:3], in0=sc1[:, 0:1], in1=sc1[:, 1:2], op=ALU.mult), Sve)
        R("vector", lambda e: e.tensor_scalar(
            out=sc1[:, 2:3], in0=sc1[:, 2:3], scalar1=R127, scalar2=None,
            op0=ALU.mult), Sve)
        sdv = R("vector", lambda e: e.reciprocal(out=sc1[:, 3:4], in_=sc1[:, 2:3]), Sve)
        W("tensor", Sve, sdv)
        mv = R("tensor", lambda e: e.matmul(
            ps_gA[:, 0:2], ones1f[:1, :], sc1[:1, 2:4], start=True, stop=True), Spe)
        W("vector", Spe, mv)
        R("vector", lambda e: e.tensor_copy(out=sd_col[:], in_=ps_gA[:, 0:1]), Sve)
        R("vector", lambda e: e.tensor_copy(out=invsd_col[:], in_=ps_gA[:, 1:2]), Sve)

    barrier()

    if stage == "C":
        _dbg_out(W, R, outb, x_res, Sve, Ddbg)
        barrier()
        return _finish(nc, prog, stk, ctx_sems, cleanup_cm)

    with ExitStack() as d_stk:
        def dsb(name, shape, dt):
            return d_stk.enter_context(nc.sbuf_tensor(name, shape, dt))

        def dps(name, shape, dt=F32):
            return d_stk.enter_context(nc.psum_tensor(name, shape, dt))

        wdnint = dsb("wdnint", [128, IC, D], BF16)
        hre = dsb("hre", [128, IC, 512], BF16)
        hqA = dsb("hqA", [128, IC, 512], BF16)
        hqB = dsb("hqB", [128, IC, 512], BF16)
        hscr = dsb("hscr", [128, 512], F32)
        hi32 = dsb("hi32", [128, 512], I32)
        srd_row = dsb("srd_row", [1, D], F32)
        sdsrd_b = dsb("sdsrd_b", [128, D], F32)
        outA = dsb("outA", [128, D], F32)
        outB = dsb("outB", [128, D], F32)

        ps_dnA = dps("ps_dnA", [128, 1024])
        ps_dnB = dps("ps_dnB", [128, 1024])
        ps_m = dps("ps_m", [128, 512])

        wdn_l = R("scalar", lambda e: e.dma_start(
            out=wdnint[:], in_=wdn_sp.ap().rearrange("(c p) j -> p c j", p=128)), DwD, 16)

        wv = R("sync", lambda e: e.dma_start(
            out=v_rds.ap().rearrange("(c p) -> p c", p=128), in_=scd_cols[:],
            allow_slow_non_contiguous=True), Drt, 16)
        W("sync", Drt, wv)
        rv = R("sync", lambda e: e.dma_start(
            out=srd_row[:], in_=v_rds.ap().rearrange("(a i) -> a i", a=1)), Drt, 16)
        W("tensor", Drt, rv)
        for half in range(2):
            mv = R("tensor", lambda e, half=half: e.matmul(
                ps_m[:], ones1f[:1, :], srd_row[:1, 512 * half:512 * (half + 1)],
                start=True, stop=True), Spe)
            W("vector", Spe, mv)
            cv = R("vector", lambda e, half=half: e.tensor_copy(
                out=sdsrd_b[:, 512 * half:512 * (half + 1)], in_=ps_m[:]), Sve)
            W("tensor", Sve, cv)
        dcv = R("vector", lambda e: e.tensor_single_scalar(
            out=sdsrd_b[:], in_=sdsrd_b[:], scalar=sd_col[:], op=ALU.mult), Sve)

        dstate = {"outdma": [0, 0], "dn_evac": [0, 0],
                  "hq_pe": [0, 0], "quant_done": [0, 0], "hre_free": 0}

        def emit_hre_load(g):
            gsl = slice(512 * g, 512 * (g + 1))
            W("sync", Sve, dstate["hre_free"])
            hl = None
            for chalf in range(2):
                lo = 11 * chalf
                hi = 11 * (chalf + 1)
                hl = R("sync", lambda e, gsl=gsl, lo=lo, hi=hi: e.dma_start(
                    out=hre[:, lo:hi, :],
                    in_=h_spill.ap().rearrange("(c p) t -> p c t", p=128)[:, lo:hi, gsl]),
                    Dhre, 16)
            return hl

        def quant_ops(g, hl):
            hq = hqA if g % 2 == 0 else hqB
            ops = []
            for ic in range(IC):
                def chunk(ic=ic, hq=hq, g=g, hl=hl):
                    if ic == 0:
                        W("vector", Spe, dstate["hq_pe"][g % 2])
                        W("vector", Dhre, hl)
                    R("vector", lambda e, ic=ic: e.tensor_scalar(
                        out=hscr[:], in0=hre[:, ic, :], scalar1=invsd_col[:],
                        scalar2=127.0, op0=ALU.mult, op1=ALU.min), Sve, 1, False)
                    R("vector", lambda e: e.tensor_scalar(
                        out=hi32[:], in0=hscr[:], scalar1=-128.0, scalar2=MAGIC,
                        op0=ALU.max, op1=ALU.add), Sve, 1, False)
                    qv = R("vector", lambda e, ic=ic, hq=hq: e.tensor_scalar(
                        out=hq[:, ic, :], in0=hi32[:], scalar1=-MAGIC, scalar2=None,
                        op0=ALU.add), Sve, 1, False)
                    if ic == IC - 1:
                        dstate["quant_done"][g % 2] = qv
                        dstate["hre_free"] = qv
                ops.append(chunk)
            return ops

        for op in quant_ops(0, emit_hre_load(0)):
            op()

        per_tile = [6, 6, 5, 5]
        for g in range(NG):
            hq = hqA if g % 2 == 0 else hqB
            W("tensor", Sve, dstate["quant_done"][g % 2])
            if g == 0:
                W("tensor", DwD, wdn_l)
            pending = quant_ops(g + 1, emit_hre_load(g + 1)) if g + 1 < NG else []
            for tl in range(TPG):
                T = TPG * g + tl
                tloc = 128 * tl
                bank = ps_dnA if tl % 2 == 0 else ps_dnB
                W("tensor", Sve, dstate["dn_evac"][tl % 2])
                mv = None
                for jh in range(2):
                    for ic in range(IC):
                        mv = R("tensor", lambda e, ic=ic, jh=jh, tloc=tloc, bank=bank,
                               hq=hq:
                               e.matmul(bank[:, 512 * jh:512 * (jh + 1)],
                                        hq[:, ic, tloc:tloc + 128],
                                        wdnint[:, ic, 512 * jh:512 * (jh + 1)],
                                        start=(ic == 0), stop=(ic == IC - 1)), Spe)
                for _ in range(min(per_tile[tl], len(pending))):
                    pending.pop(0)()
                ob = outA if tl % 2 == 0 else outB
                W("vector", Spe, mv)
                W("vector", Sve, dcv)
                W("vector", (Dout0 if tl % 2 == 0 else Dout1), dstate["outdma"][tl % 2])
                R("vector", lambda e, bank=bank, ob=ob: e.scalar_tensor_tensor(
                    out=ob[:], in0=bank[:], scalar=0.0, in1=sdsrd_b[:],
                    op0=ALU.add, op1=ALU.mult), Sve, 1, False)
                ov = R("vector", lambda e, T=T, ob=ob: e.tensor_tensor(
                    out=ob[:], in0=ob[:], in1=x_res[:, T, :], op=ALU.add), Sve, 1, False)
                dstate["dn_evac"][tl % 2] = ov
                W("sync", Sve, ov)
                dstate["outdma"][tl % 2] = R("sync", lambda e, T=T, ob=ob: e.dma_start(
                    out=outb[128 * T:128 * (T + 1), :], in_=ob[:]),
                    (Dout0 if tl % 2 == 0 else Dout1), 16)
            for op in pending:
                op()
            dstate["hq_pe"][g % 2] = Spe.n

        W("sync", Dout0, Dout0.n)
        W("sync", Dout1, Dout1.n)
        W("gpsimd", Dout0, Dout0.n)
        W("gpsimd", Dout1, Dout1.n)
    barrier()

    return _finish(nc, prog, stk, ctx_sems, cleanup_cm)


def _dbg_out(W, R, outb, x_res, Sve, Ddbg):
    for T in range(NT):
        W("gpsimd", Sve, Sve.n)
        R("gpsimd", lambda e, T=T: e.dma_start(
            out=outb[128 * T:128 * (T + 1), :], in_=x_res[:, T, :]), Ddbg, 16)
    W("gpsimd", Ddbg, Ddbg.n)
    W("sync", Ddbg, Ddbg.n)


def _finish(nc, prog, stk, ctx_sems, cleanup_cm):
    with nc.Block() as block:
        @block.sync
        def _(e):
            for f in prog["sync"]:
                f(e)

        @block.scalar
        def _(e):
            for f in prog["scalar"]:
                f(e)

        @block.vector
        def _(e):
            for f in prog["vector"]:
                f(e)

        @block.tensor
        def _(e):
            for f in prog["tensor"]:
                f(e)

        @block.gpsimd
        def _(e):
            for f in prog["gpsimd"]:
                f(e)
    stk.close()
    cleanup_cm.__exit__(None, None, None)
    return nc


def _host_inputs(full):
    x = np.ascontiguousarray(np.asarray(full["L0_attn_input_tensor"], np.float32))
    com = {
        "wqT": np.ascontiguousarray(np.asarray(full["L0_attn_Wq"], np.float32).T),
        "wkT": np.ascontiguousarray(np.asarray(full["L0_attn_Wk"], np.float32).T),
        "wvT": np.ascontiguousarray(np.asarray(full["L0_attn_Wv"], np.float32).T),
        "wdT": np.ascontiguousarray(np.asarray(full["L0_attn_Wd"], np.float32).T),
        "bq_in": np.ascontiguousarray(np.asarray(full["L0_attn_bq"], np.float32)),
        "bk_in": np.asarray(full["L0_attn_bk"], np.float32).reshape(1, D),
        "bv_in": np.asarray(full["L0_attn_bv"], np.float32).reshape(1, D),
        "bd_in": np.asarray(full["L0_attn_bd"], np.float32).reshape(1, D),
        "gamma_in": np.asarray(full["L0_attn_gamma"], np.float32).reshape(1, D),
        "beta_in": np.asarray(full["L0_attn_beta"], np.float32).reshape(1, D),
        "clipx_in": np.asarray(full["L0_ffn_clip_x"], np.float32).reshape(1, 1),
        "clipd_in": np.asarray(full["L0_ffn_clip_down"], np.float32).reshape(1, 1),
        "wg_n": np.ascontiguousarray(np.asarray(full["L0_ffn_w_gate"], np.float32)),
        "wgT": np.ascontiguousarray(np.asarray(full["L0_ffn_w_gate"], np.float32).T),
        "wu_n": np.ascontiguousarray(np.asarray(full["L0_ffn_w_up"], np.float32)),
        "wuT": np.ascontiguousarray(np.asarray(full["L0_ffn_w_up"], np.float32).T),
        "wdn_n": np.ascontiguousarray(np.asarray(full["L0_ffn_w_down"], np.float32)),
        "wdnT": np.ascontiguousarray(np.asarray(full["L0_ffn_w_down"], np.float32).T),
        "ident_in": np.eye(128, dtype=BF),
        "ones1b_in": np.ones((1, 128), BF),
        "ones1f_in": np.ones((1, 128), np.float32),
        "onescol_in": np.ones((128, 1), BF),
        "bc_in": _bc_const(),
    }
    xr = x.reshape(B, S, D)
    in_maps = []
    for core in range(NCORES):
        m = dict(com)
        m["x4"] = np.ascontiguousarray(xr[2 * core:2 * core + 2].reshape(TOK, D))
        in_maps.append(m)
    return in_maps


def _bc_const():
    bc = np.zeros((16, HC * 128), BF)
    for c in range(HC):
        bc[2 * c, 128 * c:128 * c + 64] = 1
        bc[2 * c + 1, 128 * c + 64:128 * c + 128] = 1
    return bc


def kernel(**inputs):
    in_maps = _host_inputs(inputs)
    nc = build_nc("full")
    res = run_bass_kernel_spmd(nc, in_maps, core_ids=list(range(NCORES)))
    parts = [res.results[c]["outb"].reshape(2, S, D) for c in range(NCORES)]
    return np.concatenate(parts, axis=0).astype(np.float32)


# version 09
# speedup vs baseline: 1.0346x; 1.0346x over previous
import numpy as np
import ml_dtypes
from contextlib import ExitStack
import concourse.bass as bass
import concourse.mybir as mybir
from concourse.bass_utils import run_bass_kernel_spmd

F32 = mybir.dt.float32
BF16 = mybir.dt.bfloat16
I32 = mybir.dt.int32
AF = mybir.ActivationFunctionType
ALU = mybir.AluOpType
AX = mybir.AxisListType
BF = ml_dtypes.bfloat16

B, S, D = 16, 2048, 1024
NCORES = 8
TOK = (B // NCORES) * S
NT = TOK // 128
NTS = NT // 2
TPG = 4
NG = NT // TPG
GPS = NG // 2
HC = D // 128
II = 2816
IC = II // 128
MAGIC = float(np.float32(1.5 * 2 ** 23))
R127 = float(np.float32(1.0) / np.float32(127.0))
_DBG = False
EPS_LN = 1e-5
EPS_ATTN = 1e-6
EPS_ILN = 1e-12


class Ctr:
    __slots__ = ("h", "n")

    def __init__(self, h):
        self.h = h
        self.n = 0

    def bump(self, k=1):
        self.n += k
        return self.n


def build_nc(stage="full"):
    nc = bass.Bass("TRN2", target_bir_lowering=False,
                   detect_race_conditions=False)
    cleanup_cm = nc.cleanup_on_exit()
    cleanup_cm.__enter__()

    def din(name, shape, dt=F32):
        return nc.dram_tensor(name, shape, dt, kind="ExternalInput").ap()

    x4 = din("x4", [TOK, D])
    wqT = din("wqT", [D, D]); wkT = din("wkT", [D, D])
    wvT = din("wvT", [D, D]); wdT = din("wdT", [D, D])
    bq_in = din("bq_in", [D]); bk_in = din("bk_in", [1, D])
    bv_in = din("bv_in", [1, D]); bd_in = din("bd_in", [1, D])
    gamma_in = din("gamma_in", [1, D]); beta_in = din("beta_in", [1, D])
    clipx_in = din("clipx_in", [1, 1]); clipd_in = din("clipd_in", [1, 1])
    wg_n = din("wg_n", [II, D]); wgT = din("wgT", [D, II])
    wu_n = din("wu_n", [II, D]); wuT = din("wuT", [D, II])
    wdn_n = din("wdn_n", [D, II]); wdnT = din("wdnT", [II, D])
    ident_in = din("ident_in", [128, 128], BF16)
    ones1b_in = din("ones1b_in", [1, 128], BF16)
    ones1f_in = din("ones1f_in", [1, 128])
    onescol_in = din("onescol_in", [128, 1], BF16)
    bc_in = din("bc_in", [16, HC * 128], BF16)

    outb = nc.dram_tensor("outb", [TOK, D], F32, kind="ExternalOutput").ap()

    h_spill = nc.dram_tensor("h_spill", [II, TOK], BF16)
    wg_sp = nc.dram_tensor("wg_sp", [D, II], BF16)
    wu_sp = nc.dram_tensor("wu_sp", [D, II], BF16)
    wdn_sp = nc.dram_tensor("wdn_sp", [II, D], BF16)
    v_rg = nc.dram_tensor("v_rg", [II], F32)
    v_ru = nc.dram_tensor("v_ru", [II], F32)
    v_rdn = nc.dram_tensor("v_rdn", [II], F32)
    v_rds = nc.dram_tensor("v_rds", [D], F32)
    cc1_in = nc.dram_tensor("cc1_in", [128, 1], F32)
    cc1_out = nc.dram_tensor("cc1_out", [128, 1], F32)
    cc2_in = nc.dram_tensor("cc2_in", [128, 1], F32)
    cc2_out = nc.dram_tensor("cc2_out", [128, 1], F32)
    dbg_xq = nc.dram_tensor("dbg_xq", [D, TOK], BF16)
    dbg_xr = nc.dram_tensor("dbg_xr", [TOK, D], BF16)
    dbg_xt = nc.dram_tensor("dbg_xt", [TOK, D], BF16)
    dbg_s = nc.dram_tensor("dbg_s", [128, 2 * NT], F32)

    prog = {k: [] for k in ("sync", "scalar", "vector", "tensor", "gpsimd")}
    watermark = {k: {} for k in prog}
    ENGS = ("sync", "scalar", "vector", "tensor", "gpsimd")

    def em(eng, fn):
        prog[eng].append(fn)

    def W(eng, ctr, val):
        if val is None or val <= 0:
            return
        wm = watermark[eng]
        if wm.get(id(ctr), -1) >= val:
            return
        wm[id(ctr)] = val
        em(eng, lambda e, h=ctr.h, v=val: e.wait_ge(h, v))

    def R(eng, maker, inc=None, inck=1, ss=True):
        if inc is None:
            em(eng, maker)
            return None
        v = inc.bump(inck)
        em(eng, lambda e, m=maker, h=inc.h, k=inck: m(e).then_inc(h, k))
        if ss and eng == "vector" and inck == 1:
            W(eng, inc, v)
        return v

    ctx_sems = []

    def mk_sem(name):
        cm = nc.semaphore(name)
        h = cm.__enter__()
        ctx_sems.append(cm)
        return Ctr(h)

    Spe = mk_sem("Spe")
    Sve = mk_sem("Sve")
    Sac = mk_sem("Sac")
    Scc = mk_sem("Scc")
    Dpers = mk_sem("Dpers")
    Drt = mk_sem("Drt")
    Dx = mk_sem("Dx")
    DwA = mk_sem("DwA")
    DwB = mk_sem("DwB")
    DwQ = mk_sem("DwQ")
    Dspill = mk_sem("Dspill")
    DawA = mk_sem("DawA")
    DawB = mk_sem("DawB")
    DA_bk = mk_sem("DA_bk"); DA_bv = mk_sem("DA_bv")
    DA_wk = mk_sem("DA_wk"); DA_wv = mk_sem("DA_wv")
    DB_wd = mk_sem("DB_wd"); DB_wq = mk_sem("DB_wq"); DB_bd = mk_sem("DB_bd")
    Dh0 = mk_sem("Dh0"); Dh1 = mk_sem("Dh1"); Dh2 = mk_sem("Dh2")
    DwC = mk_sem("DwC")
    DwD = mk_sem("DwD")
    Dhre = mk_sem("Dhre")
    Dout0 = mk_sem("Dout0"); Dout1 = mk_sem("Dout1")
    Ddbg = mk_sem("Ddbg")
    ALL_SEMS = [Spe, Sve, Sac, Dpers, Drt, Dx, DwA, DwB, DwQ, Dspill,
                DawA, DawB, DA_bk, DA_bv, DA_wk, DA_wv, DB_wd, DB_wq, DB_bd,
                Dh0, Dh1, Dh2, DwC, DwD, Dhre, Dout0, Dout1]

    def barrier():
        for eng in ENGS:
            for sm in ALL_SEMS:
                W(eng, sm, sm.n)

    stk = ExitStack()

    def sb(name, shape, dt):
        return stk.enter_context(nc.sbuf_tensor(name, shape, dt))

    x_res = sb("x_res", [128, NT, D], BF16)
    ident = sb("ident", [128, 128], BF16)
    ones1b = sb("ones1b", [1, 128], BF16)
    ones1f = sb("ones1f", [1, 128], F32)
    onescol = sb("onescol", [128, 1], BF16)
    bc_sb = sb("bc_sb", [16, HC * 128], BF16)
    bq_cols = sb("bq_cols", [128, HC], F32)
    clipx_sb = sb("clipx_sb", [1, 1], F32)
    clipd_sb = sb("clipd_sb", [1, 1], F32)
    mu1s = sb("mu1s", [128, NT], F32)
    rstd1s = sb("rstd1s", [128, NT], F32)
    nmu1s = sb("nmu1s", [128, NT], F32)
    mu2s = sb("mu2s", [128, NT], F32)
    rstd2s = sb("rstd2s", [128, NT], F32)
    s1s = sb("s1s", [128, NT], F32)
    s2s = sb("s2s", [128, NT], F32)
    kv_sb = sb("kv_sb", [128, 2, 1024], BF16)
    ksum_acc = sb("ksum_acc", [128, 16], F32)
    kcols = sb("kcols", [128, 2, HC, 16], BF16)
    scg_cols = sb("scg_cols", [128, IC], F32)
    scu_cols = sb("scu_cols", [128, IC], F32)
    scd_cols = sb("scd_cols", [128, HC], F32)
    sxg_cols = sb("sxg_cols", [128, IC], F32)
    sxu_cols = sb("sxu_cols", [128, IC], F32)
    ymax_acc = sb("ymax_acc", [128, 1], F32)
    hmax_acc = sb("hmax_acc", [128, 1], F32)
    dinv_col = sb("dinv_col", [128, 1], F32)
    eps1_col = sb("eps1_col", [128, 1], F32)
    epsi_col = sb("epsi_col", [128, 1], F32)
    yrow = sb("yrow", [1, 128], F32)
    hrow = sb("hrow", [1, 128], F32)
    sc1 = sb("sc1", [1, 4], F32)
    sx_col = sb("sx_col", [128, 1], F32)
    invsx_col = sb("invsx_col", [128, 1], F32)
    sd_col = sb("sd_col", [128, 1], F32)
    invsd_col = sb("invsd_col", [128, 1], F32)

    R("sync", lambda e: e.dma_start(out=ident[:], in_=ident_in), Dpers, 16)
    R("sync", lambda e: e.dma_start(out=ones1b[:], in_=ones1b_in), Dpers, 16)
    R("sync", lambda e: e.dma_start(out=ones1f[:], in_=ones1f_in), Dpers, 16)
    R("sync", lambda e: e.dma_start(out=onescol[:], in_=onescol_in), Dpers, 16)
    R("sync", lambda e: e.dma_start(out=bc_sb[:], in_=bc_in), Dpers, 16)
    R("sync", lambda e: e.dma_start(
        out=bq_cols[:], in_=bq_in.rearrange("(c p) -> p c", p=128),
        allow_slow_non_contiguous=True), Dpers, 16)
    R("sync", lambda e: e.dma_start(out=clipx_sb[:], in_=clipx_in), Dpers, 16)
    pers_x = R("sync", lambda e: e.dma_start(out=clipd_sb[:], in_=clipd_in), Dpers, 16)
    for eng in ("tensor", "vector", "scalar"):
        W(eng, Dpers, pers_x)

    R("vector", lambda e: e.memset(ksum_acc[:], 0.0), Sve)
    R("vector", lambda e: e.memset(dinv_col[:], 1.0 / D), Sve)
    R("vector", lambda e: e.memset(eps1_col[:], EPS_LN), Sve)
    R("vector", lambda e: e.memset(epsi_col[:], EPS_ILN), Sve)
    R("vector", lambda e: e.memset(kcols[:], 0.0), Sve)
    R("vector", lambda e: e.memset(kv_sb[:], 0.0), Sve)
    R("vector", lambda e: e.memset(ymax_acc[:], 0.0), Sve)
    R("vector", lambda e: e.memset(hmax_acc[:], 0.0), Sve)

    with ExitStack() as a_stk:
        def asb(name, shape, dt):
            return a_stk.enter_context(nc.sbuf_tensor(name, shape, dt))

        def aps(name, shape, dt=F32):
            return a_stk.enter_context(nc.psum_tensor(name, shape, dt))

        bk_row = asb("bk_row", [1, D], BF16)
        bv_row = asb("bv_row", [1, D], BF16)
        wk_sb = asb("wk_sb", [128, HC, D], BF16)
        wv_sb = asb("wv_sb", [128, HC, D], BF16)
        xf32A = asb("xf32A", [128, D], F32)
        xf32B = asb("xf32B", [128, D], F32)
        xlntokA = asb("xlntokA", [128, D], BF16)
        xlntokB = asb("xlntokB", [128, D], BF16)
        xfmA = asb("xfmA", [128, HC, 128], BF16)
        xfmB = asb("xfmB", [128, HC, 128], BF16)
        k_act = asb("k_act", [128, D], BF16)
        v_act = asb("v_act", [128, D], BF16)
        asq = asb("asq", [128, D], F32)
        vcol = asb("vcol", [128, 1], F32)
        scol = asb("scol", [128, 1], F32)
        qcol = asb("qcol", [128, 1], F32)
        scrE = asb("scrE", [128, 512], F32)
        scrR = asb("scrR", [128, 512], F32)
        ksum_t = asb("ksum_t", [128, 8], F32)
        kv_acc = asb("kv_acc", [128, 512], F32)

        ps_k = aps("ps_k", [128, 2, 512])
        ps_v = aps("ps_v", [128, 2, 512])
        ps_kv = aps("ps_kv", [128, 512])
        ps_ks = aps("ps_ks", [128, 512])
        ps_tr = aps("ps_tr", [128, 4, 128])

        wstA = asb("wstA", [128, II], F32)
        wstB = asb("wstB", [128, II], F32)
        r_b = asb("r_b", [128, II], F32)
        wi32 = asb("wi32", [128, II], I32)
        wint = asb("wint", [128, II], BF16)
        rrow = asb("rrow", [1, II], F32)
        rcols = asb("rcols", [128, IC], F32)
        ps_bc = aps("ps_bc", [128, 512])

        setup_units = []
        pstate = {"buf_rd": [0, 0], "wint_sp": 0, "bc_evac": 0}
        plvs = {}

        def prep_weight_units(nat_ap, sc_cols, n_chunks, nat_w,
                              T_ap, nT_chunks, T_width, vec_hbm, spill_ap, wid):
            def ld(c):
                buf = wstA if c % 2 == 0 else wstB
                dsem = DwA if c % 2 == 0 else DwB
                def f(c=c, buf=buf, dsem=dsem):
                    W("scalar", Sve, pstate["buf_rd"][c % 2])
                    plvs[(wid, "r", c)] = R(
                        "scalar", lambda e, c=c, buf=buf: e.dma_start(
                            out=buf[:, 0:nat_w],
                            in_=nat_ap[c * 128:(c + 1) * 128, :]), dsem, 16)
                return f

            def red(c):
                buf = wstA if c % 2 == 0 else wstB
                dsem = DwA if c % 2 == 0 else DwB
                def f(c=c, buf=buf, dsem=dsem):
                    W("vector", dsem, plvs[(wid, "r", c)])
                    pstate["buf_rd"][c % 2] = R(
                        "vector", lambda e, c=c, buf=buf: e.tensor_reduce(
                            out=sc_cols[:, c:c + 1], in_=buf[:, 0:nat_w],
                            axis=AX.X, op=ALU.max,
                            apply_absolute_value=True), Sve)
                return f

            def glue1():
                def f():
                    R("vector", lambda e: e.tensor_scalar(
                        out=sc_cols[:, 0:n_chunks], in0=sc_cols[:, 0:n_chunks],
                        scalar1=R127, scalar2=None, op0=ALU.mult), Sve)
                    rv = R("vector", lambda e: e.reciprocal(
                        out=rcols[:, 0:n_chunks], in_=sc_cols[:, 0:n_chunks]), Sve)
                    W("sync", Sve, rv)
                    wv = R("sync", lambda e: e.dma_start(
                        out=vec_hbm.ap().rearrange("(c p) -> p c", p=128)[:, 0:n_chunks],
                        in_=rcols[:, 0:n_chunks],
                        allow_slow_non_contiguous=True), Drt, 16)
                    W("sync", Drt, wv)
                    plvs[(wid, "rt")] = R("sync", lambda e: e.dma_start(
                        out=rrow[:, 0:T_width],
                        in_=vec_hbm.ap().rearrange("(a i) -> a i", a=1)[:, 0:T_width]),
                        Drt, 16)
                return f

            def glue2():
                def f():
                    W("tensor", Drt, plvs[(wid, "rt")])
                    nb = (T_width + 511) // 512
                    for b_ in range(nb):
                        lo, hi = b_ * 512, min(T_width, b_ * 512 + 512)
                        W("tensor", Sve, pstate["bc_evac"])
                        mv = R("tensor", lambda e, lo=lo, hi=hi: e.matmul(
                            ps_bc[:, 0:hi - lo], ones1f[:1, :], rrow[:1, lo:hi],
                            start=True, stop=True), Spe)
                        W("vector", Spe, mv)
                        pstate["bc_evac"] = R(
                            "vector", lambda e, lo=lo, hi=hi: e.tensor_copy(
                                out=r_b[:, lo:hi], in_=ps_bc[:, 0:hi - lo]), Sve)
                return f

            def qld(c):
                buf = wstA if c % 2 == 0 else wstB
                dsem = DwA if c % 2 == 0 else DwB
                def f(c=c, buf=buf, dsem=dsem):
                    W("scalar", Sve, pstate["buf_rd"][c % 2])
                    plvs[(wid, "q", c)] = R(
                        "scalar", lambda e, c=c, buf=buf: e.dma_start(
                            out=buf[:, 0:T_width],
                            in_=T_ap[c * 128:(c + 1) * 128, :]), dsem, 16)
                return f

            def quant(c):
                buf = wstA if c % 2 == 0 else wstB
                dsem = DwA if c % 2 == 0 else DwB
                def f(c=c, buf=buf, dsem=dsem):
                    W("vector", dsem, plvs[(wid, "q", c)])
                    W("vector", Spe, pstate.get("bc_pe", 0))
                    R("vector", lambda e, buf=buf: e.tensor_mul(
                        buf[:, 0:T_width], buf[:, 0:T_width],
                        r_b[:, 0:T_width]), Sve, 1, False)
                    pstate["buf_rd"][c % 2] = R(
                        "vector", lambda e, buf=buf: e.tensor_scalar(
                            out=wi32[:, 0:T_width], in0=buf[:, 0:T_width],
                            scalar1=MAGIC, scalar2=None, op0=ALU.add), Sve, 1, False)
                    W("vector", Dspill, pstate["wint_sp"])
                    qv = R("vector", lambda e: e.tensor_scalar(
                        out=wint[:, 0:T_width], in0=wi32[:, 0:T_width],
                        scalar1=-MAGIC, scalar2=None, op0=ALU.add), Sve, 1, False)
                    W("scalar", Sve, qv)
                    pstate["wint_sp"] = R(
                        "scalar", lambda e, c=c: e.dma_start(
                            out=spill_ap.ap()[c * 128:(c + 1) * 128, :],
                            in_=wint[:, 0:T_width]), Dspill, 16)
                return f

            def combine(*fs):
                def f():
                    for g_ in fs:
                        g_()
                return f

            u = [combine(ld(0), ld(1))]
            for c in range(n_chunks):
                fs = [red(c)]
                if c + 2 < n_chunks:
                    fs.append(ld(c + 2))
                u.append(combine(*fs))
            u.append(glue1())
            u.append(glue2())
            u.append(combine(qld(0), qld(1)))
            for c in range(nT_chunks):
                fs = [quant(c)]
                if c + 2 < nT_chunks:
                    fs.append(qld(c + 2))
                u.append(combine(*fs))
            return u

        if not (stage.endswith('_np') or stage.endswith('_min')):
            setup_units += prep_weight_units(
                wg_n, scg_cols, IC, D, wgT, HC, II, v_rg, wg_sp, "g")
            setup_units += prep_weight_units(
                wu_n, scu_cols, IC, D, wuT, HC, II, v_ru, wu_sp, "u")
            setup_units += prep_weight_units(
                wdn_n, scd_cols, HC, II, wdnT, IC, D, v_rdn, wdn_sp, "d")

        R("vector", lambda e: e.memset(kv_acc[:], 0.0), Sve)
        R("gpsimd", lambda e: e.dma_start(out=bk_row[:], in_=bk_in), DA_bk, 16)
        R("gpsimd", lambda e: e.dma_start(out=bv_row[:], in_=bv_in), DA_bv, 16)
        R("gpsimd", lambda e: e.dma_start(
            out=wk_sb[:], in_=wkT.rearrange("(c p) j -> p c j", p=128)), DA_wk, 16)
        R("gpsimd", lambda e: e.dma_start(
            out=wv_sb[:], in_=wvT.rearrange("(c p) j -> p c j", p=128)), DA_wv, 16)

        state = {
            "xf32_ve": [0, 0], "xf32_ac": [0, 0], "asq_rd": 0, "vcol_rd": 0,
            "xlntok_pe": [0, 0],
            "tr_evac": 0,
            "xfm_pe": [0, 0],
            "kv_mm": 0,
            "psk_rd": 0, "scr_rd": 0,
            "ksum_evac": 0, "kv_evac": 0,
        }

        def emit_ln1_tile(T):
            xf32 = xf32A if T % 2 == 0 else xf32B
            W("sync", Sve, state["xf32_ve"][T % 2])
            W("sync", Sac, state["xf32_ac"][T % 2])
            lv = R("sync", lambda e, T=T, xf32=xf32: e.dma_start(
                out=xf32[:], in_=x4[128 * T:128 * (T + 1), :]), Dx, 16)
            W("scalar", Dx, lv)
            W("scalar", Sve, state["asq_rd"])
            sqv = R("scalar", lambda e, xf32=xf32: e.activation(
                out=asq[:], in_=xf32[:], func=AF.Square), Sac)
            W("vector", Dx, lv)
            scv = R("vector", lambda e, xf32=xf32: e.tensor_reduce(
                out=scol[:], in_=xf32[:], axis=AX.X, op=ALU.add), Sve)
            W("vector", Sac, sqv)
            qv_ = R("vector", lambda e: e.tensor_reduce(
                out=qcol[:], in_=asq[:], axis=AX.X, op=ALU.add), Sve)
            state["asq_rd"] = qv_
            W("scalar", Sve, scv)
            W("scalar", Sve, state["vcol_rd"])
            R("scalar", lambda e, T=T: e.activation(
                out=mu1s[:, T:T + 1], in_=scol[:], func=AF.Identity,
                scale=dinv_col[:]), Sac)
            msv = R("scalar", lambda e: e.activation(
                out=vcol[:], in_=scol[:], func=AF.Square, scale=dinv_col[:]), Sac)
            W("vector", Sac, msv)
            av = R("vector", lambda e: e.tensor_scalar(
                out=qcol[:], in0=qcol[:], scalar1=1.0 / D, scalar2=vcol[:],
                op0=ALU.mult, op1=ALU.subtract), Sve)
            state["vcol_rd"] = av
            W("scalar", Sve, av)
            sv = R("scalar", lambda e: e.activation(
                out=scol[:], in_=qcol[:], func=AF.Sqrt, bias=eps1_col[:]), Sac)
            W("vector", Sac, sv)
            R("vector", lambda e, T=T: e.reciprocal(out=rstd1s[:, T:T + 1], in_=scol[:]), Sve)
            nv = R("vector", lambda e, T=T: e.tensor_scalar(
                out=nmu1s[:, T:T + 1], in0=rstd1s[:, T:T + 1],
                scalar1=mu1s[:, T:T + 1], scalar2=-1.0,
                op0=ALU.mult, op1=ALU.mult), Sve)
            cv = R("vector", lambda e, T=T, xf32=xf32: e.tensor_copy(
                out=x_res[:, T, :], in_=xf32[:]), Sve, 1, False)
            xt = xlntokA if T % 2 == 0 else xlntokB
            W("scalar", Spe, state["xlntok_pe"][T % 2])
            W("scalar", Sve, nv)
            xv = R("scalar", lambda e, xt=xt, T=T, xf32=xf32: e.activation(
                out=xt[:], in_=xf32[:], func=AF.Identity,
                bias=nmu1s[:, T:T + 1], scale=rstd1s[:, T:T + 1]), Sac)
            state["xf32_ve"][T % 2] = cv
            state["xf32_ac"][T % 2] = xv
            return xt, xv

        def emit_transposes(T, xt, xv):
            xfm = xfmA if T % 2 == 0 else xfmB
            W("tensor", Sac, xv)
            W("tensor", Spe, state["xfm_pe"][T % 2])
            last_mv = None
            for half in range(2):
                W("tensor", Sve, state["tr_evac"])
                for cc in range(4):
                    c = 4 * half + cc
                    last_mv = R("tensor", lambda e, c=c, cc=cc, xt=xt: e.matmul(
                        ps_tr[:, cc, :], xt[:, 128 * c:128 * (c + 1)], ident[:],
                        start=True, stop=True), Spe)
                W("vector", Spe, last_mv)
                state["tr_evac"] = R("vector", lambda e, half=half, xfm=xfm: e.tensor_copy(
                    out=xfm[:, 4 * half:4 * half + 4, :], in_=ps_tr[:]), Sve, 1, False)
            state["xlntok_pe"][T % 2] = last_mv
            return xfm

        def emit_kv_tile(T, seq, tis, xfm):
            W("tensor", Sve, state["psk_rd"])
            W("tensor", Sve, state["tr_evac"])
            W("tensor", DA_bk, 16)
            W("tensor", DA_bv, 16)
            W("tensor", DA_wk, 16)
            W("tensor", DA_wv, 16)
            kv_last = None
            for tgt, wsb, brow in ((ps_k, wk_sb, bk_row), (ps_v, wv_sb, bv_row)):
                for fh in range(2):
                    for c in range(HC):
                        R("tensor", lambda e, tgt=tgt, wsb=wsb, c=c, fh=fh, xfm=xfm:
                          e.matmul(tgt[:, fh, :], xfm[:, c, :],
                                   wsb[:, c, 512 * fh:512 * (fh + 1)],
                                   start=(c == 0), stop=False), Spe)
                    kv_last = R("tensor", lambda e, tgt=tgt, brow=brow, fh=fh: e.matmul(
                        tgt[:, fh, :], ones1b[:1, :],
                        brow[:1, 512 * fh:512 * (fh + 1)],
                        start=False, stop=True), Spe)
            state["xfm_pe"][T % 2] = kv_last
            W("scalar", Spe, kv_last)
            W("vector", Spe, state["kv_mm"])
            for fh in range(2):
                W("scalar", Sve, state["psk_rd"])
                e1 = R("scalar", lambda e, fh=fh: e.activation(
                    out=scrE[:], in_=ps_k[:, fh, :], func=AF.Exp), Sac)
                W("vector", Sac, e1)
                R("vector", lambda e, fh=fh: e.tensor_scalar(
                    out=scrR[:], in0=ps_k[:, fh, :], scalar1=0.0,
                    scalar2=None, op0=ALU.max), Sve, 1, False)
                ev2 = R("vector", lambda e, fh=fh: e.scalar_tensor_tensor(
                    out=k_act[:, 512 * fh:512 * (fh + 1)], in0=scrE[:],
                    scalar=1.0, in1=scrR[:], op0=ALU.min, op1=ALU.add), Sve, 1, False)
                state["psk_rd"] = ev2
            R("vector", lambda e: e.tensor_copy(out=v_act[:, 0:512], in_=ps_v[:, 0, :]), Sve, 1, False)
            vv_ = R("vector", lambda e: e.tensor_copy(out=v_act[:, 512:1024], in_=ps_v[:, 1, :]), Sve, 1, False)
            state["psk_rd"] = vv_
            W("tensor", Sve, vv_)
            W("tensor", Sve, state["kv_evac"])
            kvm = None
            for hd in range(16):
                hh = hd % 2
                p = hd // 2
                kvm = R("tensor", lambda e, hd=hd, hh=hh, p=p: e.matmul(
                    ps_kv[64 * hh:64 * (hh + 1), 64 * p:64 * (p + 1)],
                    k_act[:, 64 * hd:64 * (hd + 1)],
                    v_act[:, 64 * hd:64 * (hd + 1)],
                    start=True, stop=True), Spe)
            W("vector", Spe, kvm)
            state["kv_evac"] = R("vector", lambda e: e.tensor_tensor(
                out=kv_acc[:], in0=kv_acc[:], in1=ps_kv[:], op=ALU.add), Sve, 1, False)
            W("tensor", Sve, state["ksum_evac"])
            km = None
            for hd in range(16):
                hh = hd % 2
                p = hd // 2
                km = R("tensor", lambda e, hd=hd, hh=hh, p=p: e.matmul(
                    ps_ks[64 * hh:64 * (hh + 1), p:p + 1],
                    k_act[:, 64 * hd:64 * (hd + 1)], onescol[:],
                    start=True, stop=True), Spe)
            state["kv_mm"] = km
            W("vector", Spe, km)
            R("vector", lambda e: e.tensor_copy(out=ksum_t[:], in_=ps_ks[:, 0:8]), Sve)
            state["ksum_evac"] = R("vector", lambda e, seq=seq: e.tensor_tensor(
                out=ksum_acc[:, 8 * seq:8 * seq + 8],
                in0=ksum_acc[:, 8 * seq:8 * seq + 8],
                in1=ksum_t[:], op=ALU.add), Sve)

        minimal = stage.endswith("_min")
        n_su = len(setup_units)
        su_popped = 0
        ln1_out = [emit_ln1_tile(0), emit_ln1_tile(1)]
        for T in range(NT):
            seq = T // NTS
            xt, xv = ln1_out[T % 2]
            if not minimal:
                xfm = emit_transposes(T, xt, xv)
            if T + 2 < NT:
                ln1_out[T % 2] = emit_ln1_tile(T + 2)
            if minimal:
                continue
            emit_kv_tile(T, seq, T % NTS, xfm)
            want = (T + 1) * n_su // NT
            while su_popped < want:
                setup_units[su_popped]()
                su_popped += 1

            if T % NTS == NTS - 1 and not minimal:
                for p in range(8):
                    R("vector", lambda e, seq=seq, p=p: e.tensor_copy(
                        out=kv_sb[0:64, seq, 128 * p:128 * p + 64],
                        in_=kv_acc[0:64, 64 * p:64 * (p + 1)]), Sve)
                    R("vector", lambda e, seq=seq, p=p: e.tensor_copy(
                        out=kv_sb[64:128, seq, 128 * p + 64:128 * p + 128],
                        in_=kv_acc[64:128, 64 * p:64 * (p + 1)]), Sve)
                if seq == 0:
                    R("vector", lambda e: e.memset(kv_acc[:], 0.0), Sve)
                R("vector", lambda e, seq=seq: e.tensor_scalar(
                    out=ksum_acc[:, 8 * seq:8 * seq + 8],
                    in0=ksum_acc[:, 8 * seq:8 * seq + 8],
                    scalar1=EPS_ATTN, scalar2=None, op0=ALU.add), Sve)
                kcv = None
                for c in range(HC):
                    R("vector", lambda e, seq=seq, c=c: e.tensor_scalar(
                        out=kcols[0:64, seq, c, 2 * c:2 * c + 1],
                        in0=onescol[0:64, :],
                        scalar1=ksum_acc[0:64, 8 * seq + c:8 * seq + c + 1],
                        scalar2=None, op0=ALU.mult), Sve)
                    kcv = R("vector", lambda e, seq=seq, c=c: e.tensor_scalar(
                        out=kcols[64:128, seq, c, 2 * c + 1:2 * c + 2],
                        in0=onescol[64:128, :],
                        scalar1=ksum_acc[64:128, 8 * seq + c:8 * seq + c + 1],
                        scalar2=None, op0=ALU.mult), Sve)
                W("tensor", Sve, kcv)

    barrier()

    if stage == "A":
        _dbg_out(W, R, outb, x_res, Sve, Ddbg)
        barrier()
        return _finish(nc, prog, stk, ctx_sems, cleanup_cm)

    if stage.startswith("Astats"):
        W("gpsimd", Sve, Sve.n)
        R("gpsimd", lambda e: e.dma_start(
            out=outb.rearrange("a b -> (a b)")[0:128 * 32]
            .rearrange("(p f) -> p f", p=128), in_=mu1s[:]), Ddbg, 16)
        R("gpsimd", lambda e: e.dma_start(
            out=outb.rearrange("a b -> (a b)")[128 * 32:128 * 64]
            .rearrange("(p f) -> p f", p=128), in_=rstd1s[:]), Ddbg, 16)
        W("gpsimd", Ddbg, Ddbg.n)
        W("sync", Ddbg, Ddbg.n)
        barrier()
        return _finish(nc, prog, stk, ctx_sems, cleanup_cm)

    with ExitStack() as b_stk:
        def bsb(name, shape, dt):
            return b_stk.enter_context(nc.sbuf_tensor(name, shape, dt))

        def bps(name, shape, dt=F32):
            return b_stk.enter_context(nc.psum_tensor(name, shape, dt))

        wq_sb = bsb("wq_sb", [128, HC, D], BF16)
        wd_sb = bsb("wd_sb", [128, HC, D], BF16)
        gam_row = bsb("gam_row", [1, D], F32)
        bet_row = bsb("bet_row", [1, D], F32)
        bd_row = bsb("bd_row", [1, D], BF16)
        gamma_b = bsb("gamma_b", [128, D], F32)
        beta_b = bsb("beta_b", [128, D], F32)
        xlntok_gA = bsb("xlntok_gA", [128, TPG, D], BF16)
        xlntok_gB = bsb("xlntok_gB", [128, TPG, D], BF16)
        xfm_gA = bsb("xfm_gA", [128, HC, 512], BF16)
        xfm_gB = bsb("xfm_gB", [128, HC, 512], BF16)
        q_gA = bsb("q_gA", [128, HC, 512], BF16)
        q_gB = bsb("q_gB", [128, HC, 512], BF16)
        ctx_sb = bsb("ctx_sb", [128, HC, 512], BF16)
        z_all = bsb("z_all", [16, 512], BF16)
        z_f32 = bsb("z_f32", [16, 512], F32)
        zb_sb = bsb("zb_sb", [128, 512], BF16)
        hs_sb = bsb("hs_sb", [128, D], F32)
        ym_sb = bsb("ym_sb", [128, D], F32)
        vcolb = bsb("vcolb", [128, 1], F32)
        scolb = bsb("scolb", [128, 1], F32)
        qcolb = bsb("qcolb", [128, 1], F32)
        mucol = bsb("mucol", [128, 1], F32)
        rstdi = bsb("rstdi", [128, 1], F32)
        nmui = bsb("nmui", [128, 1], F32)
        ycol = bsb("ycol", [128, 1], F32)
        scrEb = bsb("scrEb", [128, 512], F32)
        scrRb = bsb("scrRb", [128, 512], F32)

        ps_tr_b = bps("ps_tr_b", [128, 4, 128])
        ps_tr_b2 = bps("ps_tr_b2", [128, 4, 128])
        ps_q1 = bps("ps_q1", [128, 512])
        ps_q2 = bps("ps_q2", [128, 512])
        ps_dn = bps("ps_dn", [16, 512])
        ps_zb = bps("ps_zb", [128, 512])
        ps_hs = bps("ps_hs", [128, 1024])

        R("sync", lambda e: e.dma_start(out=gam_row[:], in_=gamma_in), DawB, 16)
        R("sync", lambda e: e.dma_start(out=bet_row[:], in_=beta_in), DawB, 16)
        R("gpsimd", lambda e: e.dma_start(
            out=wd_sb[:], in_=wdT.rearrange("(c p) j -> p c j", p=128)), DB_wd, 16)
        R("gpsimd", lambda e: e.dma_start(
            out=wq_sb[:], in_=wqT.rearrange("(c p) j -> p c j", p=128)), DB_wq, 16)
        R("gpsimd", lambda e: e.dma_start(out=bd_row[:], in_=bd_in), DB_bd, 16)
        W("tensor", DB_wd, 16)
        W("tensor", DB_wq, 16)
        W("tensor", DB_bd, 16)
        for row, dstb in ((gam_row, gamma_b), (bet_row, beta_b)):
            for half in range(2):
                mv = R("tensor", lambda e, row=row, half=half: e.matmul(
                    ps_zb[:], ones1f[:1, :], row[:1, 512 * half:512 * (half + 1)],
                    start=True, stop=True), Spe)
                W("vector", Spe, mv)
                cv = R("vector", lambda e, dstb=dstb, half=half: e.tensor_copy(
                    out=dstb[:, 512 * half:512 * (half + 1)], in_=ps_zb[:]), Sve)
                W("tensor", Sve, cv)

        bstate = {
            "tr_evac": 0, "tr_evac2": [0, 0],
            "xlntok_rd": [0, 0], "tr_pe": [0, 0], "xfm_pe": [0, 0],
            "q_stt": [0, 0], "bank_free": [0, 0],
            "ctx_pe": 0, "zb_evac": 0, "hs_rd": 0, "scr_rd": 0,
            "vcolb_rd": 0,
        }

        def prep_xln(g):
            par = g % 2
            xlg = xlntok_gA if par == 0 else xlntok_gB
            xvs = []
            for tl in range(TPG):
                T = TPG * g + tl
                W("scalar", Spe, bstate["tr_pe"][par])
                W("scalar", Sve, bstate["xlntok_rd"][par])
                xvs.append(R("scalar", lambda e, T=T, tl=tl, xlg=xlg: e.activation(
                    out=xlg[:, tl, :], in_=x_res[:, T, :], func=AF.Identity,
                    bias=nmu1s[:, T:T + 1], scale=rstd1s[:, T:T + 1]), Sac))
            return xvs

        def prep_tr_tile(g, tl, xv):
            par = g % 2
            xlg = xlntok_gA if par == 0 else xlntok_gB
            xfg = xfm_gA if par == 0 else xfm_gB
            W("tensor", Sac, xv)
            last_mv = None
            for half in range(2):
                ps = ps_tr_b if half == 0 else ps_tr_b2
                W("tensor", Sve, bstate["tr_evac2"][half])
                for cc in range(4):
                    c = 4 * half + cc
                    last_mv = R("tensor", lambda e, c=c, cc=cc, tl=tl, xlg=xlg, ps=ps:
                                e.matmul(ps[:, cc, :],
                                         xlg[:, tl, 128 * c:128 * (c + 1)],
                                         ident[:], start=True, stop=True), Spe)
                W("vector", Spe, last_mv)
                ev = R(
                    "vector", lambda e, half=half, tl=tl, xfg=xfg, ps=ps: e.tensor_copy(
                        out=xfg[:, 4 * half:4 * half + 4, 128 * tl:128 * (tl + 1)],
                        in_=ps[:]), Sve, 1, False)
                bstate["tr_evac2"][half] = ev
                bstate["tr_evac"] = ev
            if tl == TPG - 1:
                bstate["tr_pe"][par] = last_mv

        def prep_q(g):
            par = g % 2
            xfg = xfm_gA if par == 0 else xfm_gB
            qg = q_gA if par == 0 else q_gB
            W("tensor", Sve, bstate["tr_evac"])
            for j in range(HC):
                bank = ps_q1 if j % 2 == 0 else ps_q2
                W("tensor", Sve, bstate["bank_free"][j % 2])
                mv = None
                for c in range(HC):
                    mv = R("tensor", lambda e, c=c, j=j, bank=bank, xfg=xfg: e.matmul(
                        bank[:], wq_sb[:, c, 128 * j:128 * (j + 1)],
                        xfg[:, c, :], start=(c == 0), stop=(c == HC - 1)), Spe)
                W("scalar", Spe, mv)
                W("scalar", Sve, bstate["scr_rd"])
                ev1 = R("scalar", lambda e, j=j, bank=bank: e.activation(
                    out=scrEb[:], in_=bank[:], func=AF.Exp,
                    bias=bq_cols[:, j:j + 1], scale=1.0), Sac)
                W("vector", Sac, ev1)
                R("vector", lambda e, j=j, bank=bank: e.tensor_scalar(
                    out=scrRb[:], in0=bank[:], scalar1=bq_cols[:, j:j + 1],
                    scalar2=0.0, op0=ALU.add, op1=ALU.max), Sve, 1, False)
                W("vector", Spe, bstate["q_pe"][par] if "q_pe" in bstate else 0)
                ev2 = R("vector", lambda e, j=j, qg=qg: e.scalar_tensor_tensor(
                    out=qg[:, j, :], in0=scrEb[:], scalar=1.0,
                    in1=scrRb[:], op0=ALU.min, op1=ALU.add), Sve, 1, False)
                bstate["bank_free"][j % 2] = ev2
                bstate["scr_rd"] = ev2
            bstate["q_stt"][par] = bstate["bank_free"][(HC - 1) % 2]

        def prep_group(g):
            xvs = prep_xln(g)
            for tl in range(TPG):
                prep_tr_tile(g, tl, xvs[tl])
            prep_q(g)

        bstate["q_pe"] = [0, 0]
        prep_group(0)

        for g in range(NG):
            seq = g // GPS
            par = g % 2
            xlg = xlntok_gA if par == 0 else xlntok_gB
            qg = q_gA if par == 0 else q_gB
            W("tensor", Sve, bstate["q_stt"][par])
            dv = None
            for c in range(HC):
                dv = R("tensor", lambda e, c=c, seq=seq, qg=qg: e.matmul(
                    ps_dn[:], kcols[:, seq, c, :], qg[:, c, :],
                    start=(c == 0), stop=(c == HC - 1)), Spe)
            W("vector", Spe, dv)
            R("vector", lambda e: e.reciprocal(out=z_f32[:], in_=ps_dn[:]), Sve)
            zv = R("vector", lambda e: e.tensor_copy(out=z_all[:], in_=z_f32[:]), Sve)
            W("tensor", Sve, zv)
            first_ctx = True
            for c in range(HC):
                W("tensor", Sve, bstate["zb_evac"])
                bv_ = R("tensor", lambda e, c=c: e.matmul(
                    ps_zb[:], bc_sb[:, 128 * c:128 * (c + 1)], z_all[:],
                    start=True, stop=True), Spe)
                W("vector", Spe, bv_)
                bstate["zb_evac"] = R("vector", lambda e: e.tensor_copy(
                    out=zb_sb[:], in_=ps_zb[:]), Sve, 1, False)
                bank = ps_q1 if c % 2 == 0 else ps_q2
                W("tensor", Sve, bstate["bank_free"][c % 2])
                cv = None
                for hh in range(2):
                    cv = R("tensor", lambda e, c=c, hh=hh, seq=seq, bank=bank, qg=qg:
                           e.matmul(bank[64 * hh:64 * (hh + 1), :],
                                    kv_sb[:, seq, 128 * c + 64 * hh:128 * c + 64 * hh + 64],
                                    qg[:, c, :], start=True, stop=True), Spe)
                W("vector", Spe, cv)
                W("vector", Sve, bstate["zb_evac"])
                if first_ctx:
                    W("vector", Spe, bstate["ctx_pe"])
                    first_ctx = False
                ev = R("vector", lambda e, c=c, bank=bank: e.tensor_tensor(
                    out=ctx_sb[:, c, :], in0=bank[:], in1=zb_sb[:],
                    op=ALU.mult), Sve, 1, False)
                bstate["bank_free"][c % 2] = ev
            bstate["q_pe"][par] = Spe.n
            ctx_done = bstate["bank_free"][:]
            xvs_next = prep_xln(g + 1) if g + 1 < NG else None
            W("tensor", Sve, ctx_done[0])
            W("tensor", Sve, ctx_done[1])
            W("tensor", DB_wd, 16)
            for tl in range(TPG):
                T = TPG * g + tl
                tloc = 128 * tl
                W("tensor", Sve, bstate["hs_rd"])
                mv = None
                for jh in range(2):
                    for c in range(HC):
                        R("tensor", lambda e, c=c, jh=jh, tloc=tloc: e.matmul(
                            ps_hs[:, 512 * jh:512 * (jh + 1)],
                            ctx_sb[:, c, tloc:tloc + 128],
                            wd_sb[:, c, 512 * jh:512 * (jh + 1)],
                            start=(c == 0), stop=False), Spe)
                    mv = R("tensor", lambda e, jh=jh: e.matmul(
                        ps_hs[:, 512 * jh:512 * (jh + 1)], ones1b[:1, :],
                        bd_row[:1, 512 * jh:512 * (jh + 1)],
                        start=False, stop=True), Spe)
                W("vector", Spe, mv)
                hv = R("vector", lambda e, tl=tl, xlg=xlg: e.scalar_tensor_tensor(
                    out=hs_sb[:], in0=ps_hs[:], scalar=0.0, in1=xlg[:, tl, :],
                    op0=ALU.add, op1=ALU.add), Sve, 1, False)
                bstate["hs_rd"] = hv
                if tl == TPG - 1:
                    bstate["xlntok_rd"][par] = hv
                W("scalar", Sve, hv)
                sqv = R("scalar", lambda e: e.activation(
                    out=ym_sb[:], in_=hs_sb[:], func=AF.Square), Sac)
                scv = R("vector", lambda e: e.tensor_reduce(
                    out=scolb[:], in_=hs_sb[:], axis=AX.X, op=ALU.add), Sve)
                W("vector", Sac, sqv)
                R("vector", lambda e: e.tensor_reduce(
                    out=qcolb[:], in_=ym_sb[:], axis=AX.X, op=ALU.add), Sve)
                W("scalar", Sve, scv)
                W("scalar", Sve, bstate["vcolb_rd"])
                R("scalar", lambda e: e.activation(
                    out=mucol[:], in_=scolb[:], func=AF.Identity,
                    scale=dinv_col[:]), Sac)
                msv = R("scalar", lambda e: e.activation(
                    out=vcolb[:], in_=scolb[:], func=AF.Square, scale=dinv_col[:]), Sac)
                W("vector", Sac, msv)
                av = R("vector", lambda e: e.tensor_scalar(
                    out=qcolb[:], in0=qcolb[:], scalar1=1.0 / D, scalar2=vcolb[:],
                    op0=ALU.mult, op1=ALU.subtract), Sve)
                bstate["vcolb_rd"] = av
                W("scalar", Sve, av)
                sv = R("scalar", lambda e: e.activation(
                    out=scolb[:], in_=qcolb[:], func=AF.Sqrt, bias=epsi_col[:]), Sac)
                W("vector", Sac, sv)
                R("vector", lambda e: e.reciprocal(out=rstdi[:], in_=scolb[:]), Sve)
                nv = R("vector", lambda e: e.tensor_scalar(
                    out=nmui[:], in0=rstdi[:], scalar1=mucol[:], scalar2=-1.0,
                    op0=ALU.mult, op1=ALU.mult), Sve)
                W("scalar", Sve, nv)
                hn = R("scalar", lambda e: e.activation(
                    out=hs_sb[:], in_=hs_sb[:], func=AF.Identity,
                    bias=nmui[:], scale=rstdi[:]), Sac)
                W("vector", Sac, hn)
                R("vector", lambda e: e.scalar_tensor_tensor(
                    out=hs_sb[:], in0=hs_sb[:], scalar=0.0, in1=gamma_b[:],
                    op0=ALU.add, op1=ALU.mult), Sve, 1, False)
                R("vector", lambda e: e.tensor_add(hs_sb[:], hs_sb[:], beta_b[:]), Sve, 1, False)
                xmv = R("vector", lambda e, T=T: e.tensor_tensor(
                    out=x_res[:, T, :], in0=hs_sb[:], in1=x_res[:, T, :],
                    op=ALU.add), Sve, 1, False)
                W("scalar", Sve, xmv)
                sqv = R("scalar", lambda e, T=T: e.activation(
                    out=ym_sb[:], in_=x_res[:, T, :], func=AF.Square), Sac)
                scv = R("vector", lambda e, T=T: e.tensor_reduce(
                    out=scolb[:], in_=x_res[:, T, :], axis=AX.X, op=ALU.add), Sve)
                W("vector", Sac, sqv)
                R("vector", lambda e: e.tensor_reduce(
                    out=qcolb[:], in_=ym_sb[:], axis=AX.X, op=ALU.add), Sve)
                W("scalar", Sve, scv)
                W("scalar", Sve, bstate["vcolb_rd"])
                R("scalar", lambda e, T=T: e.activation(
                    out=mu2s[:, T:T + 1], in_=scolb[:], func=AF.Identity,
                    scale=dinv_col[:]), Sac)
                msv = R("scalar", lambda e: e.activation(
                    out=vcolb[:], in_=scolb[:], func=AF.Square, scale=dinv_col[:]), Sac)
                W("vector", Sac, msv)
                av = R("vector", lambda e: e.tensor_scalar(
                    out=qcolb[:], in0=qcolb[:], scalar1=1.0 / D, scalar2=vcolb[:],
                    op0=ALU.mult, op1=ALU.subtract), Sve)
                bstate["vcolb_rd"] = av
                W("scalar", Sve, av)
                sv = R("scalar", lambda e: e.activation(
                    out=scolb[:], in_=qcolb[:], func=AF.Sqrt, bias=eps1_col[:]), Sac)
                W("vector", Sac, sv)
                R("vector", lambda e, T=T: e.reciprocal(out=rstd2s[:, T:T + 1], in_=scolb[:]), Sve)
                R("vector", lambda e, T=T: e.tensor_scalar(
                    out=ym_sb[:], in0=x_res[:, T, :], scalar1=mu2s[:, T:T + 1],
                    scalar2=None, op0=ALU.subtract), Sve, 1, False)
                R("vector", lambda e: e.tensor_reduce(
                    out=ycol[:], in_=ym_sb[:], axis=AX.X, op=ALU.max,
                    apply_absolute_value=True), Sve)
                R("vector", lambda e, T=T: e.tensor_single_scalar(
                    out=ycol[:], in_=ycol[:], scalar=rstd2s[:, T:T + 1],
                    op=ALU.mult), Sve)
                R("vector", lambda e: e.tensor_max(ymax_acc[:], ymax_acc[:], ycol[:]), Sve)
                if xvs_next is not None:
                    prep_tr_tile(g + 1, tl, xvs_next[tl])
            if xvs_next is not None:
                prep_q(g + 1)
            bstate["ctx_pe"] = Spe.n

        W("sync", Sve, Sve.n)
        c1 = R("sync", lambda e: e.dma_start(out=cc1_in.ap(), in_=ymax_acc[:]), Drt, 16)
        W("gpsimd", Drt, c1)
        cv = R("gpsimd", lambda e: e.collective_compute(
            "AllReduce", ALU.max, replica_groups=[list(range(NCORES))],
            ins=[cc1_in.ap()], outs=[cc1_out.ap()]), Scc, 1)
        W("sync", Scc, cv)
        rv = R("sync", lambda e: e.dma_start(
            out=yrow[:], in_=cc1_out.ap().rearrange("p a -> a p")), Drt, 16)
        W("vector", Drt, rv)
        R("vector", lambda e: e.tensor_reduce(
            out=sc1[:, 0:1], in_=yrow[:], axis=AX.X, op=ALU.max), Sve)
        sgv = R("scalar", lambda e: e.activation(
            out=sc1[:, 1:2], in_=clipx_sb[:], func=AF.Sigmoid), Sac)
        W("vector", Sac, sgv)
        R("vector", lambda e: e.tensor_tensor(
            out=sc1[:, 2:3], in0=sc1[:, 0:1], in1=sc1[:, 1:2], op=ALU.mult), Sve)
        R("vector", lambda e: e.tensor_scalar(
            out=sc1[:, 2:3], in0=sc1[:, 2:3], scalar1=R127, scalar2=None,
            op0=ALU.mult), Sve)
        sxv = R("vector", lambda e: e.reciprocal(out=sc1[:, 3:4], in_=sc1[:, 2:3]), Sve)
        W("tensor", Sve, sxv)
        mv = R("tensor", lambda e: e.matmul(
            ps_zb[:, 0:2], ones1f[:1, :], sc1[:1, 2:4], start=True, stop=True), Spe)
        W("vector", Spe, mv)
        R("vector", lambda e: e.tensor_copy(out=sx_col[:], in_=ps_zb[:, 0:1]), Sve)
        R("vector", lambda e: e.tensor_copy(out=invsx_col[:], in_=ps_zb[:, 1:2]), Sve)
        R("vector", lambda e: e.tensor_single_scalar(
            out=s1s[:], in_=rstd2s[:], scalar=invsx_col[:], op=ALU.mult), Sve)
        R("vector", lambda e: e.tensor_scalar(
            out=s2s[:], in0=mu2s[:], scalar1=-1.0, scalar2=None,
            op0=ALU.mult), Sve)
        R("vector", lambda e: e.tensor_mul(s2s[:], s2s[:], s1s[:]), Sve)
        R("vector", lambda e: e.tensor_single_scalar(
            out=sxg_cols[:], in_=scg_cols[:], scalar=sx_col[:], op=ALU.mult), Sve)
        R("vector", lambda e: e.tensor_single_scalar(
            out=sxu_cols[:], in_=scu_cols[:], scalar=sx_col[:], op=ALU.mult), Sve)

    barrier()

    if stage in ("Bq", "Bctx", "Bxln"):
        W("gpsimd", Ddbg, Ddbg.n)
        W("sync", Ddbg, Ddbg.n)
        barrier()
        return _finish(nc, prog, stk, ctx_sems, cleanup_cm)
    if stage == "B":
        _dbg_out(W, R, outb, x_res, Sve, Ddbg)
        barrier()
        return _finish(nc, prog, stk, ctx_sems, cleanup_cm)

    with ExitStack() as c_stk:
        def csb(name, shape, dt):
            return c_stk.enter_context(nc.sbuf_tensor(name, shape, dt))

        def cps(name, shape, dt=F32):
            return c_stk.enter_context(nc.psum_tensor(name, shape, dt))

        wgint = csb("wgint", [128, HC, II], BF16)
        wuint = csb("wuint", [128, HC, II], BF16)
        xqscr = csb("xqscr", [128, D], F32)
        xqi32 = csb("xqi32", [128, D], I32)
        xqtokA = csb("xqtokA", [128, D], BF16)
        xqtokB = csb("xqtokB", [128, D], BF16)
        xq_fm = csb("xq_fm", [128, HC, 512], BF16)
        silu_sb = csb("silu_sb", [128, 512], F32)
        h_sb0 = csb("h_sb0", [128, 512], BF16)
        h_sb1 = csb("h_sb1", [128, 512], BF16)
        h_sb2 = csb("h_sb2", [128, 512], BF16)
        hcol = csb("hcol", [128, 1], F32)

        ps_gA = cps("ps_gA", [128, 512])
        ps_gB = cps("ps_gB", [128, 512])
        ps_uA = cps("ps_uA", [128, 512])
        ps_uB = cps("ps_uB", [128, 512])
        ps_tr2 = cps("ps_tr2", [128, 4, 128])

        R("scalar", lambda e: e.dma_start(
            out=wgint[:], in_=wg_sp.ap().rearrange("(c p) i -> p c i", p=128)), DwC, 16)
        wu_l = R("scalar", lambda e: e.dma_start(
            out=wuint[:], in_=wu_sp.ap().rearrange("(c p) i -> p c i", p=128)), DwC, 16)

        if _DBG:
            for T in range(NT):
                R("sync", lambda e, T=T: e.dma_start(
                    out=dbg_xr[128 * T:128 * (T + 1), :], in_=x_res[:, T, :]), Drt, 16)
            R("sync", lambda e: e.dma_start(out=dbg_s.ap()[:, 0:NT], in_=s1s[:]), Drt, 16)
            R("sync", lambda e: e.dma_start(out=dbg_s.ap()[:, NT:2 * NT], in_=s2s[:]), Drt, 16)
        h_bufs = [h_sb0, h_sb1, h_sb2]
        h_sems = [Dh0, Dh1, Dh2]
        cstate = {"hdma": [0, 0, 0], "tr2_evac": 0, "xqtok_pe": [0, 0],
                  "xqfm_pe": 0, "g_evac": [0, 0], "u_evac": [0, 0],
                  "xqscr_rd": 0, "h_stt": 0}

        for g in range(NG):
            gsl = slice(512 * g, 512 * (g + 1))
            for tl in range(TPG):
                T = TPG * g + tl
                tloc = 128 * tl
                W("scalar", Sve, cstate["xqscr_rd"])
                uv = R("scalar", lambda e, T=T: e.activation(
                    out=xqscr[:], in_=x_res[:, T, :], func=AF.Identity,
                    bias=s2s[:, T:T + 1], scale=s1s[:, T:T + 1]), Sac)
                W("vector", Sac, uv)
                R("vector", lambda e: e.tensor_scalar(
                    out=xqscr[:], in0=xqscr[:], scalar1=127.0, scalar2=-128.0,
                    op0=ALU.min, op1=ALU.max), Sve, 1, False)
                cstate["xqscr_rd"] = R("vector", lambda e: e.tensor_scalar(
                    out=xqi32[:], in0=xqscr[:], scalar1=MAGIC, scalar2=None,
                    op0=ALU.add), Sve, 1, False)
                xt = xqtokA if T % 2 == 0 else xqtokB
                W("vector", Spe, cstate["xqtok_pe"][T % 2])
                qv = R("vector", lambda e, xt=xt: e.tensor_scalar(
                    out=xt[:], in0=xqi32[:], scalar1=-MAGIC, scalar2=None,
                    op0=ALU.add), Sve, 1, False)
                if _DBG:
                    W("sync", Sve, qv)
                    dmp = R("sync", lambda e, T=T, xt=xt: e.dma_start(
                        out=dbg_xt[128 * T:128 * (T + 1), :], in_=xt[:]), Drt, 16)
                    W("vector", Drt, dmp)
                W("tensor", Sve, qv)
                W("tensor", Spe, cstate["xqfm_pe"])
                last_mv = None
                for half in range(2):
                    W("tensor", Sve, cstate["tr2_evac"])
                    for cc in range(4):
                        c = 4 * half + cc
                        last_mv = R("tensor", lambda e, c=c, cc=cc, xt=xt: e.matmul(
                            ps_tr2[:, cc, :], xt[:, 128 * c:128 * (c + 1)],
                            ident[:], start=True, stop=True), Spe)
                    W("vector", Spe, last_mv)
                    cstate["tr2_evac"] = R("vector", lambda e, half=half, tloc=tloc: e.tensor_copy(
                        out=xq_fm[:, 4 * half:4 * half + 4, tloc:tloc + 128],
                        in_=ps_tr2[:]),Sve, 1, False)
                cstate["xqtok_pe"][T % 2] = last_mv
            if _DBG:
                W("sync", Sve, cstate["tr2_evac"])
                dmp = R("sync", lambda e, g=g: e.dma_start(
                    out=dbg_xq.ap().rearrange("(c p) t -> p c t", p=128)[:, :, 512 * g:512 * (g + 1)],
                    in_=xq_fm[:]), Drt, 16)
                W("vector", Drt, dmp)
            W("tensor", Sve, cstate["tr2_evac"])
            W("tensor", DwC, wu_l)
            for ic in range(IC):
                gbank = ps_gA if ic % 2 == 0 else ps_gB
                ubank = ps_uA if ic % 2 == 0 else ps_uB
                W("tensor", Sve, cstate["g_evac"][ic % 2])
                W("tensor", Sve, cstate["u_evac"][ic % 2])
                gv = uv2 = None
                for c in range(HC):
                    gv = R("tensor", lambda e, c=c, ic=ic, gbank=gbank: e.matmul(
                        gbank[:], wgint[:, c, 128 * ic:128 * (ic + 1)],
                        xq_fm[:, c, :], start=(c == 0), stop=(c == HC - 1)), Spe)
                for c in range(HC):
                    uv2 = R("tensor", lambda e, c=c, ic=ic, ubank=ubank: e.matmul(
                        ubank[:], wuint[:, c, 128 * ic:128 * (ic + 1)],
                        xq_fm[:, c, :], start=(c == 0), stop=(c == HC - 1)), Spe)
                W("scalar", Spe, gv)
                W("scalar", Sve, cstate["h_stt"])
                sv = R("scalar", lambda e, ic=ic, gbank=gbank: e.activation(
                    out=silu_sb[:], in_=gbank[:], func=AF.Silu,
                    scale=sxg_cols[:, ic:ic + 1]), Sac)
                bidx = (g * IC + ic) % 3
                hb = h_bufs[bidx]
                W("vector", h_sems[bidx], cstate["hdma"][bidx])
                W("vector", Sac, sv)
                W("vector", Spe, uv2)
                hv = R("vector", lambda e, ic=ic, ubank=ubank, hb=hb: e.scalar_tensor_tensor(
                    out=hb[:], in0=ubank[:], scalar=sxu_cols[:, ic:ic + 1],
                    in1=silu_sb[:], op0=ALU.mult, op1=ALU.mult), Sve, 1, False)
                cstate["g_evac"][ic % 2] = hv
                cstate["u_evac"][ic % 2] = hv
                cstate["h_stt"] = hv
                R("vector", lambda e, hb=hb: e.tensor_reduce(
                    out=hcol[:], in_=hb[:], axis=AX.X, op=ALU.max,
                    apply_absolute_value=True), Sve)
                R("vector", lambda e: e.tensor_max(hmax_acc[:], hmax_acc[:], hcol[:]), Sve)
                W("sync", Sve, hv)
                cstate["hdma"][bidx] = R("sync", lambda e, ic=ic, gsl=gsl, hb=hb: e.dma_start(
                    out=h_spill.ap()[128 * ic:128 * (ic + 1), gsl], in_=hb[:]),
                    h_sems[bidx], 16)
            cstate["xqfm_pe"] = Spe.n

        W("sync", Sve, Sve.n)
        c2 = R("sync", lambda e: e.dma_start(out=cc2_in.ap(), in_=hmax_acc[:]), Drt, 16)
        W("gpsimd", Drt, c2)
        cv2 = R("gpsimd", lambda e: e.collective_compute(
            "AllReduce", ALU.max, replica_groups=[list(range(NCORES))],
            ins=[cc2_in.ap()], outs=[cc2_out.ap()]), Scc, 1)
        W("sync", Scc, cv2)
        rv = R("sync", lambda e: e.dma_start(
            out=hrow[:], in_=cc2_out.ap().rearrange("p a -> a p")), Drt, 16)
        W("vector", Drt, rv)
        R("vector", lambda e: e.tensor_reduce(
            out=sc1[:, 0:1], in_=hrow[:], axis=AX.X, op=ALU.max), Sve)
        sgv = R("scalar", lambda e: e.activation(
            out=sc1[:, 1:2], in_=clipd_sb[:], func=AF.Sigmoid), Sac)
        W("vector", Sac, sgv)
        R("vector", lambda e: e.tensor_tensor(
            out=sc1[:, 2:3], in0=sc1[:, 0:1], in1=sc1[:, 1:2], op=ALU.mult), Sve)
        R("vector", lambda e: e.tensor_scalar(
            out=sc1[:, 2:3], in0=sc1[:, 2:3], scalar1=R127, scalar2=None,
            op0=ALU.mult), Sve)
        sdv = R("vector", lambda e: e.reciprocal(out=sc1[:, 3:4], in_=sc1[:, 2:3]), Sve)
        W("tensor", Sve, sdv)
        mv = R("tensor", lambda e: e.matmul(
            ps_gA[:, 0:2], ones1f[:1, :], sc1[:1, 2:4], start=True, stop=True), Spe)
        W("vector", Spe, mv)
        R("vector", lambda e: e.tensor_copy(out=sd_col[:], in_=ps_gA[:, 0:1]), Sve)
        R("vector", lambda e: e.tensor_copy(out=invsd_col[:], in_=ps_gA[:, 1:2]), Sve)

    barrier()

    if stage == "C":
        _dbg_out(W, R, outb, x_res, Sve, Ddbg)
        barrier()
        return _finish(nc, prog, stk, ctx_sems, cleanup_cm)

    with ExitStack() as d_stk:
        def dsb(name, shape, dt):
            return d_stk.enter_context(nc.sbuf_tensor(name, shape, dt))

        def dps(name, shape, dt=F32):
            return d_stk.enter_context(nc.psum_tensor(name, shape, dt))

        wdnint = dsb("wdnint", [128, IC, D], BF16)
        hre = dsb("hre", [128, IC, 512], BF16)
        hqA = dsb("hqA", [128, IC, 512], BF16)
        hqB = dsb("hqB", [128, IC, 512], BF16)
        hscr = dsb("hscr", [128, 512], F32)
        hi32 = dsb("hi32", [128, 512], I32)
        srd_row = dsb("srd_row", [1, D], F32)
        sdsrd_b = dsb("sdsrd_b", [128, D], F32)
        outA = dsb("outA", [128, D], F32)
        outB = dsb("outB", [128, D], F32)

        ps_dnA = dps("ps_dnA", [128, 1024])
        ps_dnB = dps("ps_dnB", [128, 1024])
        ps_m = dps("ps_m", [128, 512])

        wdn_l = R("scalar", lambda e: e.dma_start(
            out=wdnint[:], in_=wdn_sp.ap().rearrange("(c p) j -> p c j", p=128)), DwD, 16)

        wv = R("sync", lambda e: e.dma_start(
            out=v_rds.ap().rearrange("(c p) -> p c", p=128), in_=scd_cols[:],
            allow_slow_non_contiguous=True), Drt, 16)
        W("sync", Drt, wv)
        rv = R("sync", lambda e: e.dma_start(
            out=srd_row[:], in_=v_rds.ap().rearrange("(a i) -> a i", a=1)), Drt, 16)
        W("tensor", Drt, rv)
        for half in range(2):
            mv = R("tensor", lambda e, half=half: e.matmul(
                ps_m[:], ones1f[:1, :], srd_row[:1, 512 * half:512 * (half + 1)],
                start=True, stop=True), Spe)
            W("vector", Spe, mv)
            cv = R("vector", lambda e, half=half: e.tensor_copy(
                out=sdsrd_b[:, 512 * half:512 * (half + 1)], in_=ps_m[:]), Sve)
            W("tensor", Sve, cv)
        dcv = R("vector", lambda e: e.tensor_single_scalar(
            out=sdsrd_b[:], in_=sdsrd_b[:], scalar=sd_col[:], op=ALU.mult), Sve)

        dstate = {"outdma": [0, 0], "dn_evac": [0, 0],
                  "hq_pe": [0, 0], "quant_done": [0, 0], "hre_free": 0}

        def emit_hre_load(g):
            gsl = slice(512 * g, 512 * (g + 1))
            W("sync", Sve, dstate["hre_free"])
            hl = None
            for chalf in range(2):
                lo = 11 * chalf
                hi = 11 * (chalf + 1)
                hl = R("sync", lambda e, gsl=gsl, lo=lo, hi=hi: e.dma_start(
                    out=hre[:, lo:hi, :],
                    in_=h_spill.ap().rearrange("(c p) t -> p c t", p=128)[:, lo:hi, gsl]),
                    Dhre, 16)
            return hl

        def quant_ops(g, hl):
            hq = hqA if g % 2 == 0 else hqB
            ops = []
            for ic in range(IC):
                def chunk(ic=ic, hq=hq, g=g, hl=hl):
                    if ic == 0:
                        W("vector", Spe, dstate["hq_pe"][g % 2])
                        W("vector", Dhre, hl)
                    R("vector", lambda e, ic=ic: e.tensor_scalar(
                        out=hscr[:], in0=hre[:, ic, :], scalar1=invsd_col[:],
                        scalar2=127.0, op0=ALU.mult, op1=ALU.min), Sve, 1, False)
                    R("vector", lambda e: e.tensor_scalar(
                        out=hi32[:], in0=hscr[:], scalar1=-128.0, scalar2=MAGIC,
                        op0=ALU.max, op1=ALU.add), Sve, 1, False)
                    qv = R("vector", lambda e, ic=ic, hq=hq: e.tensor_scalar(
                        out=hq[:, ic, :], in0=hi32[:], scalar1=-MAGIC, scalar2=None,
                        op0=ALU.add), Sve, 1, False)
                    if ic == IC - 1:
                        dstate["quant_done"][g % 2] = qv
                        dstate["hre_free"] = qv
                ops.append(chunk)
            return ops

        for op in quant_ops(0, emit_hre_load(0)):
            op()

        per_tile = [6, 6, 5, 5]
        for g in range(NG):
            hq = hqA if g % 2 == 0 else hqB
            W("tensor", Sve, dstate["quant_done"][g % 2])
            if g == 0:
                W("tensor", DwD, wdn_l)
            pending = quant_ops(g + 1, emit_hre_load(g + 1)) if g + 1 < NG else []
            for tl in range(TPG):
                T = TPG * g + tl
                tloc = 128 * tl
                bank = ps_dnA if tl % 2 == 0 else ps_dnB
                W("tensor", Sve, dstate["dn_evac"][tl % 2])
                mv = None
                for jh in range(2):
                    for ic in range(IC):
                        mv = R("tensor", lambda e, ic=ic, jh=jh, tloc=tloc, bank=bank,
                               hq=hq:
                               e.matmul(bank[:, 512 * jh:512 * (jh + 1)],
                                        hq[:, ic, tloc:tloc + 128],
                                        wdnint[:, ic, 512 * jh:512 * (jh + 1)],
                                        start=(ic == 0), stop=(ic == IC - 1)), Spe)
                for _ in range(min(per_tile[tl], len(pending))):
                    pending.pop(0)()
                ob = outA if tl % 2 == 0 else outB
                W("vector", Spe, mv)
                W("vector", Sve, dcv)
                W("vector", (Dout0 if tl % 2 == 0 else Dout1), dstate["outdma"][tl % 2])
                R("vector", lambda e, bank=bank, ob=ob: e.scalar_tensor_tensor(
                    out=ob[:], in0=bank[:], scalar=0.0, in1=sdsrd_b[:],
                    op0=ALU.add, op1=ALU.mult), Sve, 1, False)
                ov = R("vector", lambda e, T=T, ob=ob: e.tensor_tensor(
                    out=ob[:], in0=ob[:], in1=x_res[:, T, :], op=ALU.add), Sve, 1, False)
                dstate["dn_evac"][tl % 2] = ov
                W("sync", Sve, ov)
                dstate["outdma"][tl % 2] = R("sync", lambda e, T=T, ob=ob: e.dma_start(
                    out=outb[128 * T:128 * (T + 1), :], in_=ob[:]),
                    (Dout0 if tl % 2 == 0 else Dout1), 16)
            for op in pending:
                op()
            dstate["hq_pe"][g % 2] = Spe.n

        W("sync", Dout0, Dout0.n)
        W("sync", Dout1, Dout1.n)
        W("gpsimd", Dout0, Dout0.n)
        W("gpsimd", Dout1, Dout1.n)
    barrier()

    return _finish(nc, prog, stk, ctx_sems, cleanup_cm)


def _dbg_out(W, R, outb, x_res, Sve, Ddbg):
    for T in range(NT):
        W("gpsimd", Sve, Sve.n)
        R("gpsimd", lambda e, T=T: e.dma_start(
            out=outb[128 * T:128 * (T + 1), :], in_=x_res[:, T, :]), Ddbg, 16)
    W("gpsimd", Ddbg, Ddbg.n)
    W("sync", Ddbg, Ddbg.n)


def _finish(nc, prog, stk, ctx_sems, cleanup_cm):
    with nc.Block() as block:
        @block.sync
        def _(e):
            for f in prog["sync"]:
                f(e)

        @block.scalar
        def _(e):
            for f in prog["scalar"]:
                f(e)

        @block.vector
        def _(e):
            for f in prog["vector"]:
                f(e)

        @block.tensor
        def _(e):
            for f in prog["tensor"]:
                f(e)

        @block.gpsimd
        def _(e):
            for f in prog["gpsimd"]:
                f(e)
    stk.close()
    cleanup_cm.__exit__(None, None, None)
    return nc


def _host_inputs(full):
    x = np.ascontiguousarray(np.asarray(full["L0_attn_input_tensor"], np.float32))
    com = {
        "wqT": np.ascontiguousarray(np.asarray(full["L0_attn_Wq"], np.float32).T),
        "wkT": np.ascontiguousarray(np.asarray(full["L0_attn_Wk"], np.float32).T),
        "wvT": np.ascontiguousarray(np.asarray(full["L0_attn_Wv"], np.float32).T),
        "wdT": np.ascontiguousarray(np.asarray(full["L0_attn_Wd"], np.float32).T),
        "bq_in": np.ascontiguousarray(np.asarray(full["L0_attn_bq"], np.float32)),
        "bk_in": np.asarray(full["L0_attn_bk"], np.float32).reshape(1, D),
        "bv_in": np.asarray(full["L0_attn_bv"], np.float32).reshape(1, D),
        "bd_in": np.asarray(full["L0_attn_bd"], np.float32).reshape(1, D),
        "gamma_in": np.asarray(full["L0_attn_gamma"], np.float32).reshape(1, D),
        "beta_in": np.asarray(full["L0_attn_beta"], np.float32).reshape(1, D),
        "clipx_in": np.asarray(full["L0_ffn_clip_x"], np.float32).reshape(1, 1),
        "clipd_in": np.asarray(full["L0_ffn_clip_down"], np.float32).reshape(1, 1),
        "wg_n": np.ascontiguousarray(np.asarray(full["L0_ffn_w_gate"], np.float32)),
        "wgT": np.ascontiguousarray(np.asarray(full["L0_ffn_w_gate"], np.float32).T),
        "wu_n": np.ascontiguousarray(np.asarray(full["L0_ffn_w_up"], np.float32)),
        "wuT": np.ascontiguousarray(np.asarray(full["L0_ffn_w_up"], np.float32).T),
        "wdn_n": np.ascontiguousarray(np.asarray(full["L0_ffn_w_down"], np.float32)),
        "wdnT": np.ascontiguousarray(np.asarray(full["L0_ffn_w_down"], np.float32).T),
        "ident_in": np.eye(128, dtype=BF),
        "ones1b_in": np.ones((1, 128), BF),
        "ones1f_in": np.ones((1, 128), np.float32),
        "onescol_in": np.ones((128, 1), BF),
        "bc_in": _bc_const(),
    }
    xr = x.reshape(B, S, D)
    in_maps = []
    for core in range(NCORES):
        m = dict(com)
        m["x4"] = np.ascontiguousarray(xr[2 * core:2 * core + 2].reshape(TOK, D))
        in_maps.append(m)
    return in_maps


def _bc_const():
    bc = np.zeros((16, HC * 128), BF)
    for c in range(HC):
        bc[2 * c, 128 * c:128 * c + 64] = 1
        bc[2 * c + 1, 128 * c + 64:128 * c + 128] = 1
    return bc


def kernel(**inputs):
    in_maps = _host_inputs(inputs)
    nc = build_nc("full")
    res = run_bass_kernel_spmd(nc, in_maps, core_ids=list(range(NCORES)))
    parts = [res.results[c]["outb"].reshape(2, S, D) for c in range(NCORES)]
    return np.concatenate(parts, axis=0).astype(np.float32)


# version 11 (final)
# speedup vs baseline: 1.0394x; 1.0046x over previous
import numpy as np
import ml_dtypes
from contextlib import ExitStack
import concourse.bass as bass
import concourse.mybir as mybir
from concourse.bass_utils import run_bass_kernel_spmd

F32 = mybir.dt.float32
BF16 = mybir.dt.bfloat16
I32 = mybir.dt.int32
AF = mybir.ActivationFunctionType
ALU = mybir.AluOpType
AX = mybir.AxisListType
BF = ml_dtypes.bfloat16

B, S, D = 16, 2048, 1024
NCORES = 8
TOK = (B // NCORES) * S
NT = TOK // 128
NTS = NT // 2
TPG = 4
NG = NT // TPG
GPS = NG // 2
HC = D // 128
II = 2816
IC = II // 128
MAGIC = float(np.float32(1.5 * 2 ** 23))
R127 = float(np.float32(1.0) / np.float32(127.0))
_DBG = False
EPS_LN = 1e-5
EPS_ATTN = 1e-6
EPS_ILN = 1e-12


class Ctr:
    __slots__ = ("h", "n")

    def __init__(self, h):
        self.h = h
        self.n = 0

    def bump(self, k=1):
        self.n += k
        return self.n


def build_nc(stage="full"):
    nc = bass.Bass("TRN2", target_bir_lowering=False,
                   detect_race_conditions=False)
    cleanup_cm = nc.cleanup_on_exit()
    cleanup_cm.__enter__()

    def din(name, shape, dt=F32):
        return nc.dram_tensor(name, shape, dt, kind="ExternalInput").ap()

    x4 = din("x4", [TOK, D])
    wqT = din("wqT", [D, D]); wkT = din("wkT", [D, D])
    wvT = din("wvT", [D, D]); wdT = din("wdT", [D, D])
    bq_in = din("bq_in", [D]); bk_in = din("bk_in", [1, D])
    bv_in = din("bv_in", [1, D]); bd_in = din("bd_in", [1, D])
    gamma_in = din("gamma_in", [1, D]); beta_in = din("beta_in", [1, D])
    clipx_in = din("clipx_in", [1, 1]); clipd_in = din("clipd_in", [1, 1])
    wg_n = din("wg_n", [II, D]); wgT = din("wgT", [D, II])
    wu_n = din("wu_n", [II, D]); wuT = din("wuT", [D, II])
    wdn_n = din("wdn_n", [D, II]); wdnT = din("wdnT", [II, D])
    ident_in = din("ident_in", [128, 128], BF16)
    ones1b_in = din("ones1b_in", [1, 128], BF16)
    ones1f_in = din("ones1f_in", [1, 128])
    onescol_in = din("onescol_in", [128, 1], BF16)
    bc_in = din("bc_in", [16, HC * 128], BF16)

    outb = nc.dram_tensor("outb", [TOK, D], F32, kind="ExternalOutput").ap()

    h_spill = nc.dram_tensor("h_spill", [II, TOK], BF16)
    wg_sp = nc.dram_tensor("wg_sp", [D, II], BF16)
    wu_sp = nc.dram_tensor("wu_sp", [D, II], BF16)
    wdn_sp = nc.dram_tensor("wdn_sp", [II, D], BF16)
    v_rg = nc.dram_tensor("v_rg", [II], F32)
    v_ru = nc.dram_tensor("v_ru", [II], F32)
    v_rdn = nc.dram_tensor("v_rdn", [II], F32)
    v_rds = nc.dram_tensor("v_rds", [D], F32)
    cc1_in = nc.dram_tensor("cc1_in", [128, 1], F32)
    cc1_out = nc.dram_tensor("cc1_out", [128, 1], F32)
    cc2_in = nc.dram_tensor("cc2_in", [128, 1], F32)
    cc2_out = nc.dram_tensor("cc2_out", [128, 1], F32)
    dbg_xq = nc.dram_tensor("dbg_xq", [D, TOK], BF16)
    dbg_xr = nc.dram_tensor("dbg_xr", [TOK, D], BF16)
    dbg_xt = nc.dram_tensor("dbg_xt", [TOK, D], BF16)
    dbg_s = nc.dram_tensor("dbg_s", [128, 2 * NT], F32)

    prog = {k: [] for k in ("sync", "scalar", "vector", "tensor", "gpsimd")}
    watermark = {k: {} for k in prog}
    ENGS = ("sync", "scalar", "vector", "tensor", "gpsimd")

    def em(eng, fn):
        prog[eng].append(fn)

    def W(eng, ctr, val):
        if val is None or val <= 0:
            return
        wm = watermark[eng]
        if wm.get(id(ctr), -1) >= val:
            return
        wm[id(ctr)] = val
        em(eng, lambda e, h=ctr.h, v=val: e.wait_ge(h, v))

    def R(eng, maker, inc=None, inck=1, ss=True):
        if inc is None:
            em(eng, maker)
            return None
        v = inc.bump(inck)
        em(eng, lambda e, m=maker, h=inc.h, k=inck: m(e).then_inc(h, k))
        if ss and eng == "vector" and inck == 1:
            W(eng, inc, v)
        return v

    ctx_sems = []

    def mk_sem(name):
        cm = nc.semaphore(name)
        h = cm.__enter__()
        ctx_sems.append(cm)
        return Ctr(h)

    Spe = mk_sem("Spe")
    Sve = mk_sem("Sve")
    Sac = mk_sem("Sac")
    Scc = mk_sem("Scc")
    Dpers = mk_sem("Dpers")
    Drt = mk_sem("Drt")
    Dx = mk_sem("Dx")
    DwA = mk_sem("DwA")
    DwB = mk_sem("DwB")
    DwQ = mk_sem("DwQ")
    Dspill = mk_sem("Dspill")
    DawA = mk_sem("DawA")
    DawB = mk_sem("DawB")
    DA_bk = mk_sem("DA_bk"); DA_bv = mk_sem("DA_bv")
    DA_wk = mk_sem("DA_wk"); DA_wv = mk_sem("DA_wv")
    DB_wd = mk_sem("DB_wd"); DB_wq = mk_sem("DB_wq"); DB_bd = mk_sem("DB_bd")
    Dh0 = mk_sem("Dh0"); Dh1 = mk_sem("Dh1"); Dh2 = mk_sem("Dh2")
    DwC = mk_sem("DwC")
    DwD = mk_sem("DwD")
    Dhre = mk_sem("Dhre")
    Dout0 = mk_sem("Dout0"); Dout1 = mk_sem("Dout1")
    Ddbg = mk_sem("Ddbg")
    ALL_SEMS = [Spe, Sve, Sac, Dpers, Drt, Dx, DwA, DwB, DwQ, Dspill,
                DawA, DawB, DA_bk, DA_bv, DA_wk, DA_wv, DB_wd, DB_wq, DB_bd,
                Dh0, Dh1, Dh2, DwC, DwD, Dhre, Dout0, Dout1]

    def barrier():
        for eng in ENGS:
            for sm in ALL_SEMS:
                W(eng, sm, sm.n)

    stk = ExitStack()

    def sb(name, shape, dt):
        return stk.enter_context(nc.sbuf_tensor(name, shape, dt))

    x_res = sb("x_res", [128, NT, D], BF16)
    ident = sb("ident", [128, 128], BF16)
    ones1b = sb("ones1b", [1, 128], BF16)
    ones1f = sb("ones1f", [1, 128], F32)
    onescol = sb("onescol", [128, 1], BF16)
    bc_sb = sb("bc_sb", [16, HC * 128], BF16)
    bq_cols = sb("bq_cols", [128, HC], F32)
    clipx_sb = sb("clipx_sb", [1, 1], F32)
    clipd_sb = sb("clipd_sb", [1, 1], F32)
    mu1s = sb("mu1s", [128, NT], F32)
    rstd1s = sb("rstd1s", [128, NT], F32)
    nmu1s = sb("nmu1s", [128, NT], F32)
    mu2s = sb("mu2s", [128, NT], F32)
    rstd2s = sb("rstd2s", [128, NT], F32)
    s1s = sb("s1s", [128, NT], F32)
    s2s = sb("s2s", [128, NT], F32)
    kv_sb = sb("kv_sb", [128, 2, 1024], BF16)
    ksum_acc = sb("ksum_acc", [128, 16], F32)
    kcols = sb("kcols", [128, 2, HC, 16], BF16)
    scg_cols = sb("scg_cols", [128, IC], F32)
    scu_cols = sb("scu_cols", [128, IC], F32)
    scd_cols = sb("scd_cols", [128, HC], F32)
    sxg_cols = sb("sxg_cols", [128, IC], F32)
    sxu_cols = sb("sxu_cols", [128, IC], F32)
    ymax_acc = sb("ymax_acc", [128, 1], F32)
    hmax_acc = sb("hmax_acc", [128, 1], F32)
    dinv_col = sb("dinv_col", [128, 1], F32)
    eps1_col = sb("eps1_col", [128, 1], F32)
    epsi_col = sb("epsi_col", [128, 1], F32)
    yrow = sb("yrow", [1, 128], F32)
    hrow = sb("hrow", [1, 128], F32)
    sc1 = sb("sc1", [1, 4], F32)
    sx_col = sb("sx_col", [128, 1], F32)
    invsx_col = sb("invsx_col", [128, 1], F32)
    sd_col = sb("sd_col", [128, 1], F32)
    invsd_col = sb("invsd_col", [128, 1], F32)

    R("sync", lambda e: e.dma_start(out=ident[:], in_=ident_in), Dpers, 16)
    R("sync", lambda e: e.dma_start(out=ones1b[:], in_=ones1b_in), Dpers, 16)
    R("sync", lambda e: e.dma_start(out=ones1f[:], in_=ones1f_in), Dpers, 16)
    R("sync", lambda e: e.dma_start(out=onescol[:], in_=onescol_in), Dpers, 16)
    R("sync", lambda e: e.dma_start(out=bc_sb[:], in_=bc_in), Dpers, 16)
    R("sync", lambda e: e.dma_start(
        out=bq_cols[:], in_=bq_in.rearrange("(c p) -> p c", p=128),
        allow_slow_non_contiguous=True), Dpers, 16)
    R("sync", lambda e: e.dma_start(out=clipx_sb[:], in_=clipx_in), Dpers, 16)
    pers_x = R("sync", lambda e: e.dma_start(out=clipd_sb[:], in_=clipd_in), Dpers, 16)
    for eng in ("tensor", "vector", "scalar"):
        W(eng, Dpers, pers_x)

    R("vector", lambda e: e.memset(ksum_acc[:], 0.0), Sve)
    R("vector", lambda e: e.memset(dinv_col[:], 1.0 / D), Sve)
    R("vector", lambda e: e.memset(eps1_col[:], EPS_LN), Sve)
    R("vector", lambda e: e.memset(epsi_col[:], EPS_ILN), Sve)
    R("vector", lambda e: e.memset(kcols[:], 0.0), Sve)
    R("vector", lambda e: e.memset(kv_sb[:], 0.0), Sve)
    R("vector", lambda e: e.memset(ymax_acc[:], 0.0), Sve)
    R("vector", lambda e: e.memset(hmax_acc[:], 0.0), Sve)

    with ExitStack() as a_stk:
        def asb(name, shape, dt):
            return a_stk.enter_context(nc.sbuf_tensor(name, shape, dt))

        def aps(name, shape, dt=F32):
            return a_stk.enter_context(nc.psum_tensor(name, shape, dt))

        bk_row = asb("bk_row", [1, D], BF16)
        bv_row = asb("bv_row", [1, D], BF16)
        wk_sb = asb("wk_sb", [128, HC, D], BF16)
        wv_sb = asb("wv_sb", [128, HC, D], BF16)
        xf32A = asb("xf32A", [128, D], F32)
        xf32B = asb("xf32B", [128, D], F32)
        xlntokA = asb("xlntokA", [128, D], BF16)
        xlntokB = asb("xlntokB", [128, D], BF16)
        xfmA = asb("xfmA", [128, HC, 128], BF16)
        xfmB = asb("xfmB", [128, HC, 128], BF16)
        k_act = asb("k_act", [128, D], BF16)
        v_act = asb("v_act", [128, D], BF16)
        asq = asb("asq", [128, D], F32)
        vcol = asb("vcol", [128, 1], F32)
        scol = asb("scol", [128, 1], F32)
        qcol = asb("qcol", [128, 1], F32)
        scrE = asb("scrE", [128, 512], F32)
        scrR = asb("scrR", [128, 512], F32)
        ksum_t = asb("ksum_t", [128, 8], F32)
        kv_acc = asb("kv_acc", [128, 512], F32)

        ps_k = aps("ps_k", [128, 2, 512])
        ps_v = aps("ps_v", [128, 2, 512])
        ps_kv = aps("ps_kv", [128, 512])
        ps_ks = aps("ps_ks", [128, 512])
        ps_tr = aps("ps_tr", [128, 4, 128])

        wstA = asb("wstA", [128, II], F32)
        wstB = asb("wstB", [128, II], F32)
        r_b = asb("r_b", [128, II], F32)
        wi32 = asb("wi32", [128, II], I32)
        wint = asb("wint", [128, II], BF16)
        rrow = asb("rrow", [1, II], F32)
        rcols = asb("rcols", [128, IC], F32)
        ps_bc = aps("ps_bc", [128, 512])

        setup_units = []
        pstate = {"buf_rd": [0, 0], "wint_sp": 0, "bc_evac": 0}
        plvs = {}

        def prep_weight_units(nat_ap, sc_cols, n_chunks, nat_w,
                              T_ap, nT_chunks, T_width, vec_hbm, spill_ap, wid):
            def ld(c):
                buf = wstA if c % 2 == 0 else wstB
                dsem = DwA if c % 2 == 0 else DwB
                def f(c=c, buf=buf, dsem=dsem):
                    W("scalar", Sve, pstate["buf_rd"][c % 2])
                    plvs[(wid, "r", c)] = R(
                        "scalar", lambda e, c=c, buf=buf: e.dma_start(
                            out=buf[:, 0:nat_w],
                            in_=nat_ap[c * 128:(c + 1) * 128, :]), dsem, 16)
                return f

            def red(c):
                buf = wstA if c % 2 == 0 else wstB
                dsem = DwA if c % 2 == 0 else DwB
                def f(c=c, buf=buf, dsem=dsem):
                    W("vector", dsem, plvs[(wid, "r", c)])
                    pstate["buf_rd"][c % 2] = R(
                        "vector", lambda e, c=c, buf=buf: e.tensor_reduce(
                            out=sc_cols[:, c:c + 1], in_=buf[:, 0:nat_w],
                            axis=AX.X, op=ALU.max,
                            apply_absolute_value=True), Sve)
                return f

            def glue1():
                def f():
                    R("vector", lambda e: e.tensor_scalar(
                        out=sc_cols[:, 0:n_chunks], in0=sc_cols[:, 0:n_chunks],
                        scalar1=R127, scalar2=None, op0=ALU.mult), Sve)
                    rv = R("vector", lambda e: e.reciprocal(
                        out=rcols[:, 0:n_chunks], in_=sc_cols[:, 0:n_chunks]), Sve)
                    W("sync", Sve, rv)
                    wv = R("sync", lambda e: e.dma_start(
                        out=vec_hbm.ap().rearrange("(c p) -> p c", p=128)[:, 0:n_chunks],
                        in_=rcols[:, 0:n_chunks],
                        allow_slow_non_contiguous=True), Drt, 16)
                    W("sync", Drt, wv)
                    plvs[(wid, "rt")] = R("sync", lambda e: e.dma_start(
                        out=rrow[:, 0:T_width],
                        in_=vec_hbm.ap().rearrange("(a i) -> a i", a=1)[:, 0:T_width]),
                        Drt, 16)
                return f

            def glue2():
                def f():
                    W("tensor", Drt, plvs[(wid, "rt")])
                    nb = (T_width + 511) // 512
                    for b_ in range(nb):
                        lo, hi = b_ * 512, min(T_width, b_ * 512 + 512)
                        W("tensor", Sve, pstate["bc_evac"])
                        mv = R("tensor", lambda e, lo=lo, hi=hi: e.matmul(
                            ps_bc[:, 0:hi - lo], ones1f[:1, :], rrow[:1, lo:hi],
                            start=True, stop=True), Spe)
                        W("vector", Spe, mv)
                        pstate["bc_evac"] = R(
                            "vector", lambda e, lo=lo, hi=hi: e.tensor_copy(
                                out=r_b[:, lo:hi], in_=ps_bc[:, 0:hi - lo]), Sve)
                return f

            def qld(c):
                buf = wstA if c % 2 == 0 else wstB
                dsem = DwA if c % 2 == 0 else DwB
                def f(c=c, buf=buf, dsem=dsem):
                    W("scalar", Sve, pstate["buf_rd"][c % 2])
                    plvs[(wid, "q", c)] = R(
                        "scalar", lambda e, c=c, buf=buf: e.dma_start(
                            out=buf[:, 0:T_width],
                            in_=T_ap[c * 128:(c + 1) * 128, :]), dsem, 16)
                return f

            def quant(c):
                buf = wstA if c % 2 == 0 else wstB
                dsem = DwA if c % 2 == 0 else DwB
                def f(c=c, buf=buf, dsem=dsem):
                    W("vector", dsem, plvs[(wid, "q", c)])
                    W("vector", Spe, pstate.get("bc_pe", 0))
                    R("vector", lambda e, buf=buf: e.tensor_mul(
                        buf[:, 0:T_width], buf[:, 0:T_width],
                        r_b[:, 0:T_width]), Sve, 1, False)
                    pstate["buf_rd"][c % 2] = R(
                        "vector", lambda e, buf=buf: e.tensor_scalar(
                            out=wi32[:, 0:T_width], in0=buf[:, 0:T_width],
                            scalar1=MAGIC, scalar2=None, op0=ALU.add), Sve, 1, False)
                    W("vector", Dspill, pstate["wint_sp"])
                    qv = R("vector", lambda e: e.tensor_scalar(
                        out=wint[:, 0:T_width], in0=wi32[:, 0:T_width],
                        scalar1=-MAGIC, scalar2=None, op0=ALU.add), Sve, 1, False)
                    W("scalar", Sve, qv)
                    pstate["wint_sp"] = R(
                        "scalar", lambda e, c=c: e.dma_start(
                            out=spill_ap.ap()[c * 128:(c + 1) * 128, :],
                            in_=wint[:, 0:T_width]), Dspill, 16)
                return f

            def combine(*fs):
                def f():
                    for g_ in fs:
                        g_()
                return f

            u = [combine(ld(0), ld(1))]
            for c in range(n_chunks):
                fs = [red(c)]
                if c + 2 < n_chunks:
                    fs.append(ld(c + 2))
                u.append(combine(*fs))
            u.append(glue1())
            u.append(glue2())
            u.append(combine(qld(0), qld(1)))
            for c in range(nT_chunks):
                fs = [quant(c)]
                if c + 2 < nT_chunks:
                    fs.append(qld(c + 2))
                u.append(combine(*fs))
            return u

        if not (stage.endswith('_np') or stage.endswith('_min')):
            setup_units += prep_weight_units(
                wg_n, scg_cols, IC, D, wgT, HC, II, v_rg, wg_sp, "g")
            setup_units += prep_weight_units(
                wu_n, scu_cols, IC, D, wuT, HC, II, v_ru, wu_sp, "u")
            setup_units += prep_weight_units(
                wdn_n, scd_cols, HC, II, wdnT, IC, D, v_rdn, wdn_sp, "d")

        R("vector", lambda e: e.memset(kv_acc[:], 0.0), Sve)
        R("gpsimd", lambda e: e.dma_start(out=bk_row[:], in_=bk_in), DA_bk, 16)
        R("gpsimd", lambda e: e.dma_start(out=bv_row[:], in_=bv_in), DA_bv, 16)
        R("gpsimd", lambda e: e.dma_start(
            out=wk_sb[:], in_=wkT.rearrange("(c p) j -> p c j", p=128)), DA_wk, 16)
        R("gpsimd", lambda e: e.dma_start(
            out=wv_sb[:], in_=wvT.rearrange("(c p) j -> p c j", p=128)), DA_wv, 16)

        state = {
            "xf32_ve": [0, 0], "xf32_ac": [0, 0], "asq_rd": 0, "vcol_rd": 0,
            "xlntok_pe": [0, 0],
            "tr_evac": 0,
            "xfm_pe": [0, 0],
            "kv_mm": 0,
            "psk_rd": 0, "scr_rd": 0,
            "ksum_evac": 0, "kv_evac": 0, "psv_ac": 0,
        }

        def emit_ln1_tile(T):
            xf32 = xf32A if T % 2 == 0 else xf32B
            W("sync", Sve, state["xf32_ve"][T % 2])
            W("sync", Sac, state["xf32_ac"][T % 2])
            lv = R("sync", lambda e, T=T, xf32=xf32: e.dma_start(
                out=xf32[:], in_=x4[128 * T:128 * (T + 1), :]), Dx, 16)
            W("scalar", Dx, lv)
            W("scalar", Sve, state["asq_rd"])
            sqv = R("scalar", lambda e, xf32=xf32: e.activation(
                out=asq[:], in_=xf32[:], func=AF.Square), Sac)
            W("vector", Dx, lv)
            scv = R("vector", lambda e, xf32=xf32: e.tensor_reduce(
                out=scol[:], in_=xf32[:], axis=AX.X, op=ALU.add), Sve)
            W("vector", Sac, sqv)
            qv_ = R("vector", lambda e: e.tensor_reduce(
                out=qcol[:], in_=asq[:], axis=AX.X, op=ALU.add), Sve)
            state["asq_rd"] = qv_
            W("scalar", Sve, scv)
            W("scalar", Sve, state["vcol_rd"])
            R("scalar", lambda e, T=T: e.activation(
                out=mu1s[:, T:T + 1], in_=scol[:], func=AF.Identity,
                scale=dinv_col[:]), Sac)
            msv = R("scalar", lambda e: e.activation(
                out=vcol[:], in_=scol[:], func=AF.Square, scale=dinv_col[:]), Sac)
            W("vector", Sac, msv)
            av = R("vector", lambda e: e.tensor_scalar(
                out=qcol[:], in0=qcol[:], scalar1=1.0 / D, scalar2=vcol[:],
                op0=ALU.mult, op1=ALU.subtract), Sve)
            state["vcol_rd"] = av
            W("scalar", Sve, av)
            sv = R("scalar", lambda e: e.activation(
                out=scol[:], in_=qcol[:], func=AF.Sqrt, bias=eps1_col[:]), Sac)
            W("vector", Sac, sv)
            R("vector", lambda e, T=T: e.reciprocal(out=rstd1s[:, T:T + 1], in_=scol[:]), Sve)
            nv = R("vector", lambda e, T=T: e.tensor_scalar(
                out=nmu1s[:, T:T + 1], in0=rstd1s[:, T:T + 1],
                scalar1=mu1s[:, T:T + 1], scalar2=-1.0,
                op0=ALU.mult, op1=ALU.mult), Sve)
            cv = R("vector", lambda e, T=T, xf32=xf32: e.tensor_copy(
                out=x_res[:, T, :], in_=xf32[:]), Sve, 1, False)
            xt = xlntokA if T % 2 == 0 else xlntokB
            W("scalar", Spe, state["xlntok_pe"][T % 2])
            W("scalar", Sve, nv)
            xv = R("scalar", lambda e, xt=xt, T=T, xf32=xf32: e.activation(
                out=xt[:], in_=xf32[:], func=AF.Identity,
                bias=nmu1s[:, T:T + 1], scale=rstd1s[:, T:T + 1]), Sac)
            state["xf32_ve"][T % 2] = cv
            state["xf32_ac"][T % 2] = xv
            return xt, xv

        def emit_transposes(T, xt, xv):
            xfm = xfmA if T % 2 == 0 else xfmB
            W("tensor", Sac, xv)
            W("tensor", Spe, state["xfm_pe"][T % 2])
            last_mv = None
            for half in range(2):
                W("tensor", Sve, state["tr_evac"])
                for cc in range(4):
                    c = 4 * half + cc
                    last_mv = R("tensor", lambda e, c=c, cc=cc, xt=xt: e.matmul(
                        ps_tr[:, cc, :], xt[:, 128 * c:128 * (c + 1)], ident[:],
                        start=True, stop=True), Spe)
                W("vector", Spe, last_mv)
                state["tr_evac"] = R("vector", lambda e, half=half, xfm=xfm: e.tensor_copy(
                    out=xfm[:, 4 * half:4 * half + 4, :], in_=ps_tr[:]), Sve, 1, False)
            state["xlntok_pe"][T % 2] = last_mv
            return xfm

        def emit_kv_tile(T, seq, tis, xfm):
            W("tensor", Sve, state["psk_rd"])
            W("tensor", Sac, state["psv_ac"])
            W("tensor", Sve, state["tr_evac"])
            W("tensor", DA_bk, 16)
            W("tensor", DA_bv, 16)
            W("tensor", DA_wk, 16)
            W("tensor", DA_wv, 16)
            kv_last = None
            for tgt, wsb, brow in ((ps_k, wk_sb, bk_row), (ps_v, wv_sb, bv_row)):
                for fh in range(2):
                    for c in range(HC):
                        R("tensor", lambda e, tgt=tgt, wsb=wsb, c=c, fh=fh, xfm=xfm:
                          e.matmul(tgt[:, fh, :], xfm[:, c, :],
                                   wsb[:, c, 512 * fh:512 * (fh + 1)],
                                   start=(c == 0), stop=False), Spe)
                    kv_last = R("tensor", lambda e, tgt=tgt, brow=brow, fh=fh: e.matmul(
                        tgt[:, fh, :], ones1b[:1, :],
                        brow[:1, 512 * fh:512 * (fh + 1)],
                        start=False, stop=True), Spe)
            state["xfm_pe"][T % 2] = kv_last
            W("scalar", Spe, kv_last)
            W("vector", Spe, state["kv_mm"])
            for fh in range(2):
                W("scalar", Sve, state["psk_rd"])
                e1 = R("scalar", lambda e, fh=fh: e.activation(
                    out=scrE[:], in_=ps_k[:, fh, :], func=AF.Exp), Sac)
                W("vector", Sac, e1)
                R("vector", lambda e, fh=fh: e.tensor_scalar(
                    out=scrR[:], in0=ps_k[:, fh, :], scalar1=0.0,
                    scalar2=None, op0=ALU.max), Sve, 1, False)
                ev2 = R("vector", lambda e, fh=fh: e.scalar_tensor_tensor(
                    out=k_act[:, 512 * fh:512 * (fh + 1)], in0=scrE[:],
                    scalar=1.0, in1=scrR[:], op0=ALU.min, op1=ALU.add), Sve, 1, False)
                state["psk_rd"] = ev2
            W("scalar", Spe, state["kv_mm"])
            R("scalar", lambda e: e.activation(
                out=v_act[:, 0:512], in_=ps_v[:, 0, :], func=AF.Identity), Sac)
            vv_ = R("scalar", lambda e: e.activation(
                out=v_act[:, 512:1024], in_=ps_v[:, 1, :], func=AF.Identity), Sac)
            state["psv_ac"] = vv_
            W("tensor", Sve, state["psk_rd"])
            W("tensor", Sac, vv_)
            W("tensor", Sve, state["kv_evac"])
            kvm = None
            for hd in range(16):
                hh = hd % 2
                p = hd // 2
                kvm = R("tensor", lambda e, hd=hd, hh=hh, p=p: e.matmul(
                    ps_kv[64 * hh:64 * (hh + 1), 64 * p:64 * (p + 1)],
                    k_act[:, 64 * hd:64 * (hd + 1)],
                    v_act[:, 64 * hd:64 * (hd + 1)],
                    start=True, stop=True), Spe)
            W("vector", Spe, kvm)
            state["kv_evac"] = R("vector", lambda e: e.tensor_tensor(
                out=kv_acc[:], in0=kv_acc[:], in1=ps_kv[:], op=ALU.add), Sve, 1, False)
            W("tensor", Sve, state["ksum_evac"])
            km = None
            for hd in range(16):
                hh = hd % 2
                p = hd // 2
                km = R("tensor", lambda e, hd=hd, hh=hh, p=p: e.matmul(
                    ps_ks[64 * hh:64 * (hh + 1), p:p + 1],
                    k_act[:, 64 * hd:64 * (hd + 1)], onescol[:],
                    start=True, stop=True), Spe)
            state["kv_mm"] = km
            W("vector", Spe, km)
            R("vector", lambda e: e.tensor_copy(out=ksum_t[:], in_=ps_ks[:, 0:8]), Sve)
            state["ksum_evac"] = R("vector", lambda e, seq=seq: e.tensor_tensor(
                out=ksum_acc[:, 8 * seq:8 * seq + 8],
                in0=ksum_acc[:, 8 * seq:8 * seq + 8],
                in1=ksum_t[:], op=ALU.add), Sve)

        minimal = stage.endswith("_min")
        n_su = len(setup_units)
        su_popped = 0
        ln1_out = [emit_ln1_tile(0), emit_ln1_tile(1)]
        for T in range(NT):
            seq = T // NTS
            xt, xv = ln1_out[T % 2]
            if not minimal:
                xfm = emit_transposes(T, xt, xv)
            if T + 2 < NT:
                ln1_out[T % 2] = emit_ln1_tile(T + 2)
            if minimal:
                continue
            emit_kv_tile(T, seq, T % NTS, xfm)
            want = (T + 1) * n_su // NT
            while su_popped < want:
                setup_units[su_popped]()
                su_popped += 1

            if T % NTS == NTS - 1 and not minimal:
                for p in range(8):
                    R("vector", lambda e, seq=seq, p=p: e.tensor_copy(
                        out=kv_sb[0:64, seq, 128 * p:128 * p + 64],
                        in_=kv_acc[0:64, 64 * p:64 * (p + 1)]), Sve)
                    R("vector", lambda e, seq=seq, p=p: e.tensor_copy(
                        out=kv_sb[64:128, seq, 128 * p + 64:128 * p + 128],
                        in_=kv_acc[64:128, 64 * p:64 * (p + 1)]), Sve)
                if seq == 0:
                    R("vector", lambda e: e.memset(kv_acc[:], 0.0), Sve)
                R("vector", lambda e, seq=seq: e.tensor_scalar(
                    out=ksum_acc[:, 8 * seq:8 * seq + 8],
                    in0=ksum_acc[:, 8 * seq:8 * seq + 8],
                    scalar1=EPS_ATTN, scalar2=None, op0=ALU.add), Sve)
                kcv = None
                for c in range(HC):
                    R("vector", lambda e, seq=seq, c=c: e.tensor_scalar(
                        out=kcols[0:64, seq, c, 2 * c:2 * c + 1],
                        in0=onescol[0:64, :],
                        scalar1=ksum_acc[0:64, 8 * seq + c:8 * seq + c + 1],
                        scalar2=None, op0=ALU.mult), Sve)
                    kcv = R("vector", lambda e, seq=seq, c=c: e.tensor_scalar(
                        out=kcols[64:128, seq, c, 2 * c + 1:2 * c + 2],
                        in0=onescol[64:128, :],
                        scalar1=ksum_acc[64:128, 8 * seq + c:8 * seq + c + 1],
                        scalar2=None, op0=ALU.mult), Sve)
                W("tensor", Sve, kcv)

    barrier()

    if stage == "A":
        _dbg_out(W, R, outb, x_res, Sve, Ddbg)
        barrier()
        return _finish(nc, prog, stk, ctx_sems, cleanup_cm)

    if stage.startswith("Astats"):
        W("gpsimd", Sve, Sve.n)
        R("gpsimd", lambda e: e.dma_start(
            out=outb.rearrange("a b -> (a b)")[0:128 * 32]
            .rearrange("(p f) -> p f", p=128), in_=mu1s[:]), Ddbg, 16)
        R("gpsimd", lambda e: e.dma_start(
            out=outb.rearrange("a b -> (a b)")[128 * 32:128 * 64]
            .rearrange("(p f) -> p f", p=128), in_=rstd1s[:]), Ddbg, 16)
        W("gpsimd", Ddbg, Ddbg.n)
        W("sync", Ddbg, Ddbg.n)
        barrier()
        return _finish(nc, prog, stk, ctx_sems, cleanup_cm)

    with ExitStack() as b_stk:
        def bsb(name, shape, dt):
            return b_stk.enter_context(nc.sbuf_tensor(name, shape, dt))

        def bps(name, shape, dt=F32):
            return b_stk.enter_context(nc.psum_tensor(name, shape, dt))

        wq_sb = bsb("wq_sb", [128, HC, D], BF16)
        wd_sb = bsb("wd_sb", [128, HC, D], BF16)
        gam_row = bsb("gam_row", [1, D], F32)
        bet_row = bsb("bet_row", [1, D], F32)
        bd_row = bsb("bd_row", [1, D], BF16)
        gamma_b = bsb("gamma_b", [128, D], F32)
        beta_b = bsb("beta_b", [128, D], F32)
        xlntok_gA = bsb("xlntok_gA", [128, TPG, D], BF16)
        xlntok_gB = bsb("xlntok_gB", [128, TPG, D], BF16)
        xfm_gA = bsb("xfm_gA", [128, HC, 512], BF16)
        xfm_gB = bsb("xfm_gB", [128, HC, 512], BF16)
        q_gA = bsb("q_gA", [128, HC, 512], BF16)
        q_gB = bsb("q_gB", [128, HC, 512], BF16)
        ctx_sb = bsb("ctx_sb", [128, HC, 512], BF16)
        z_all = bsb("z_all", [16, 512], BF16)
        z_f32 = bsb("z_f32", [16, 512], F32)
        zb_sb = bsb("zb_sb", [128, 512], BF16)
        hs_sb = bsb("hs_sb", [128, D], F32)
        ym_sb = bsb("ym_sb", [128, D], F32)
        vcolb = bsb("vcolb", [128, 1], F32)
        scolb = bsb("scolb", [128, 1], F32)
        qcolb = bsb("qcolb", [128, 1], F32)
        mucol = bsb("mucol", [128, 1], F32)
        rstdi = bsb("rstdi", [128, 1], F32)
        nmui = bsb("nmui", [128, 1], F32)
        ycol = bsb("ycol", [128, 1], F32)
        scrEb = bsb("scrEb", [128, 512], F32)
        scrRb = bsb("scrRb", [128, 512], F32)

        ps_tr_b = bps("ps_tr_b", [128, 4, 128])
        ps_tr_b2 = bps("ps_tr_b2", [128, 4, 128])
        ps_q1 = bps("ps_q1", [128, 512])
        ps_q2 = bps("ps_q2", [128, 512])
        ps_dn = bps("ps_dn", [16, 512])
        ps_zb = bps("ps_zb", [128, 512])
        ps_hs = bps("ps_hs", [128, 1024])

        R("sync", lambda e: e.dma_start(out=gam_row[:], in_=gamma_in), DawB, 16)
        R("sync", lambda e: e.dma_start(out=bet_row[:], in_=beta_in), DawB, 16)
        R("gpsimd", lambda e: e.dma_start(
            out=wd_sb[:], in_=wdT.rearrange("(c p) j -> p c j", p=128)), DB_wd, 16)
        R("gpsimd", lambda e: e.dma_start(
            out=wq_sb[:], in_=wqT.rearrange("(c p) j -> p c j", p=128)), DB_wq, 16)
        R("gpsimd", lambda e: e.dma_start(out=bd_row[:], in_=bd_in), DB_bd, 16)
        W("tensor", DB_wd, 16)
        W("tensor", DB_wq, 16)
        W("tensor", DB_bd, 16)
        for row, dstb in ((gam_row, gamma_b), (bet_row, beta_b)):
            for half in range(2):
                mv = R("tensor", lambda e, row=row, half=half: e.matmul(
                    ps_zb[:], ones1f[:1, :], row[:1, 512 * half:512 * (half + 1)],
                    start=True, stop=True), Spe)
                W("vector", Spe, mv)
                cv = R("vector", lambda e, dstb=dstb, half=half: e.tensor_copy(
                    out=dstb[:, 512 * half:512 * (half + 1)], in_=ps_zb[:]), Sve)
                W("tensor", Sve, cv)

        bstate = {
            "tr_evac": 0, "tr_evac2": [0, 0],
            "xlntok_rd": [0, 0], "tr_pe": [0, 0], "xfm_pe": [0, 0],
            "q_stt": [0, 0], "bank_free": [0, 0],
            "ctx_pe": 0, "zb_evac": 0, "hs_rd": 0, "scr_rd": 0,
            "vcolb_rd": 0,
        }

        def prep_xln(g):
            par = g % 2
            xlg = xlntok_gA if par == 0 else xlntok_gB
            xvs = []
            for tl in range(TPG):
                T = TPG * g + tl
                W("scalar", Spe, bstate["tr_pe"][par])
                W("scalar", Sve, bstate["xlntok_rd"][par])
                xvs.append(R("scalar", lambda e, T=T, tl=tl, xlg=xlg: e.activation(
                    out=xlg[:, tl, :], in_=x_res[:, T, :], func=AF.Identity,
                    bias=nmu1s[:, T:T + 1], scale=rstd1s[:, T:T + 1]), Sac))
            return xvs

        def prep_tr_tile(g, tl, xv):
            par = g % 2
            xlg = xlntok_gA if par == 0 else xlntok_gB
            xfg = xfm_gA if par == 0 else xfm_gB
            W("tensor", Sac, xv)
            last_mv = None
            for half in range(2):
                ps = ps_tr_b if half == 0 else ps_tr_b2
                W("tensor", Sve, bstate["tr_evac2"][half])
                for cc in range(4):
                    c = 4 * half + cc
                    last_mv = R("tensor", lambda e, c=c, cc=cc, tl=tl, xlg=xlg, ps=ps:
                                e.matmul(ps[:, cc, :],
                                         xlg[:, tl, 128 * c:128 * (c + 1)],
                                         ident[:], start=True, stop=True), Spe)
                W("vector", Spe, last_mv)
                ev = R(
                    "vector", lambda e, half=half, tl=tl, xfg=xfg, ps=ps: e.tensor_copy(
                        out=xfg[:, 4 * half:4 * half + 4, 128 * tl:128 * (tl + 1)],
                        in_=ps[:]), Sve, 1, False)
                bstate["tr_evac2"][half] = ev
                bstate["tr_evac"] = ev
            if tl == TPG - 1:
                bstate["tr_pe"][par] = last_mv

        def prep_q(g):
            par = g % 2
            xfg = xfm_gA if par == 0 else xfm_gB
            qg = q_gA if par == 0 else q_gB
            W("tensor", Sve, bstate["tr_evac"])
            for j in range(HC):
                bank = ps_q1 if j % 2 == 0 else ps_q2
                W("tensor", Sve, bstate["bank_free"][j % 2])
                mv = None
                for c in range(HC):
                    mv = R("tensor", lambda e, c=c, j=j, bank=bank, xfg=xfg: e.matmul(
                        bank[:], wq_sb[:, c, 128 * j:128 * (j + 1)],
                        xfg[:, c, :], start=(c == 0), stop=(c == HC - 1)), Spe)
                W("scalar", Spe, mv)
                W("scalar", Sve, bstate["scr_rd"])
                ev1 = R("scalar", lambda e, j=j, bank=bank: e.activation(
                    out=scrEb[:], in_=bank[:], func=AF.Exp,
                    bias=bq_cols[:, j:j + 1], scale=1.0), Sac)
                W("vector", Sac, ev1)
                R("vector", lambda e, j=j, bank=bank: e.tensor_scalar(
                    out=scrRb[:], in0=bank[:], scalar1=bq_cols[:, j:j + 1],
                    scalar2=0.0, op0=ALU.add, op1=ALU.max), Sve, 1, False)
                W("vector", Spe, bstate["q_pe"][par] if "q_pe" in bstate else 0)
                ev2 = R("vector", lambda e, j=j, qg=qg: e.scalar_tensor_tensor(
                    out=qg[:, j, :], in0=scrEb[:], scalar=1.0,
                    in1=scrRb[:], op0=ALU.min, op1=ALU.add), Sve, 1, False)
                bstate["bank_free"][j % 2] = ev2
                bstate["scr_rd"] = ev2
            bstate["q_stt"][par] = bstate["bank_free"][(HC - 1) % 2]

        def prep_group(g):
            xvs = prep_xln(g)
            for tl in range(TPG):
                prep_tr_tile(g, tl, xvs[tl])
            prep_q(g)

        bstate["q_pe"] = [0, 0]
        prep_group(0)

        for g in range(NG):
            seq = g // GPS
            par = g % 2
            xlg = xlntok_gA if par == 0 else xlntok_gB
            qg = q_gA if par == 0 else q_gB
            W("tensor", Sve, bstate["q_stt"][par])
            dv = None
            for c in range(HC):
                dv = R("tensor", lambda e, c=c, seq=seq, qg=qg: e.matmul(
                    ps_dn[:], kcols[:, seq, c, :], qg[:, c, :],
                    start=(c == 0), stop=(c == HC - 1)), Spe)
            W("vector", Spe, dv)
            R("vector", lambda e: e.reciprocal(out=z_f32[:], in_=ps_dn[:]), Sve)
            zv = R("vector", lambda e: e.tensor_copy(out=z_all[:], in_=z_f32[:]), Sve)
            W("tensor", Sve, zv)
            first_ctx = True
            for c in range(HC):
                W("tensor", Sve, bstate["zb_evac"])
                bv_ = R("tensor", lambda e, c=c: e.matmul(
                    ps_zb[:], bc_sb[:, 128 * c:128 * (c + 1)], z_all[:],
                    start=True, stop=True), Spe)
                W("vector", Spe, bv_)
                bstate["zb_evac"] = R("vector", lambda e: e.tensor_copy(
                    out=zb_sb[:], in_=ps_zb[:]), Sve, 1, False)
                bank = ps_q1 if c % 2 == 0 else ps_q2
                W("tensor", Sve, bstate["bank_free"][c % 2])
                cv = None
                for hh in range(2):
                    cv = R("tensor", lambda e, c=c, hh=hh, seq=seq, bank=bank, qg=qg:
                           e.matmul(bank[64 * hh:64 * (hh + 1), :],
                                    kv_sb[:, seq, 128 * c + 64 * hh:128 * c + 64 * hh + 64],
                                    qg[:, c, :], start=True, stop=True), Spe)
                W("vector", Spe, cv)
                W("vector", Sve, bstate["zb_evac"])
                if first_ctx:
                    W("vector", Spe, bstate["ctx_pe"])
                    first_ctx = False
                ev = R("vector", lambda e, c=c, bank=bank: e.tensor_tensor(
                    out=ctx_sb[:, c, :], in0=bank[:], in1=zb_sb[:],
                    op=ALU.mult), Sve, 1, False)
                bstate["bank_free"][c % 2] = ev
            bstate["q_pe"][par] = Spe.n
            ctx_done = bstate["bank_free"][:]
            xvs_next = prep_xln(g + 1) if g + 1 < NG else None
            W("tensor", Sve, ctx_done[0])
            W("tensor", Sve, ctx_done[1])
            W("tensor", DB_wd, 16)
            for tl in range(TPG):
                T = TPG * g + tl
                tloc = 128 * tl
                W("tensor", Sve, bstate["hs_rd"])
                mv = None
                for jh in range(2):
                    for c in range(HC):
                        R("tensor", lambda e, c=c, jh=jh, tloc=tloc: e.matmul(
                            ps_hs[:, 512 * jh:512 * (jh + 1)],
                            ctx_sb[:, c, tloc:tloc + 128],
                            wd_sb[:, c, 512 * jh:512 * (jh + 1)],
                            start=(c == 0), stop=False), Spe)
                    mv = R("tensor", lambda e, jh=jh: e.matmul(
                        ps_hs[:, 512 * jh:512 * (jh + 1)], ones1b[:1, :],
                        bd_row[:1, 512 * jh:512 * (jh + 1)],
                        start=False, stop=True), Spe)
                W("vector", Spe, mv)
                hv = R("vector", lambda e, tl=tl, xlg=xlg: e.scalar_tensor_tensor(
                    out=hs_sb[:], in0=ps_hs[:], scalar=0.0, in1=xlg[:, tl, :],
                    op0=ALU.add, op1=ALU.add), Sve, 1, False)
                bstate["hs_rd"] = hv
                if tl == TPG - 1:
                    bstate["xlntok_rd"][par] = hv
                W("scalar", Sve, hv)
                sqv = R("scalar", lambda e: e.activation(
                    out=ym_sb[:], in_=hs_sb[:], func=AF.Square), Sac)
                scv = R("vector", lambda e: e.tensor_reduce(
                    out=scolb[:], in_=hs_sb[:], axis=AX.X, op=ALU.add), Sve)
                W("vector", Sac, sqv)
                R("vector", lambda e: e.tensor_reduce(
                    out=qcolb[:], in_=ym_sb[:], axis=AX.X, op=ALU.add), Sve)
                W("scalar", Sve, scv)
                W("scalar", Sve, bstate["vcolb_rd"])
                R("scalar", lambda e: e.activation(
                    out=mucol[:], in_=scolb[:], func=AF.Identity,
                    scale=dinv_col[:]), Sac)
                msv = R("scalar", lambda e: e.activation(
                    out=vcolb[:], in_=scolb[:], func=AF.Square, scale=dinv_col[:]), Sac)
                W("vector", Sac, msv)
                av = R("vector", lambda e: e.tensor_scalar(
                    out=qcolb[:], in0=qcolb[:], scalar1=1.0 / D, scalar2=vcolb[:],
                    op0=ALU.mult, op1=ALU.subtract), Sve)
                bstate["vcolb_rd"] = av
                W("scalar", Sve, av)
                sv = R("scalar", lambda e: e.activation(
                    out=scolb[:], in_=qcolb[:], func=AF.Sqrt, bias=epsi_col[:]), Sac)
                W("vector", Sac, sv)
                R("vector", lambda e: e.reciprocal(out=rstdi[:], in_=scolb[:]), Sve)
                nv = R("vector", lambda e: e.tensor_scalar(
                    out=nmui[:], in0=rstdi[:], scalar1=mucol[:], scalar2=-1.0,
                    op0=ALU.mult, op1=ALU.mult), Sve)
                W("scalar", Sve, nv)
                hn = R("scalar", lambda e: e.activation(
                    out=hs_sb[:], in_=hs_sb[:], func=AF.Identity,
                    bias=nmui[:], scale=rstdi[:]), Sac)
                W("vector", Sac, hn)
                R("vector", lambda e: e.scalar_tensor_tensor(
                    out=hs_sb[:], in0=hs_sb[:], scalar=0.0, in1=gamma_b[:],
                    op0=ALU.add, op1=ALU.mult), Sve, 1, False)
                R("vector", lambda e: e.tensor_add(hs_sb[:], hs_sb[:], beta_b[:]), Sve, 1, False)
                xmv = R("vector", lambda e, T=T: e.tensor_tensor(
                    out=x_res[:, T, :], in0=hs_sb[:], in1=x_res[:, T, :],
                    op=ALU.add), Sve, 1, False)
                W("scalar", Sve, xmv)
                sqv = R("scalar", lambda e, T=T: e.activation(
                    out=ym_sb[:], in_=x_res[:, T, :], func=AF.Square), Sac)
                scv = R("vector", lambda e, T=T: e.tensor_reduce(
                    out=scolb[:], in_=x_res[:, T, :], axis=AX.X, op=ALU.add), Sve)
                W("vector", Sac, sqv)
                R("vector", lambda e: e.tensor_reduce(
                    out=qcolb[:], in_=ym_sb[:], axis=AX.X, op=ALU.add), Sve)
                W("scalar", Sve, scv)
                W("scalar", Sve, bstate["vcolb_rd"])
                R("scalar", lambda e, T=T: e.activation(
                    out=mu2s[:, T:T + 1], in_=scolb[:], func=AF.Identity,
                    scale=dinv_col[:]), Sac)
                msv = R("scalar", lambda e: e.activation(
                    out=vcolb[:], in_=scolb[:], func=AF.Square, scale=dinv_col[:]), Sac)
                W("vector", Sac, msv)
                av = R("vector", lambda e: e.tensor_scalar(
                    out=qcolb[:], in0=qcolb[:], scalar1=1.0 / D, scalar2=vcolb[:],
                    op0=ALU.mult, op1=ALU.subtract), Sve)
                bstate["vcolb_rd"] = av
                W("scalar", Sve, av)
                sv = R("scalar", lambda e: e.activation(
                    out=scolb[:], in_=qcolb[:], func=AF.Sqrt, bias=eps1_col[:]), Sac)
                W("vector", Sac, sv)
                R("vector", lambda e, T=T: e.reciprocal(out=rstd2s[:, T:T + 1], in_=scolb[:]), Sve)
                R("vector", lambda e, T=T: e.tensor_scalar(
                    out=ym_sb[:], in0=x_res[:, T, :], scalar1=mu2s[:, T:T + 1],
                    scalar2=None, op0=ALU.subtract), Sve, 1, False)
                R("vector", lambda e: e.tensor_reduce(
                    out=ycol[:], in_=ym_sb[:], axis=AX.X, op=ALU.max,
                    apply_absolute_value=True), Sve)
                R("vector", lambda e, T=T: e.tensor_single_scalar(
                    out=ycol[:], in_=ycol[:], scalar=rstd2s[:, T:T + 1],
                    op=ALU.mult), Sve)
                R("vector", lambda e: e.tensor_max(ymax_acc[:], ymax_acc[:], ycol[:]), Sve)
                if xvs_next is not None:
                    prep_tr_tile(g + 1, tl, xvs_next[tl])
            if xvs_next is not None:
                prep_q(g + 1)
            bstate["ctx_pe"] = Spe.n

        W("sync", Sve, Sve.n)
        c1 = R("sync", lambda e: e.dma_start(out=cc1_in.ap(), in_=ymax_acc[:]), Drt, 16)
        W("gpsimd", Drt, c1)
        cv = R("gpsimd", lambda e: e.collective_compute(
            "AllReduce", ALU.max, replica_groups=[list(range(NCORES))],
            ins=[cc1_in.ap()], outs=[cc1_out.ap()]), Scc, 1)
        W("sync", Scc, cv)
        rv = R("sync", lambda e: e.dma_start(
            out=yrow[:], in_=cc1_out.ap().rearrange("p a -> a p")), Drt, 16)
        W("vector", Drt, rv)
        R("vector", lambda e: e.tensor_reduce(
            out=sc1[:, 0:1], in_=yrow[:], axis=AX.X, op=ALU.max), Sve)
        sgv = R("scalar", lambda e: e.activation(
            out=sc1[:, 1:2], in_=clipx_sb[:], func=AF.Sigmoid), Sac)
        W("vector", Sac, sgv)
        R("vector", lambda e: e.tensor_tensor(
            out=sc1[:, 2:3], in0=sc1[:, 0:1], in1=sc1[:, 1:2], op=ALU.mult), Sve)
        R("vector", lambda e: e.tensor_scalar(
            out=sc1[:, 2:3], in0=sc1[:, 2:3], scalar1=R127, scalar2=None,
            op0=ALU.mult), Sve)
        sxv = R("vector", lambda e: e.reciprocal(out=sc1[:, 3:4], in_=sc1[:, 2:3]), Sve)
        W("tensor", Sve, sxv)
        mv = R("tensor", lambda e: e.matmul(
            ps_zb[:, 0:2], ones1f[:1, :], sc1[:1, 2:4], start=True, stop=True), Spe)
        W("vector", Spe, mv)
        R("vector", lambda e: e.tensor_copy(out=sx_col[:], in_=ps_zb[:, 0:1]), Sve)
        R("vector", lambda e: e.tensor_copy(out=invsx_col[:], in_=ps_zb[:, 1:2]), Sve)
        R("vector", lambda e: e.tensor_single_scalar(
            out=s1s[:], in_=rstd2s[:], scalar=invsx_col[:], op=ALU.mult), Sve)
        R("vector", lambda e: e.tensor_scalar(
            out=s2s[:], in0=mu2s[:], scalar1=-1.0, scalar2=None,
            op0=ALU.mult), Sve)
        R("vector", lambda e: e.tensor_mul(s2s[:], s2s[:], s1s[:]), Sve)
        R("vector", lambda e: e.tensor_single_scalar(
            out=sxg_cols[:], in_=scg_cols[:], scalar=sx_col[:], op=ALU.mult), Sve)
        R("vector", lambda e: e.tensor_single_scalar(
            out=sxu_cols[:], in_=scu_cols[:], scalar=sx_col[:], op=ALU.mult), Sve)

    barrier()

    if stage in ("Bq", "Bctx", "Bxln"):
        W("gpsimd", Ddbg, Ddbg.n)
        W("sync", Ddbg, Ddbg.n)
        barrier()
        return _finish(nc, prog, stk, ctx_sems, cleanup_cm)
    if stage == "B":
        _dbg_out(W, R, outb, x_res, Sve, Ddbg)
        barrier()
        return _finish(nc, prog, stk, ctx_sems, cleanup_cm)

    with ExitStack() as c_stk:
        def csb(name, shape, dt):
            return c_stk.enter_context(nc.sbuf_tensor(name, shape, dt))

        def cps(name, shape, dt=F32):
            return c_stk.enter_context(nc.psum_tensor(name, shape, dt))

        wgint = csb("wgint", [128, HC, II], BF16)
        wuint = csb("wuint", [128, HC, II], BF16)
        xqscr = csb("xqscr", [128, D], F32)
        xqi32 = csb("xqi32", [128, D], I32)
        xqtokA = csb("xqtokA", [128, D], BF16)
        xqtokB = csb("xqtokB", [128, D], BF16)
        xq_fm = csb("xq_fm", [128, HC, 512], BF16)
        silu_sb = csb("silu_sb", [128, 512], F32)
        h_sb0 = csb("h_sb0", [128, 512], BF16)
        h_sb1 = csb("h_sb1", [128, 512], BF16)
        h_sb2 = csb("h_sb2", [128, 512], BF16)
        hcol = csb("hcol", [128, 1], F32)

        ps_gA = cps("ps_gA", [128, 512])
        ps_gB = cps("ps_gB", [128, 512])
        ps_uA = cps("ps_uA", [128, 512])
        ps_uB = cps("ps_uB", [128, 512])
        ps_tr2 = cps("ps_tr2", [128, 4, 128])

        R("scalar", lambda e: e.dma_start(
            out=wgint[:], in_=wg_sp.ap().rearrange("(c p) i -> p c i", p=128)), DwC, 16)
        wu_l = R("scalar", lambda e: e.dma_start(
            out=wuint[:], in_=wu_sp.ap().rearrange("(c p) i -> p c i", p=128)), DwC, 16)

        if _DBG:
            for T in range(NT):
                R("sync", lambda e, T=T: e.dma_start(
                    out=dbg_xr[128 * T:128 * (T + 1), :], in_=x_res[:, T, :]), Drt, 16)
            R("sync", lambda e: e.dma_start(out=dbg_s.ap()[:, 0:NT], in_=s1s[:]), Drt, 16)
            R("sync", lambda e: e.dma_start(out=dbg_s.ap()[:, NT:2 * NT], in_=s2s[:]), Drt, 16)
        h_bufs = [h_sb0, h_sb1, h_sb2]
        h_sems = [Dh0, Dh1, Dh2]
        cstate = {"hdma": [0, 0, 0], "tr2_evac": 0, "xqtok_pe": [0, 0],
                  "xqfm_pe": 0, "g_evac": [0, 0], "u_evac": [0, 0],
                  "xqscr_rd": 0, "h_stt": 0}

        for g in range(NG):
            gsl = slice(512 * g, 512 * (g + 1))
            for tl in range(TPG):
                T = TPG * g + tl
                tloc = 128 * tl
                W("scalar", Sve, cstate["xqscr_rd"])
                uv = R("scalar", lambda e, T=T: e.activation(
                    out=xqscr[:], in_=x_res[:, T, :], func=AF.Identity,
                    bias=s2s[:, T:T + 1], scale=s1s[:, T:T + 1]), Sac)
                W("vector", Sac, uv)
                R("vector", lambda e: e.tensor_scalar(
                    out=xqscr[:], in0=xqscr[:], scalar1=127.0, scalar2=-128.0,
                    op0=ALU.min, op1=ALU.max), Sve, 1, False)
                cstate["xqscr_rd"] = R("vector", lambda e: e.tensor_scalar(
                    out=xqi32[:], in0=xqscr[:], scalar1=MAGIC, scalar2=None,
                    op0=ALU.add), Sve, 1, False)
                xt = xqtokA if T % 2 == 0 else xqtokB
                W("vector", Spe, cstate["xqtok_pe"][T % 2])
                qv = R("vector", lambda e, xt=xt: e.tensor_scalar(
                    out=xt[:], in0=xqi32[:], scalar1=-MAGIC, scalar2=None,
                    op0=ALU.add), Sve, 1, False)
                if _DBG:
                    W("sync", Sve, qv)
                    dmp = R("sync", lambda e, T=T, xt=xt: e.dma_start(
                        out=dbg_xt[128 * T:128 * (T + 1), :], in_=xt[:]), Drt, 16)
                    W("vector", Drt, dmp)
                W("tensor", Sve, qv)
                W("tensor", Spe, cstate["xqfm_pe"])
                last_mv = None
                for half in range(2):
                    W("tensor", Sve, cstate["tr2_evac"])
                    for cc in range(4):
                        c = 4 * half + cc
                        last_mv = R("tensor", lambda e, c=c, cc=cc, xt=xt: e.matmul(
                            ps_tr2[:, cc, :], xt[:, 128 * c:128 * (c + 1)],
                            ident[:], start=True, stop=True), Spe)
                    W("vector", Spe, last_mv)
                    cstate["tr2_evac"] = R("vector", lambda e, half=half, tloc=tloc: e.tensor_copy(
                        out=xq_fm[:, 4 * half:4 * half + 4, tloc:tloc + 128],
                        in_=ps_tr2[:]),Sve, 1, False)
                cstate["xqtok_pe"][T % 2] = last_mv
            if _DBG:
                W("sync", Sve, cstate["tr2_evac"])
                dmp = R("sync", lambda e, g=g: e.dma_start(
                    out=dbg_xq.ap().rearrange("(c p) t -> p c t", p=128)[:, :, 512 * g:512 * (g + 1)],
                    in_=xq_fm[:]), Drt, 16)
                W("vector", Drt, dmp)
            W("tensor", Sve, cstate["tr2_evac"])
            W("tensor", DwC, wu_l)
            for ic in range(IC):
                gbank = ps_gA if ic % 2 == 0 else ps_gB
                ubank = ps_uA if ic % 2 == 0 else ps_uB
                W("tensor", Sve, cstate["g_evac"][ic % 2])
                W("tensor", Sve, cstate["u_evac"][ic % 2])
                gv = uv2 = None
                for c in range(HC):
                    gv = R("tensor", lambda e, c=c, ic=ic, gbank=gbank: e.matmul(
                        gbank[:], wgint[:, c, 128 * ic:128 * (ic + 1)],
                        xq_fm[:, c, :], start=(c == 0), stop=(c == HC - 1)), Spe)
                for c in range(HC):
                    uv2 = R("tensor", lambda e, c=c, ic=ic, ubank=ubank: e.matmul(
                        ubank[:], wuint[:, c, 128 * ic:128 * (ic + 1)],
                        xq_fm[:, c, :], start=(c == 0), stop=(c == HC - 1)), Spe)
                W("scalar", Spe, gv)
                W("scalar", Sve, cstate["h_stt"])
                sv = R("scalar", lambda e, ic=ic, gbank=gbank: e.activation(
                    out=silu_sb[:], in_=gbank[:], func=AF.Silu,
                    scale=sxg_cols[:, ic:ic + 1]), Sac)
                bidx = (g * IC + ic) % 3
                hb = h_bufs[bidx]
                W("vector", h_sems[bidx], cstate["hdma"][bidx])
                W("vector", Sac, sv)
                W("vector", Spe, uv2)
                hv = R("vector", lambda e, ic=ic, ubank=ubank, hb=hb: e.scalar_tensor_tensor(
                    out=hb[:], in0=ubank[:], scalar=sxu_cols[:, ic:ic + 1],
                    in1=silu_sb[:], op0=ALU.mult, op1=ALU.mult), Sve, 1, False)
                cstate["g_evac"][ic % 2] = hv
                cstate["u_evac"][ic % 2] = hv
                cstate["h_stt"] = hv
                R("vector", lambda e, hb=hb: e.tensor_reduce(
                    out=hcol[:], in_=hb[:], axis=AX.X, op=ALU.max,
                    apply_absolute_value=True), Sve)
                R("vector", lambda e: e.tensor_max(hmax_acc[:], hmax_acc[:], hcol[:]), Sve)
                W("sync", Sve, hv)
                cstate["hdma"][bidx] = R("sync", lambda e, ic=ic, gsl=gsl, hb=hb: e.dma_start(
                    out=h_spill.ap()[128 * ic:128 * (ic + 1), gsl], in_=hb[:]),
                    h_sems[bidx], 16)
            cstate["xqfm_pe"] = Spe.n

        W("sync", Sve, Sve.n)
        c2 = R("sync", lambda e: e.dma_start(out=cc2_in.ap(), in_=hmax_acc[:]), Drt, 16)
        W("gpsimd", Drt, c2)
        cv2 = R("gpsimd", lambda e: e.collective_compute(
            "AllReduce", ALU.max, replica_groups=[list(range(NCORES))],
            ins=[cc2_in.ap()], outs=[cc2_out.ap()]), Scc, 1)
        W("sync", Scc, cv2)
        rv = R("sync", lambda e: e.dma_start(
            out=hrow[:], in_=cc2_out.ap().rearrange("p a -> a p")), Drt, 16)
        W("vector", Drt, rv)
        R("vector", lambda e: e.tensor_reduce(
            out=sc1[:, 0:1], in_=hrow[:], axis=AX.X, op=ALU.max), Sve)
        sgv = R("scalar", lambda e: e.activation(
            out=sc1[:, 1:2], in_=clipd_sb[:], func=AF.Sigmoid), Sac)
        W("vector", Sac, sgv)
        R("vector", lambda e: e.tensor_tensor(
            out=sc1[:, 2:3], in0=sc1[:, 0:1], in1=sc1[:, 1:2], op=ALU.mult), Sve)
        R("vector", lambda e: e.tensor_scalar(
            out=sc1[:, 2:3], in0=sc1[:, 2:3], scalar1=R127, scalar2=None,
            op0=ALU.mult), Sve)
        sdv = R("vector", lambda e: e.reciprocal(out=sc1[:, 3:4], in_=sc1[:, 2:3]), Sve)
        W("tensor", Sve, sdv)
        mv = R("tensor", lambda e: e.matmul(
            ps_gA[:, 0:2], ones1f[:1, :], sc1[:1, 2:4], start=True, stop=True), Spe)
        W("vector", Spe, mv)
        R("vector", lambda e: e.tensor_copy(out=sd_col[:], in_=ps_gA[:, 0:1]), Sve)
        R("vector", lambda e: e.tensor_copy(out=invsd_col[:], in_=ps_gA[:, 1:2]), Sve)

    barrier()

    if stage == "C":
        _dbg_out(W, R, outb, x_res, Sve, Ddbg)
        barrier()
        return _finish(nc, prog, stk, ctx_sems, cleanup_cm)

    with ExitStack() as d_stk:
        def dsb(name, shape, dt):
            return d_stk.enter_context(nc.sbuf_tensor(name, shape, dt))

        def dps(name, shape, dt=F32):
            return d_stk.enter_context(nc.psum_tensor(name, shape, dt))

        wdnint = dsb("wdnint", [128, IC, D], BF16)
        hre = dsb("hre", [128, IC, 512], BF16)
        hqA = dsb("hqA", [128, IC, 512], BF16)
        hqB = dsb("hqB", [128, IC, 512], BF16)
        hscr = dsb("hscr", [128, 512], F32)
        hi32 = dsb("hi32", [128, 512], I32)
        srd_row = dsb("srd_row", [1, D], F32)
        sdsrd_b = dsb("sdsrd_b", [128, D], F32)
        outA = dsb("outA", [128, D], F32)
        outB = dsb("outB", [128, D], F32)

        ps_dnA = dps("ps_dnA", [128, 1024])
        ps_dnB = dps("ps_dnB", [128, 1024])
        ps_m = dps("ps_m", [128, 512])

        wdn_l = R("scalar", lambda e: e.dma_start(
            out=wdnint[:], in_=wdn_sp.ap().rearrange("(c p) j -> p c j", p=128)), DwD, 16)

        wv = R("sync", lambda e: e.dma_start(
            out=v_rds.ap().rearrange("(c p) -> p c", p=128), in_=scd_cols[:],
            allow_slow_non_contiguous=True), Drt, 16)
        W("sync", Drt, wv)
        rv = R("sync", lambda e: e.dma_start(
            out=srd_row[:], in_=v_rds.ap().rearrange("(a i) -> a i", a=1)), Drt, 16)
        W("tensor", Drt, rv)
        for half in range(2):
            mv = R("tensor", lambda e, half=half: e.matmul(
                ps_m[:], ones1f[:1, :], srd_row[:1, 512 * half:512 * (half + 1)],
                start=True, stop=True), Spe)
            W("vector", Spe, mv)
            cv = R("vector", lambda e, half=half: e.tensor_copy(
                out=sdsrd_b[:, 512 * half:512 * (half + 1)], in_=ps_m[:]), Sve)
            W("tensor", Sve, cv)
        dcv = R("vector", lambda e: e.tensor_single_scalar(
            out=sdsrd_b[:], in_=sdsrd_b[:], scalar=sd_col[:], op=ALU.mult), Sve)

        dstate = {"outdma": [0, 0], "dn_evac": [0, 0],
                  "hq_pe": [0, 0], "quant_done": [0, 0], "hre_free": 0}

        def emit_hre_load(g):
            gsl = slice(512 * g, 512 * (g + 1))
            W("sync", Sve, dstate["hre_free"])
            hl = None
            for chalf in range(2):
                lo = 11 * chalf
                hi = 11 * (chalf + 1)
                hl = R("sync", lambda e, gsl=gsl, lo=lo, hi=hi: e.dma_start(
                    out=hre[:, lo:hi, :],
                    in_=h_spill.ap().rearrange("(c p) t -> p c t", p=128)[:, lo:hi, gsl]),
                    Dhre, 16)
            return hl

        def quant_ops(g, hl):
            hq = hqA if g % 2 == 0 else hqB
            ops = []
            for ic in range(IC):
                def chunk(ic=ic, hq=hq, g=g, hl=hl):
                    if ic == 0:
                        W("vector", Spe, dstate["hq_pe"][g % 2])
                        W("vector", Dhre, hl)
                    R("vector", lambda e, ic=ic: e.tensor_scalar(
                        out=hscr[:], in0=hre[:, ic, :], scalar1=invsd_col[:],
                        scalar2=127.0, op0=ALU.mult, op1=ALU.min), Sve, 1, False)
                    R("vector", lambda e: e.tensor_scalar(
                        out=hi32[:], in0=hscr[:], scalar1=-128.0, scalar2=MAGIC,
                        op0=ALU.max, op1=ALU.add), Sve, 1, False)
                    qv = R("vector", lambda e, ic=ic, hq=hq: e.tensor_scalar(
                        out=hq[:, ic, :], in0=hi32[:], scalar1=-MAGIC, scalar2=None,
                        op0=ALU.add), Sve, 1, False)
                    if ic == IC - 1:
                        dstate["quant_done"][g % 2] = qv
                        dstate["hre_free"] = qv
                ops.append(chunk)
            return ops

        for op in quant_ops(0, emit_hre_load(0)):
            op()

        per_tile = [6, 6, 5, 5]
        for g in range(NG):
            hq = hqA if g % 2 == 0 else hqB
            W("tensor", Sve, dstate["quant_done"][g % 2])
            if g == 0:
                W("tensor", DwD, wdn_l)
            pending = quant_ops(g + 1, emit_hre_load(g + 1)) if g + 1 < NG else []
            for tl in range(TPG):
                T = TPG * g + tl
                tloc = 128 * tl
                bank = ps_dnA if tl % 2 == 0 else ps_dnB
                W("tensor", Sve, dstate["dn_evac"][tl % 2])
                mv = None
                for jh in range(2):
                    for ic in range(IC):
                        mv = R("tensor", lambda e, ic=ic, jh=jh, tloc=tloc, bank=bank,
                               hq=hq:
                               e.matmul(bank[:, 512 * jh:512 * (jh + 1)],
                                        hq[:, ic, tloc:tloc + 128],
                                        wdnint[:, ic, 512 * jh:512 * (jh + 1)],
                                        start=(ic == 0), stop=(ic == IC - 1)), Spe)
                for _ in range(min(per_tile[tl], len(pending))):
                    pending.pop(0)()
                ob = outA if tl % 2 == 0 else outB
                W("vector", Spe, mv)
                W("vector", Sve, dcv)
                W("vector", (Dout0 if tl % 2 == 0 else Dout1), dstate["outdma"][tl % 2])
                R("vector", lambda e, bank=bank, ob=ob: e.scalar_tensor_tensor(
                    out=ob[:], in0=bank[:], scalar=0.0, in1=sdsrd_b[:],
                    op0=ALU.add, op1=ALU.mult), Sve, 1, False)
                ov = R("vector", lambda e, T=T, ob=ob: e.tensor_tensor(
                    out=ob[:], in0=ob[:], in1=x_res[:, T, :], op=ALU.add), Sve, 1, False)
                dstate["dn_evac"][tl % 2] = ov
                W("sync", Sve, ov)
                dstate["outdma"][tl % 2] = R("sync", lambda e, T=T, ob=ob: e.dma_start(
                    out=outb[128 * T:128 * (T + 1), :], in_=ob[:]),
                    (Dout0 if tl % 2 == 0 else Dout1), 16)
            for op in pending:
                op()
            dstate["hq_pe"][g % 2] = Spe.n

        W("sync", Dout0, Dout0.n)
        W("sync", Dout1, Dout1.n)
        W("gpsimd", Dout0, Dout0.n)
        W("gpsimd", Dout1, Dout1.n)
    barrier()

    return _finish(nc, prog, stk, ctx_sems, cleanup_cm)


def _dbg_out(W, R, outb, x_res, Sve, Ddbg):
    for T in range(NT):
        W("gpsimd", Sve, Sve.n)
        R("gpsimd", lambda e, T=T: e.dma_start(
            out=outb[128 * T:128 * (T + 1), :], in_=x_res[:, T, :]), Ddbg, 16)
    W("gpsimd", Ddbg, Ddbg.n)
    W("sync", Ddbg, Ddbg.n)


def _finish(nc, prog, stk, ctx_sems, cleanup_cm):
    with nc.Block() as block:
        @block.sync
        def _(e):
            for f in prog["sync"]:
                f(e)

        @block.scalar
        def _(e):
            for f in prog["scalar"]:
                f(e)

        @block.vector
        def _(e):
            for f in prog["vector"]:
                f(e)

        @block.tensor
        def _(e):
            for f in prog["tensor"]:
                f(e)

        @block.gpsimd
        def _(e):
            for f in prog["gpsimd"]:
                f(e)
    stk.close()
    cleanup_cm.__exit__(None, None, None)
    return nc


def _host_inputs(full):
    x = np.ascontiguousarray(np.asarray(full["L0_attn_input_tensor"], np.float32))
    com = {
        "wqT": np.ascontiguousarray(np.asarray(full["L0_attn_Wq"], np.float32).T),
        "wkT": np.ascontiguousarray(np.asarray(full["L0_attn_Wk"], np.float32).T),
        "wvT": np.ascontiguousarray(np.asarray(full["L0_attn_Wv"], np.float32).T),
        "wdT": np.ascontiguousarray(np.asarray(full["L0_attn_Wd"], np.float32).T),
        "bq_in": np.ascontiguousarray(np.asarray(full["L0_attn_bq"], np.float32)),
        "bk_in": np.asarray(full["L0_attn_bk"], np.float32).reshape(1, D),
        "bv_in": np.asarray(full["L0_attn_bv"], np.float32).reshape(1, D),
        "bd_in": np.asarray(full["L0_attn_bd"], np.float32).reshape(1, D),
        "gamma_in": np.asarray(full["L0_attn_gamma"], np.float32).reshape(1, D),
        "beta_in": np.asarray(full["L0_attn_beta"], np.float32).reshape(1, D),
        "clipx_in": np.asarray(full["L0_ffn_clip_x"], np.float32).reshape(1, 1),
        "clipd_in": np.asarray(full["L0_ffn_clip_down"], np.float32).reshape(1, 1),
        "wg_n": np.ascontiguousarray(np.asarray(full["L0_ffn_w_gate"], np.float32)),
        "wgT": np.ascontiguousarray(np.asarray(full["L0_ffn_w_gate"], np.float32).T),
        "wu_n": np.ascontiguousarray(np.asarray(full["L0_ffn_w_up"], np.float32)),
        "wuT": np.ascontiguousarray(np.asarray(full["L0_ffn_w_up"], np.float32).T),
        "wdn_n": np.ascontiguousarray(np.asarray(full["L0_ffn_w_down"], np.float32)),
        "wdnT": np.ascontiguousarray(np.asarray(full["L0_ffn_w_down"], np.float32).T),
        "ident_in": np.eye(128, dtype=BF),
        "ones1b_in": np.ones((1, 128), BF),
        "ones1f_in": np.ones((1, 128), np.float32),
        "onescol_in": np.ones((128, 1), BF),
        "bc_in": _bc_const(),
    }
    xr = x.reshape(B, S, D)
    in_maps = []
    for core in range(NCORES):
        m = dict(com)
        m["x4"] = np.ascontiguousarray(xr[2 * core:2 * core + 2].reshape(TOK, D))
        in_maps.append(m)
    return in_maps


def _bc_const():
    bc = np.zeros((16, HC * 128), BF)
    for c in range(HC):
        bc[2 * c, 128 * c:128 * c + 64] = 1
        bc[2 * c + 1, 128 * c + 64:128 * c + 128] = 1
    return bc


def kernel(**inputs):
    in_maps = _host_inputs(inputs)
    nc = build_nc("full")
    res = run_bass_kernel_spmd(nc, in_maps, core_ids=list(range(NCORES)))
    parts = [res.results[c]["outb"].reshape(2, S, D) for c in range(NCORES)]
    return np.concatenate(parts, axis=0).astype(np.float32)
